# version 33
# baseline (speedup 1.0000x reference)
import contextlib
import math
import numpy as np
import ml_dtypes

import concourse.bass as bass
import concourse.mybir as mybir
from concourse.bass_utils import run_bass_kernel_spmd

F32 = mybir.dt.float32
BF16 = mybir.dt.bfloat16
AF = mybir.ActivationFunctionType
ALU = mybir.AluOpType
AX = mybir.AxisListType
BFNP = ml_dtypes.bfloat16

D = 1024
DC = 8
R = 16
FF = 4096
FFC = 32
CONVK = 4
GAMMA_MIN, GAMMA_MAX = 0.15, 1.0

ENGINES = ("tensor", "vector", "scalar", "sync", "gpsimd")


def preload_set(nc, needed):
    from concourse.hw_specs import get_activation_tables
    tabs = list(get_activation_tables(nc.m.arch).values())
    for idx, funcs in enumerate(tabs):
        if all(f in funcs for f in needed):
            def fn(h, idx=idx):
                return h.add_instruction(mybir.InstLoadActFuncSet(
                    name=nc.get_next_instruction_name(),
                    act_func_set_id=idx, ins=[], outs=[]))
            return fn
    raise ValueError(f"no table set contains {needed}")
DMA_SEMS = ("sx0", "swst", "sh2st", "shl", "pxt", "pw", "pst", "pld")
SEMS = ("pe", "dve", "act", "pool") + DMA_SEMS
ESEM = {"tensor": "pe", "vector": "dve", "scalar": "act", "gpsimd": "pool"}


class Rec:

    def __init__(self):
        self.ops = {e: [] for e in ENGINES}
        self.cnt = {s: 0 for s in SEMS}

    def w(self, eng, sem, val):
        if val and val > 0:
            self.ops[eng].append(("wait", sem, val))

    def do(self, eng, fn, inc=None, sw=True):
        if eng in ("vector", "scalar") or (eng == "gpsimd" and inc not in DMA_SEMS):
            own = ESEM[eng]
            if sw:
                self.w(eng, own, self.cnt[own])
            self.cnt[own] += 1
            self.ops[eng].append(("op", fn, own))
            if inc is not None and inc != own:
                raise ValueError(f"{eng} ops auto-inc {own}; got inc={inc}")
            return self.cnt[own]
        if inc is not None:
            self.cnt[inc] += 16 if inc in DMA_SEMS else 1
        self.ops[eng].append(("op", fn, inc))
        return self.cnt[inc] if inc else None

    def barrier(self):
        engines = ("tensor", "vector", "scalar", "gpsimd")
        snap0 = dict(self.cnt)
        for e in engines:
            for s in DMA_SEMS:
                self.w(e, s, snap0[s])
        self.do("tensor", self.tensor_barrier_fn, inc="pe")
        self.do("vector", lambda h: h.engine_nop())
        self.do("scalar", self.scalar_barrier_fn)
        self.do("gpsimd", lambda h: h.engine_nop())
        snap = dict(self.cnt)
        for e in ENGINES:
            for e2 in engines:
                if e2 != e:
                    self.w(e, ESEM[e2], snap[ESEM[e2]])
            for s in DMA_SEMS:
                self.w(e, s, snap[s])

    def replay(self, nc, sems):
        def mk_body(ename):
            def body(h):
                for item in self.ops[ename]:
                    if item[0] == "wait":
                        h.wait_ge(sems[item[1]], item[2])
                    else:
                        inst = item[1](h)
                        if item[2] is not None:
                            inst.then_inc(sems[item[2]],
                                          16 if item[2] in DMA_SEMS else 1)
            return body
        with nc.Block() as block:
            block.tensor(mk_body("tensor"))
            block.vector(mk_body("vector"))
            block.scalar(mk_body("scalar"))
            block.sync(mk_body("sync"))
            block.gpsimd(mk_body("gpsimd"))


class Arena:
    def __init__(self, handle, cols):
        self.h, self.cols, self.off, self.marks = handle, cols, 0, []

    def alloc(self, ncols, align=16):
        o = -(-self.off // align) * align
        assert o + ncols <= self.cols, f"arena overflow {o}+{ncols}>{self.cols}"
        self.off = o + ncols
        return self.h[:, o:o + ncols]

    def push(self):
        self.marks.append(self.off)

    def pop(self):
        self.off = self.marks.pop()


def host_constants():
    c = {}
    c["ident_f"] = np.eye(128, dtype=np.float32)
    q = np.arange(128)[:, None]
    p = np.arange(128)[None, :]
    sq, sp = q // 32, p // 32
    c["bigmask"] = np.where((sp > sq) | ((sp == sq) & (p >= q)), 1.0, 0.0).astype(BFNP)
    for j in range(CONVK):
        c[f"econv{j}"] = ((p - q) == j).astype(np.float32)
    for j in range(1, CONVK):
        c[f"ebnd{j}"] = ((p + 128 - q) == j).astype(np.float32)
    pr = np.arange(128, dtype=np.float32)[None, :].repeat(R, 0)
    c["exp_qs"] = pr % 32
    for dlt in range(4):
        c[f"exp_kv{dlt}"] = 32.0 * dlt - (pr % 32)
    c["exp_qeff"] = pr + 1.0
    c["exp_keff"] = 127.0 - pr
    c["ones_row_b"] = np.ones((1, 128), dtype=BFNP)
    c["ones_row_f"] = np.ones((1, 128), dtype=np.float32)
    c["ones_col_f"] = np.ones((128, 1), dtype=np.float32)
    c["ones_col_b"] = np.ones((128, 1), dtype=BFNP)
    c["ones16_f"] = np.ones((R, 1), dtype=np.float32)
    c["ones16r_f"] = np.ones((1, R), dtype=np.float32)
    return c


def host_prep(core, inputs, nseq, S):
    f32 = np.float32
    h = np.asarray(inputs["L0_attn_h"], dtype=f32)
    x0 = np.ascontiguousarray(h[core * nseq:(core + 1) * nseq, :S])
    d = {"x0": x0, "x0t": np.ascontiguousarray(x0.transpose(0, 2, 1))}
    d["uv"] = np.concatenate([inputs["L0_attn_u"], inputs["L0_attn_v"]], axis=1)
    d["projr"] = np.asarray(inputs["L0_attn_proj_w"], f32).T
    d["w1"] = np.asarray(inputs["L0_attn_mlp_w1"], f32)
    d["w2"] = np.asarray(inputs["L0_attn_mlp_w2"], f32)
    d["b1r"] = np.asarray(inputs["L0_attn_mlp_b1"], f32).reshape(FFC, 128).T
    d["b2row"] = np.asarray(inputs["L0_attn_mlp_b2"], f32).reshape(1, D)
    d["pbrow"] = np.asarray(inputs["L0_attn_proj_b"], f32).reshape(1, D)
    d["wupT"] = np.asarray(inputs["L0_ffn_w_up"], f32).T
    d["wdnT"] = np.asarray(inputs["L0_ffn_w_down"], f32).T
    d["n1w"] = np.asarray(inputs["L0_attn_norm1_w"], f32).reshape(DC, 128).T
    d["n2w"] = np.asarray(inputs["L0_attn_norm2_w"], f32).reshape(DC, 128).T
    d["declog"] = np.asarray(inputs["L0_attn_decay_logit"], f32).reshape(R, 1)
    d["alplog"] = np.asarray(inputs["L0_attn_alpha_logit"], f32).reshape(R, 1)
    d["gatelog"] = np.asarray(inputs["L0_attn_k_base_gate_logit"], f32).reshape(1, 1)
    d["kern"] = np.asarray(inputs["L0_attn_k_base_kernel"], f32).reshape(1, CONVK)
    d.update(host_constants())
    return {k: np.ascontiguousarray(v) for k, v in d.items()}


def build(nseq=2, S=2048, dbg=(), upto="p4"):
    seqt = S // 128
    NT = nseq * seqt
    NG = NT // 4
    assert NT % 4 == 0 or upto == "p12" 
    PASSES = ["p12", "p3", "p4"]
    upi = PASSES.index(upto)
    nc = bass.Bass("TRN2", target_bir_lowering=False, debug=False)

    di = {}
    def inp(name, shape, dt=F32):
        di[name] = nc.dram_tensor(name, list(shape), dt, kind="ExternalInput").ap()
    inp("x0", (nseq, S, D)); inp("x0t", (nseq, D, S))
    inp("uv", (D, 2 * R)); inp("projr", (D, D))
    inp("w1", (D, FF)); inp("w2", (FF, D))
    inp("b1r", (128, FFC)); inp("b2row", (1, D)); inp("pbrow", (1, D))
    inp("wupT", (D, FF)); inp("wdnT", (FF, D))
    inp("n1w", (128, DC)); inp("n2w", (128, DC))
    inp("declog", (R, 1)); inp("alplog", (R, 1)); inp("gatelog", (1, 1))
    inp("kern", (1, CONVK))
    inp("ident_f", (128, 128)); inp("bigmask", (128, 128), BF16)
    for j in range(CONVK): inp(f"econv{j}", (128, 128))
    for j in range(1, CONVK): inp(f"ebnd{j}", (128, 128))
    inp("exp_qs", (R, 128))
    for dlt in range(4): inp(f"exp_kv{dlt}", (R, 128))
    inp("exp_qeff", (R, 128)); inp("exp_keff", (R, 128))
    inp("ones_row_b", (1, 128), BF16); inp("ones_row_f", (1, 128))
    inp("ones_col_f", (128, 1)); inp("ones_col_b", (128, 1), BF16)
    inp("ones16_f", (R, 1)); inp("ones16r_f", (1, R))

    out_d = nc.dram_tensor("out", [NT, 128, D], F32, kind="ExternalOutput").ap()
    def scr(name, shape, dt):
        kind = "ExternalOutput" if name in dbg else "Internal"
        return nc.dram_tensor(name, list(shape), dt, kind=kind).ap()
    h2_d = scr("h2", (NT, 128, D), BF16)
    hn2_d = scr("hn2", (NT, 128, D), BF16)
    aout_d = scr("aout", (NT, 128, D), BF16)
    x1_d = scr("x1", (NT, 128, D), F32)
    fin_d = scr("fin", (NT, 128, D), BF16)
    raw_d = scr("raw", (NT, 128, D), BF16)
    dbg_hn = scr("dbg_hn", (NT, 128, D), BF16)
    dbg_mo = scr("dbg_mo", (NT, 128, D), BF16)
    dbg_q = scr("dbg_q", (NT, R, 128), BF16)
    dbg_at = scr("dbg_at", (NT, 128, 128), BF16)
    dbg_rows = scr("dbg_rows", (128, 8 * NT), F32)
    wqup_d = scr("wqup", (D, FF), BF16)
    wqdn_d = scr("wqdn", (FF, D), BF16)

    r = Rec()
    es = contextlib.ExitStack()
    with es:
        sems = {s: es.enter_context(nc.semaphore(s)) for s in SEMS}
        ABp = Arena(es.enter_context(nc.sbuf_tensor("ab_p", [128, 1400], BF16)), 1400)
        AFp = Arena(es.enter_context(nc.sbuf_tensor("af_p", [128, 1510], F32)), 1510)
        pb = [es.enter_context(nc.psum_tensor(f"pb{i}", [128, 512], F32)) for i in range(8)]

        def open_scope(tag, nbf, nf):
            st = contextlib.ExitStack()
            ab = Arena(st.enter_context(
                nc.sbuf_tensor(f"ab_{tag}", [128, nbf], BF16)), nbf)
            af = Arena(st.enter_context(
                nc.sbuf_tensor(f"af_{tag}", [128, nf], F32)), nf)
            return st, ab, af

        ones_row_b = ABp.alloc(128)[0:1, :]
        ones_col_b = ABp.alloc(16)[:, 0:1]
        b2row_b = ABp.alloc(D)[0:1, :]
        ident_b = ABp.alloc(16)[0:1, 0:1]
        ident_f = AFp.alloc(128)
        ones_row_f = AFp.alloc(128)[0:1, :]
        ones_col_f = AFp.alloc(16)[:, 0:1]
        ones16_f = AFp.alloc(16)[0:R, 0:1]
        ones16r_f = AFp.alloc(16)[0:1, 0:R]
        n1w = AFp.alloc(DC); n2w = AFp.alloc(DC); b1r = AFp.alloc(FFC)
        rows = {nm: AFp.alloc(NT) for nm in
                ["m", "cln", "mcln", "ccomb", "mccomb", "rstd2",
                 "suma", "ssqa", "A3", "B3", "S4", "M4",
                 "ssqg", "sumr", "ssqr", "A4", "B4", "v0", "Sdot"]}
        rt = [AFp.alloc(16)[:, 0:1] for _ in range(10)]
        thr_up = AFp.alloc(16)[:, 0:1]; nthr_up = AFp.alloc(16)[:, 0:1]
        thr_dn = AFp.alloc(16)[:, 0:1]; nthr_dn = AFp.alloc(16)[:, 0:1]
        part_arr = AFp.alloc(64)
        rtq = AFp.alloc(16)[:, 0:1]; rtq2b = AFp.alloc(16)[:, 0:1]

        for cv in (1e-5, -0.5, 0.5, 1.000001, 1e-11, 1e-6, 1.0 / D, 1.0 / FF,
                   -1.0 / D, -1.000001 / D):
            cap = AFp.alloc(16)[:, 0:1]
            r.do("vector", lambda h, a=cap, v=cv: h.memset(a, v))
            nc.const_aps.register(F32, cv, cap)

        def L(dst, src, stream="swst"):
            eng = "sync" if stream.startswith("s") else "gpsimd"
            r.w(eng, stream, r.cnt[stream])
            return r.do(eng, lambda h, d=dst, s=src: h.dma_start(out=d, in_=s), inc=stream)

        L(ones_row_b, di["ones_row_b"]); L(ones_col_b, di["ones_col_b"])
        L(b2row_b, di["b2row"], "pw")
        L(ident_f, di["ident_f"]); L(ones_row_f, di["ones_row_f"])
        L(ones_col_f, di["ones_col_f"]); L(ones16_f, di["ones16_f"])
        L(ones16r_f, di["ones16r_f"])
        L(n1w, di["n1w"]); L(n2w, di["n2w"]); L(b1r, di["b1r"])
        r.do("vector", lambda h: h.memset(ident_b, 1.0))
        barr_row = AFp.alloc(16)[0:1, 0:1]
        r.tensor_barrier_fn = lambda h: h.matmul(
            pb[1][0:1, 456:457], ident_f[0:1, 0:1], ident_f[0:1, 0:1],
            start=True, stop=True)
        r.scalar_barrier_fn = lambda h: h.memzero(barr_row)

        scW, ABw, _AFw = open_scope("w1", DC * FF + 16, 16)
        w1_bf = ABw.alloc(DC * FF)

        sc12, AB, AFr = open_scope("p12", 38200, 15400)
        bigmask = AB.alloc(128)
        uvp = AB.alloc(2 * R * DC)
        convm_b = AB.alloc(128); cm3_b = AB.alloc(128)
        uvsum_nb = AB.alloc(2 * R)[0:1, :]
        pbrow_b = AB.alloc(D)[0:1, :]
        state_b = [AB.alloc(D)[0:R, :] for _ in range(nseq)]
        x0t_b = [AB.alloc(D) for _ in range(4)]
        HNR = 3 if nseq == 2 else 5
        hn_b = [[AB.alloc(D) for _ in range(HNR)] for _ in range(nseq)]
        mo_b = [AB.alloc(D) for _ in range(2)]
        sqscr_b = AB.alloc(D)
        h2_b = [AB.alloc(D) for _ in range(6)]
        hn2_b = [AB.alloc(D) for _ in range(2)]
        at_b = [AB.alloc(128) for _ in range(2)]
        qabf = [AB.alloc(128)[0:R, :] for _ in range(4)]
        knbf = [AB.alloc(128)[0:R, :] for _ in range(4)]
        qsbf = [AB.alloc(128)[0:R, :] for _ in range(4)]
        qeffbf = [AB.alloc(128)[0:R, :] for _ in range(4)]
        kvbf = [[AB.alloc(128)[0:R, :] for _ in range(4)] for _ in range(4)]
        kefft_b = [AB.alloc(16) for _ in range(4)]
        mrow_b = [AB.alloc(128)[0:1, :] for _ in range(2)]

        qpat = {nm: AFr.alloc(128)[0:R, :] for nm in
                ["qs", "kv0", "kv1", "kv2", "kv3", "qeff", "keff"]}
        alpha_row = AFr.alloc(16)[0:1, 0:R]
        g128 = AFr.alloc(16)[0:R, 0:1]
        gam = AFr.alloc(16)[0:R, 0:1]; lng = AFr.alloc(16)[0:R, 0:1]
        gl = AFr.alloc(16)[0:1, 0:1]; kr = AFr.alloc(16)[0:1, 0:CONVK]
        dl = AFr.alloc(16)[0:R, 0:1]; al = AFr.alloc(16)[0:R, 0:1]
        alp = AFr.alloc(16)[0:R, 0:1]
        gkrow = AFr.alloc(16)[0:1, 0:CONVK]
        gk_b = AFr.alloc(CONVK)
        state_f = [AFr.alloc(D)[0:R, :] for _ in range(nseq)]
        x0_t = [AFr.alloc(D) for _ in range(4)]
        wstg = [AFr.alloc(D) for _ in range(2)]
        emsk = {**{f"econv{j}": AFr.alloc(128) for j in range(CONVK)},
                **{f"ebnd{j}": AFr.alloc(128) for j in range(1, CONVK)}}
        cscr = AFr.alloc(128)
        qkf = [AFr.alloc(256)[0:R, :] for _ in range(4)]
        sqf = [AFr.alloc(256)[0:R, :] for _ in range(2)]
        keff_f = [AFr.alloc(128)[0:R, :] for _ in range(2)]
        inv_row = [AFr.alloc(256)[0:1, :] for _ in range(2)]
        wstg1 = [AFr.alloc(1024) for _ in range(2)]

        def q_ps(pi): return pb[pi][0:R, 0:128]
        def k_ps(pi): return pb[pi][0:R, 128:256]
        def qk_ps(pi): return pb[pi][0:R, 0:256]
        def mrow_ps(pi): return pb[pi][0:1, 256:384]
        def at_ps(pi): return pb[pi][:, 256:384]
        def kefft_ps(pi): return pb[pi][:, 384:400]
        def ssq_ps(pi): return pb[2 + pi][0:1, 0:256]
        def bc_ps(pi): return pb[2 + pi][0:R, 0:256]
        def dlt_ps(pi): return pb[2 + pi][0:R, 256:512]
        def o_ps(pi, c): return pb[4 + 2 * pi + c // 4][:, 128 * (c % 4):128 * (c % 4) + 128]
        def pj_ps(pi, eh): return pb[4 + 2 * pi + eh][:, :]
        arow_ps = pb[1][0:1, 400:416]; usum_ps = pb[1][0:1, 416:448]
        gkb_ps = pb[1][:, 448:452]

        L(bigmask, di["bigmask"])
        L(pbrow_b, di["pbrow"], "pw")
        for nm in qpat:
            L(qpat[nm], di["exp_" + nm])
        for nm in emsk:
            L(emsk[nm], di[nm])
        L(gl, di["gatelog"]); L(kr, di["kern"]); L(dl, di["declog"]); L(al, di["alplog"])
        uvstg = AFr.alloc(2 * R * DC)
        L(uvstg.rearrange("p (c r) -> p c r", c=DC),
          di["uv"].rearrange("(c p) r -> p c r", p=128))
        ld0 = r.cnt["swst"]; lp0 = r.cnt["pw"]

        r.w("scalar", "swst", ld0); r.w("scalar", "pw", lp0)
        r.do("scalar", preload_set(nc, [AF.Sigmoid]))
        r.do("scalar", lambda h: h.activation(out=gam, in_=dl, func=AF.Sigmoid))
        r.do("scalar", lambda h: h.activation(out=alp, in_=al, func=AF.Sigmoid))
        r.do("scalar", lambda h: h.activation(out=gl, in_=gl, func=AF.Sigmoid), inc="act")
        a_sig = r.cnt["act"]
        r.w("vector", "act", a_sig)
        r.do("vector", lambda h: h.tensor_scalar_max(out=gam, in0=gam, scalar1=GAMMA_MIN))
        r.do("vector", lambda h: h.tensor_scalar_min(out=gam, in0=gam, scalar1=GAMMA_MAX))
        r.do("vector", lambda h: h.tensor_scalar(out=gkrow, in0=kr, scalar1=gl,
                                                 scalar2=None, op0=ALU.mult), inc="dve")
        d_gk = r.cnt["dve"]
        r.do("scalar", preload_set(nc, [AF.Ln, AF.Exp, AF.Square]))
        r.w("scalar", "dve", d_gk)
        r.do("scalar", lambda h: h.activation(out=lng, in_=gam, func=AF.Ln), inc="act")
        a_ln = r.cnt["act"]
        r.w("vector", "act", a_ln)
        for nm in qpat:
            r.do("vector", lambda h, t=qpat[nm]: h.tensor_scalar_mul(out=t, in0=t, scalar1=lng))
        r.do("vector", lambda h: h.tensor_scalar_mul(out=g128, in0=lng, scalar1=128.0), inc="dve")
        d_pat = r.cnt["dve"]
        r.w("scalar", "dve", d_pat)
        for nm in qpat:
            r.do("scalar", lambda h, t=qpat[nm]: h.activation(out=t, in_=t, func=AF.Exp))
        r.do("scalar", lambda h: h.activation(out=g128, in_=g128, func=AF.Exp), inc="act")
        a_pat = r.cnt["act"]
        r.w("tensor", "act", a_sig)
        r.do("tensor", lambda h: h.transpose(arow_ps, alp, ident_f[0:R, 0:R]), inc="pe")
        p_ar = r.cnt["pe"]
        r.w("vector", "pe", p_ar)
        r.do("vector", lambda h: h.tensor_copy(out=alpha_row, in_=arow_ps), inc="dve")
        r.w("tensor", "dve", d_gk)
        r.do("tensor", lambda h: h.matmul(gkb_ps, ones_row_f, gkrow, start=True, stop=True), inc="pe")
        r.w("vector", "pe", r.cnt["pe"])
        r.do("vector", lambda h: h.tensor_copy(out=gk_b, in_=gkb_ps), inc="dve")
        d_gkb = r.cnt["dve"]
        r.w("vector", "swst", ld0)
        r.do("vector", lambda h: h.tensor_scalar_mul(out=cscr, in0=emsk["econv0"],
                                                     scalar1=gk_b[:, 0:1]))
        for j in range(1, CONVK):
            r.do("vector", lambda h, j=j: h.scalar_tensor_tensor(
                out=cscr, in0=emsk[f"econv{j}"], scalar=gk_b[:, j:j + 1], in1=cscr,
                op0=ALU.mult, op1=ALU.add))
        r.do("vector", lambda h: h.tensor_copy(out=convm_b, in_=cscr))
        r.do("vector", lambda h: h.tensor_scalar_mul(out=cscr, in0=emsk["ebnd1"],
                                                     scalar1=gk_b[:, 1:2]))
        for j in range(2, CONVK):
            r.do("vector", lambda h, j=j: h.scalar_tensor_tensor(
                out=cscr, in0=emsk[f"ebnd{j}"], scalar=gk_b[:, j:j + 1], in1=cscr,
                op0=ALU.mult, op1=ALU.add))
        r.do("vector", lambda h: h.tensor_copy(out=cm3_b, in_=cscr))
        for c in range(DC):
            r.do("vector", lambda h, c=c: h.tensor_scalar_mul(
                out=uvp[:, 32 * c:32 * c + 32], in0=uvstg[:, 32 * c:32 * c + 32],
                scalar1=n1w[:, c:c + 1]))
        r.do("vector", lambda h: h.engine_nop(), inc="dve")
        d_uvp = r.cnt["dve"]
        r.w("tensor", "dve", d_uvp)
        for c in range(DC):
            r.do("tensor", lambda h, c=c: h.matmul(
                usum_ps, ones_col_b, uvp[:, 32 * c:32 * c + 32],
                start=(c == 0), stop=(c == DC - 1)), inc="pe" if c == DC - 1 else None)
        p_us = r.cnt["pe"]
        r.w("vector", "pe", p_us)
        r.do("vector", lambda h: h.tensor_scalar_mul(out=uvsum_nb, in0=usum_ps, scalar1=-1.0 / D), inc="dve")
        projr_bf = AB.alloc(DC * D)
        dpj = {}
        for c in range(DC):
            r.w("sync", "dve", dpj.get(c - 2, 0))
            lw = L(wstg[c % 2], di["projr"][128 * c:128 * c + 128, :])
            r.w("vector", "swst", lw)
            dpj[c] = r.do("vector", lambda h, c=c: h.tensor_scalar_mul(
                out=projr_bf[:, D * c:D * (c + 1)], in0=wstg[c % 2],
                scalar1=n1w[:, c:c + 1]), inc="dve")
        for s in range(nseq):
            r.do("vector", lambda h, s=s: h.memset(state_f[s], 0.0))
            r.do("vector", lambda h, s=s: h.memset(state_b[s], 0.0))
            r.do("vector", lambda h, s=s: h.memset(hn_b[s][HNR - 1], 0.0))
        r.do("vector", lambda h: h.memset(pb[0][:, 256:384], 0.0))
        r.do("vector", lambda h: h.memset(pb[1][:, 256:384], 0.0))
        r.barrier()

        ms = [dict() for _ in range(NT)]
        order = [(s2, t2) for t2 in range(seqt) for s2 in range(nseq)]
        def M(gi, key):
            return ms[gi][key] if gi >= 0 else 0
        def EM(j, key):
            if j < 0:
                return 0
            s2, t2 = order[j]
            return ms[s2 * seqt + t2][key]
        tiles_w = [("up", di["wupT"], ai, bi) for ai in range(DC) for bi in range(4)] + \
                  [("dn", di["wdnT"], mi, 0) for mi in range(FFC)]
        alpha_sched = {k2: [] for k2 in range(NT)}
        for i2 in range(len(tiles_w)):
            alpha_sched[i2 * NT // len(tiles_w)].append(i2)
        lastred = {}
        d_w1prep = 0

        w1_sched = {k2: [] for k2 in range(NT)}
        for i2 in range(32):
            w1_sched[i2 * NT // 32].append(i2)
        lw_d = {}

        def emit_ld_xt(k2):
            if k2 >= NT:
                return
            s2, t2 = order[k2]
            gi2 = s2 * seqt + t2
            r.w("gpsimd", "pe", EM(k2 - 4, "p_qk"))
            ms[gi2]["ld_xt"] = L(
                x0t_b[k2 % 4].rearrange("p (c n) -> p c n", c=DC),
                di["x0t"][s2, :, 128 * t2:128 * t2 + 128].rearrange("(c p) n -> p c n", p=128),
                "pxt")
        emit_ld_xt(0); emit_ld_xt(1)

        def ctx(k):
            s, t = order[k]
            gi = s * seqt + t
            return s, t, gi, k % 2

        def S1(k):
            s, t, gi, par = ctx(k)
            x0c = x0_t[k % 4]
            hnc = hn_b[s][t % HNR]
            rt0 = rt[4 * par + 0]; rt1 = rt[4 * par + 1]; rt2 = rt[4 * par + 2]
            r.w("sync", "dve", EM(k - 4, "d_ain"))
            r.w("sync", "act", EM(k - 4, "a_sq"))
            ms[gi]["ld_x0"] = L(x0c, di["x0"][s, 128 * t:128 * t + 128, :], "sx0")
            emit_ld_xt(k + 2)
            r.w("scalar", "sx0", ms[gi]["ld_x0"])
            r.w("scalar", "dve", EM(k - 2, "d_v"))
            ms[gi]["a_sq"] = r.do("scalar", lambda h, o=sqscr_b, i=x0c, a=rt0:
                                  h.activation(out=o, in_=i, func=AF.Square, accum_out=a))
            r.w("vector", "sx0", ms[gi]["ld_x0"])
            r.do("vector", lambda h, i=x0c, o=rt1: h.reduce_sum(out=o, in_=i, axis=AX.X))
            r.w("vector", "act", ms[gi]["a_sq"])
            r.do("vector", lambda h, o=rows["m"][:, gi:gi + 1], i=rt1:
                 h.tensor_scalar_mul(out=o, in0=i, scalar1=1.0 / D))
            ms[gi]["d_v"] = r.do("vector", lambda h, o=rows["v0"][:, gi:gi + 1], i=rt1,
                                 m_=rows["m"][:, gi:gi + 1], ss=rt0: h.scalar_tensor_tensor(
                                 out=o, in0=i, scalar=m_, in1=ss, op0=ALU.mult, op1=ALU.subtract))
            if "dbg_rows" in dbg:
                r.do("vector", lambda h, o=rows["ssqa"][:, gi:gi + 1], i=rt0: h.tensor_copy(out=o, in_=i))
            r.w("scalar", "dve", ms[gi]["d_v"])
            r.do("scalar", lambda h, o=rt[4 * par + 3], i=rows["v0"][:, gi:gi + 1]:
                 h.activation(out=o, in_=i, func=AF.Ln, scale=-1.0 / D, bias=1e-5))
            r.do("scalar", lambda h, o=rows["cln"][:, gi:gi + 1], i=rt[4 * par + 3]:
                 h.activation(out=o, in_=i, func=AF.Exp, scale=-0.5))
            r.do("scalar", lambda h, o=rt[8 + par], i=rows["v0"][:, gi:gi + 1]:
                 h.activation(out=o, in_=i, func=AF.Ln, scale=-1.000001 / D, bias=1e-11), sw=False)
            a_rows = r.do("scalar", lambda h, o=rows["ccomb"][:, gi:gi + 1], i=rt[8 + par]:
                          h.activation(out=o, in_=i, func=AF.Exp, scale=-0.5))
            ms[gi]["a_rows"] = a_rows
            r.w("vector", "act", a_rows)
            if t >= HNR - 1:
                r.w("vector", "pe", M(gi - (HNR - 1), "p_o"))
            ms[gi]["d_hn"] = r.do("vector", lambda h, o=hnc, i=x0c,
                                  m_=rows["m"][:, gi:gi + 1], c=rows["ccomb"][:, gi:gi + 1]:
                                  h.tensor_scalar(out=o, in0=i, scalar1=m_, scalar2=c,
                                                  op0=ALU.subtract, op1=ALU.mult))
            r.w("vector", "sh2st", EM(k - 6, "st_h2"))
            ms[gi]["d_ain"] = r.do("vector", lambda h, o=h2_b[k % 6], i=x0c,
                                   m_=rows["m"][:, gi:gi + 1], c=rows["cln"][:, gi:gi + 1]:
                                   h.tensor_scalar(out=o, in0=i, scalar1=m_, scalar2=c,
                                                   op0=ALU.subtract, op1=ALU.mult), sw=False)
            if "dbg_hn" in dbg:
                r.w("sync", "dve", ms[gi]["d_hn"])
                L(dbg_hn[gi], hnc, "sh2st")

        def S2(k):
            s, t, gi, par = ctx(k)
            r.w("tensor", "pxt", ms[gi]["ld_xt"])
            r.w("tensor", "dve", EM(k - 2, "d_at"))
            r.w("tensor", "act", EM(k - 2, "a_mrow"))
            for c in range(DC):
                p_mrow = r.do("tensor", lambda h, c=c, x=x0t_b[k % 4], o=mrow_ps(par):
                              h.matmul(o[0:1, 0:128], ones_col_b, x[:, 128 * c:128 * c + 128],
                                       start=(c == 0), stop=(c == DC - 1)),
                              inc="pe" if c == DC - 1 else None)
            r.w("scalar", "pe", p_mrow)
            ms[gi]["a_mrow"] = r.do("scalar", lambda h, o=mrow_b[par], i=mrow_ps(par):
                                    h.copy(out=o, in_=i))
            r.w("tensor", "pxt", ms[gi]["ld_xt"])
            r.w("tensor", "act", ms[gi]["a_mrow"])
            r.w("tensor", "dve", EM(k - 2, "d_qkevac"))
            for c in range(DC):
                r.do("tensor", lambda h, c=c, x=x0t_b[k % 4], o=q_ps(par): h.matmul(
                    o, uvp[:, 32 * c:32 * c + 16], x[:, 128 * c:128 * c + 128],
                    start=(c == 0), stop=False))
            r.do("tensor", lambda h, mr=mrow_b[par], o=q_ps(par): h.matmul(
                o, uvsum_nb[0:1, 0:R], mr, start=False, stop=True))
            for c in range(DC):
                r.do("tensor", lambda h, c=c, x=x0t_b[k % 4], o=k_ps(par): h.matmul(
                    o, uvp[:, 32 * c + 16:32 * c + 32], x[:, 128 * c:128 * c + 128],
                    start=(c == 0), stop=False))
            ms[gi]["p_qk"] = r.do("tensor", lambda h, mr=mrow_b[par], o=k_ps(par): h.matmul(
                o, uvsum_nb[0:1, R:2 * R], mr, start=False, stop=True), inc="pe")
            r.w("vector", "pe", ms[gi]["p_qk"])
            r.w("vector", "pe", EM(k - 4, "pe_end"))
            r.do("vector", lambda h, o=qkf[k % 4], i=qk_ps(par): h.tensor_copy(out=o, in_=i))
            ms[gi]["d_qkevac"] = r.do("vector", lambda h, o=sqf[par], i=qkf[k % 4]:
                                      h.tensor_mul(out=o, in0=i, in1=i))
            for i2 in alpha_sched.get(k, []):
                nm2, src2, a2, b2 = tiles_w[i2]
                r.w("sync", "act", lastred.get(i2 - 2, 0))
                src_ap = src2[128 * a2:128 * a2 + 128, 1024 * b2:1024 * b2 + 1024] \
                    if nm2 == "up" else src2[128 * a2:128 * a2 + 128, :]
                lw2 = L(wstg[i2 % 2], src_ap, "swst")
                r.w("scalar", "swst", lw2)
                lastred[i2] = r.do("scalar", lambda h, i2=i2, w=wstg[i2 % 2]: h.activation(
                    out=w, in_=w, func=AF.Abs, accum_out=part_arr[:, i2:i2 + 1]))

        def S3(k):
            s, t, gi, par = ctx(k)
            r.w("tensor", "dve", ms[gi]["d_qkevac"])
            r.w("tensor", "dve", EM(k - 2, "d_var"))
            r.w("tensor", "dve", EM(k - 2, "d_st"))
            p_ssq = r.do("tensor", lambda h, i=sqf[par], o=ssq_ps(par):
                         h.matmul(o, ones16_f, i, start=True, stop=True), inc="pe")
            r.w("vector", "pe", p_ssq)
            ms[gi]["d_inv"] = r.do("vector", lambda h, o=inv_row[par], i=ssq_ps(par):
                                   h.tensor_scalar_max(out=o, in0=i, scalar1=1e-16))
            r.w("scalar", "dve", ms[gi]["d_inv"])
            r.do("scalar", lambda h, o=inv_row[par]: h.activation(out=o, in_=o, func=AF.Ln))
            a_inv = r.do("scalar", lambda h, o=inv_row[par]:
                         h.activation(out=o, in_=o, func=AF.Exp, scale=-0.5))
            r.w("tensor", "act", a_inv)
            r.w("tensor", "dve", ms[gi]["d_inv"])
            r.do("tensor", lambda h, i=inv_row[par], o=bc_ps(par): h.matmul(
                o[0:R, 0:128], alpha_row, i[0:1, 0:128], start=True, stop=True))
            p_bc = r.do("tensor", lambda h, i=inv_row[par], o=bc_ps(par): h.matmul(
                o[0:R, 128:256], ones16r_f, i[0:1, 128:256], start=True, stop=True), inc="pe")
            r.w("vector", "pe", p_bc)
            r.w("vector", "pe", EM(k - 4, "pe_end"))
            r.do("vector", lambda h, o=qabf[k % 4], i=qkf[k % 4], b=bc_ps(par):
                 h.tensor_mul(out=o, in0=i[:, 0:128], in1=b[0:R, 0:128]))
            r.do("vector", lambda h, o=knbf[k % 4], i=qkf[k % 4], b=bc_ps(par):
                 h.tensor_mul(out=o, in0=i[:, 128:256], in1=b[0:R, 128:256]))
            r.do("vector", lambda h, o=qsbf[k % 4], i=qabf[k % 4]: h.tensor_mul(out=o, in0=i, in1=qpat["qs"]))
            for dlt in range(4):
                r.do("vector", lambda h, o=kvbf[dlt][k % 4], i=knbf[k % 4], pat=qpat[f"kv{dlt}"]:
                     h.tensor_mul(out=o, in0=i, in1=pat), sw=(dlt == 0))
            r.do("vector", lambda h, o=qeffbf[k % 4], i=qabf[k % 4]:
                 h.tensor_mul(out=o, in0=i, in1=qpat["qeff"]), sw=False)
            ms[gi]["d_var"] = r.do("vector", lambda h, o=keff_f[par], i=knbf[k % 4]:
                                   h.tensor_mul(out=o, in0=i, in1=qpat["keff"]), sw=False)
            if "dbg_q" in dbg:
                r.w("sync", "dve", ms[gi]["d_var"])
                L(dbg_q[gi], qabf[k % 4][0:R, :], "sh2st")

        def S4(k):
            s, t, gi, par = ctx(k)
            r.w("tensor", "dve", ms[gi]["d_var"])
            r.w("tensor", "act", EM(k - 2, "a_kt"))
            p_kt = r.do("tensor", lambda h, i=keff_f[par], o=kefft_ps(par):
                        h.transpose(o, i, ident_f[0:R, 0:R]), inc="pe")
            r.w("scalar", "pe", p_kt)
            ms[gi]["a_kt"] = r.do("scalar", lambda h, o=kefft_b[k % 4], i=kefft_ps(par):
                                  h.copy(out=o, in_=i))
            r.w("tensor", "dve", EM(k - 2, "d_at"))
            for b in range(4):
                for a in range(b, 4):
                    dlt = a - b
                    last = (b == 3 and a == 3)
                    r.do("tensor", lambda h, b=b, a=a, kv=kvbf[dlt][k % 4], qs=qsbf[k % 4],
                         o=pb[par]: h.matmul(
                        o[32 * b:32 * b + 32, 256 + 32 * a:256 + 32 * a + 32],
                        kv[:, 32 * b:32 * b + 32],
                        qs[:, 32 * a:32 * a + 32],
                        start=True, stop=True, tile_position=(0, 32 * b)),
                        inc="pe" if last else None)
            ms[gi]["p_at"] = r.cnt["pe"]
            r.w("vector", "pe", ms[gi]["p_at"])
            r.w("vector", "pe", EM(k - 2, "p_o"))
            ms[gi]["d_at"] = r.do("vector", lambda h, o=at_b[par], i=at_ps(par):
                                  h.tensor_tensor(out=o, in0=i, in1=bigmask, op=ALU.mult))
            if "dbg_at" in dbg:
                r.w("sync", "dve", ms[gi]["d_at"])
                L(dbg_at[gi], at_b[par], "sh2st")
            for i3 in w1_sched.get(k, []):
                c3, j3 = i3 // 4, i3 % 4
                r.w("sync", "dve", lw_d.get(i3 - 2, 0))
                lw3 = L(wstg1[i3 % 2],
                        di["w1"][128 * c3:128 * c3 + 128, 1024 * j3:1024 * j3 + 1024], "swst")
                r.w("vector", "swst", lw3)
                lw_d[i3] = r.do("vector", lambda h, c3=c3, j3=j3, i=wstg1[i3 % 2]:
                                h.tensor_scalar_mul(
                                    out=w1_bf[:, FF * c3 + 1024 * j3:FF * c3 + 1024 * j3 + 1024],
                                    in0=i, scalar1=n2w[:, c3:c3 + 1]))

        def S5(k):
            s, t, gi, par = ctx(k)
            hnc = hn_b[s][t % HNR]
            hnp = hn_b[s][(t - 1) % HNR]
            r.w("tensor", "dve", ms[gi]["d_at"])
            r.w("tensor", "dve", ms[gi]["d_hn"])
            if t >= 1:
                r.w("tensor", "dve", M(gi - 1, "d_st"))
            r.w("tensor", "dve", EM(k - 2, "d_h2"))
            r.w("tensor", "dve", EM(k - 2, "d_moevac"))
            r.w("tensor", "act", EM(k - 2, "a_moevac"))
            for c in range(DC):
                last_c = (c == DC - 1)
                r.do("tensor", lambda h, c=c, hn=hnc, at=at_b[par], o=o_ps(par, c): h.matmul(
                    o, hn[:, 128 * c:128 * c + 128], at, start=True, stop=False))
                r.do("tensor", lambda h, c=c, st=state_b[s], qe=qeffbf[k % 4], o=o_ps(par, c): h.matmul(
                    o, st[:, 128 * c:128 * c + 128], qe, start=False, stop=False))
                ms[gi]["p_o"] = r.do("tensor", lambda h, c=c, hn=hnc, t0=(t == 0), o=o_ps(par, c): h.matmul(
                    o, hn[:, 128 * c:128 * c + 128], convm_b, start=False, stop=t0),
                    inc="pe" if (last_c and t == 0) else None)
                if t >= 1:
                    ms[gi]["p_o"] = r.do("tensor", lambda h, c=c, hp=hnp, o=o_ps(par, c): h.matmul(
                        o, hp[:, 128 * c:128 * c + 128], cm3_b, start=False, stop=True),
                        inc="pe" if last_c else None)
            d_q4 = EM(k - 2, "d_st")
            for qq4 in range(4):
                r.w("tensor", "act", ms[gi]["a_kt"])
                r.w("tensor", "dve", d_q4)
                p_dl = r.do("tensor", lambda h, qq4=qq4, kt=kefft_b[k % 4], hn=hnc, o=dlt_ps(par):
                            h.matmul(o, kt[:, 0:R], hn[:, 256 * qq4:256 * qq4 + 256],
                                     start=True, stop=True), inc="pe")
                r.w("vector", "pe", p_dl)
                d_q4 = r.do("vector", lambda h, qq4=qq4, sf=state_f[s], dl=dlt_ps(par): h.scalar_tensor_tensor(
                    out=sf[:, 256 * qq4:256 * qq4 + 256], in0=sf[:, 256 * qq4:256 * qq4 + 256],
                    scalar=g128, in1=dl, op0=ALU.mult, op1=ALU.add))
            ms[gi]["d_st"] = r.do("vector", lambda h, sf=state_f[s], sb=state_b[s]:
                                  h.tensor_copy(out=sb[:, :], in_=sf[:, :]))
            r.w("scalar", "pe", ms[gi]["p_o"])
            r.w("vector", "pe", ms[gi]["p_o"])
            r.w("scalar", "pe", EM(k - 2, "p_pj"))
            r.w("vector", "pe", EM(k - 2, "p_pj"))
            for c in range(4):
                r.do("scalar", lambda h, c=c, o=mo_b[par], i=o_ps(par, c): h.copy(
                    out=o[:, 128 * c:128 * c + 128], in_=i), sw=(c == 0))
            for c in range(4, DC):
                r.do("vector", lambda h, c=c, o=mo_b[par], i=o_ps(par, c): h.tensor_copy(
                    out=o[:, 128 * c:128 * c + 128], in_=i), sw=(c == 4))
            ms[gi]["a_moevac"] = r.cnt["act"]
            ms[gi]["d_moevac"] = r.cnt["dve"]
            if "dbg_mo" in dbg:
                r.w("sync", "act", ms[gi]["a_moevac"])
                r.w("sync", "dve", ms[gi]["d_moevac"])
                L(dbg_mo[gi], mo_b[par], "sh2st")

        def S6(k):
            s, t, gi, par = ctx(k)
            r.w("tensor", "act", ms[gi]["a_moevac"])
            r.w("tensor", "dve", ms[gi]["d_moevac"])
            r.w("tensor", "pw", lp0)
            for eh in range(2):
                for c in range(DC):
                    r.do("tensor", lambda h, eh=eh, c=c, o=mo_b[par], pj=pj_ps(par, eh): h.matmul(
                        pj, o[:, 128 * c:128 * c + 128],
                        projr_bf[:, D * c + 512 * eh:D * c + 512 * eh + 512],
                        start=(c == 0), stop=False))
                ms[gi]["p_pj"] = r.do("tensor", lambda h, eh=eh, pj=pj_ps(par, eh): h.matmul(
                    pj, ones_row_b, pbrow_b[0:1, 512 * eh:512 * eh + 512],
                    start=False, stop=True), inc="pe" if eh == 1 else None)
            r.w("vector", "pe", ms[gi]["p_pj"])
            r.w("vector", "act", EM(k - 2, "a_rstd2"))
            for eh in range(2):
                r.do("vector", lambda h, eh=eh, o=h2_b[k % 6], pj=pj_ps(par, eh):
                     h.scalar_tensor_tensor(
                    out=o[:, 512 * eh:512 * eh + 512], in0=o[:, 512 * eh:512 * eh + 512],
                    scalar=1.0, in1=pj, op0=ALU.mult, op1=ALU.add), sw=(eh == 0))
            ms[gi]["d_h2"] = r.cnt["dve"]
            r.w("scalar", "dve", ms[gi]["d_h2"])
            r.w("scalar", "sh2st", EM(k - 2, "st_h2"))
            r.do("scalar", lambda h, o=hn2_b[par], i=h2_b[k % 6], a=rt[6]:
                 h.activation(out=o, in_=i, func=AF.Square, accum_out=a))
            r.do("scalar", lambda h, i=rt[6], o=rt[7]:
                 h.activation(out=o, in_=i, func=AF.Ln, scale=1.0 / D, bias=1e-6))
            a_rstd2 = r.do("scalar", lambda h, i=rt[7], o=rows["rstd2"][:, gi:gi + 1]:
                           h.activation(out=o, in_=i, func=AF.Exp, scale=-0.5))
            ms[gi]["a_rstd2"] = a_rstd2
            r.w("vector", "act", a_rstd2)
            d_hn2 = r.do("vector", lambda h, o=hn2_b[par], i=h2_b[k % 6],
                         sc=rows["rstd2"][:, gi:gi + 1]:
                         h.tensor_scalar_mul(out=o, in0=i, scalar1=sc))
            r.w("sync", "dve", ms[gi]["d_h2"])
            L(h2_d[gi], h2_b[k % 6], "sh2st")
            r.w("sync", "dve", d_hn2)
            ms[gi]["st_h2"] = L(hn2_d[gi], hn2_b[par], "sh2st")
            ms[gi]["pe_end"] = ms[gi]["p_pj"]

        for kk in range(NT + 5):
            for ph, off in ((S6, 5), (S5, 4), (S4, 3), (S3, 2), (S2, 1), (S1, 0)):
                kp = kk - off
                if 0 <= kp < NT:
                    ph(kp)

        if "dbg_rows" in dbg:
            for k2, nm2 in enumerate(["m", "cln", "ccomb", "mcln", "mccomb", "rstd2", "suma", "ssqa"]):
                r.w("sync", "dve", r.cnt["dve"])
                r.w("sync", "act", r.cnt["act"])
                L(dbg_rows[:, k2 * NT:(k2 + 1) * NT], rows[nm2], "sh2st")
        r.barrier()
        sc12.close()
        if upi < 1:
            finish(r, nc, sems, out_d, NT)
            return nc, di

        sc3, AB, AFr = open_scope("p3", 60550, 4200)
        w2_bf = AB.alloc(FFC * D)
        hn2T = AB.alloc(DC * 512)
        hid_b = AB.alloc(FFC * 512)
        h2t_b = [AB.alloc(D) for _ in range(2)]
        aout_b = [AB.alloc(D) for _ in range(2)]
        wqstg = [AB.alloc(D) for _ in range(2)]
        wstg3 = [AFr.alloc(D) for _ in range(2)]
        qscr = AB.alloc(D)
        x0p3 = [AFr.alloc(D) for _ in range(2)]

        for qq in range(4):
            L(w2_bf[:, 8192 * qq:8192 * qq + 8192].rearrange("p (m e) -> p m e", m=8),
              di["w2"][1024 * qq:1024 * qq + 1024, :].rearrange("(m p) e -> p m e", p=128),
              "pw")
        r.do("vector", lambda h: h.reduce_sum(out=rtq, in_=part_arr[:, 0:32], axis=AX.X))
        d_pa = r.do("vector", lambda h: h.reduce_sum(out=rtq2b, in_=part_arr[:, 32:64], axis=AX.X))
        r.w("tensor", "dve", d_pa)
        thr_sc = pb[1][0:1, 460:462]
        bcq_ps = pb[1][:, 462:464]
        for i, pp in enumerate((rtq, rtq2b)):
            r.do("tensor", lambda h, pp=pp, i=i: h.matmul(
                thr_sc[0:1, i:i + 1], ones_col_f, pp, start=True, stop=True), inc="pe")
        p_thr = r.cnt["pe"]
        rtq2 = AFr.alloc(16)[0:1, 0:2]
        r.w("vector", "pe", p_thr)
        r.do("vector", lambda h: h.tensor_scalar_mul(out=rtq2, in0=thr_sc, scalar1=0.5 / (D * FF)), inc="dve")
        d_thr = r.cnt["dve"]
        r.w("tensor", "dve", d_thr)
        r.do("tensor", lambda h: h.matmul(bcq_ps, ones_row_f, rtq2, start=True, stop=True), inc="pe")
        p_thrb = r.cnt["pe"]
        r.w("vector", "pe", p_thrb)
        r.do("vector", lambda h: h.tensor_copy(out=thr_up, in_=bcq_ps[:, 0:1]))
        r.do("vector", lambda h: h.tensor_copy(out=thr_dn, in_=bcq_ps[:, 1:2]))
        r.do("vector", lambda h: h.tensor_scalar_mul(out=nthr_up, in0=thr_up, scalar1=-1.0))
        r.do("vector", lambda h: h.tensor_scalar_mul(out=nthr_dn, in0=thr_dn, scalar1=-1.0), inc="dve")

        d_alpha = r.cnt["dve"]
        lastq = {}

        def emit_quant(i):
            nm, srct, a, b = tiles_w[i]
            r.w("gpsimd", "dve", lastq.get(i - 2, d_alpha))
            lw = L(wstg3[i % 2],
                   srct[128 * a:128 * a + 128, 1024 * b:1024 * b + 1024] if nm == "up"
                   else srct[128 * a:128 * a + 128, :], "pw")
            thr, nthr = (thr_up, nthr_up) if nm == "up" else (thr_dn, nthr_dn)
            dstd = wqup_d if nm == "up" else wqdn_d
            dst = dstd[128 * a:128 * a + 128, 1024 * b:1024 * b + 1024] if nm == "up" \
                else dstd[128 * a:128 * a + 128, :]
            r.w("vector", "pw", lw)
            r.w("vector", "pst", lastq.get(("st", i - 2), 0))
            r.do("vector", lambda h, i=i, nthr=nthr: h.tensor_scalar(
                out=qscr, in0=wstg3[i % 2], scalar1=nthr, scalar2=None, op0=ALU.is_lt))
            lastq[i] = r.do("vector", lambda h, i=i, thr=thr: h.scalar_tensor_tensor(
                out=wqstg[i % 2], in0=wstg3[i % 2], scalar=thr, in1=qscr,
                op0=ALU.is_gt, op1=ALU.subtract), inc="dve")
            r.w("gpsimd", "dve", lastq[i])
            lastq[("st", i)] = L(dst, wqstg[i % 2], "pst")
        r.barrier()

        r.do("scalar", preload_set(nc, [AF.Gelu_apprx_tanh, AF.Square]))
        st_a = {}
        d_a2 = {}
        d_dot = {}
        ld_dot = {}
        g_ms = [dict() for _ in range(NG)]
        def GM(g, k):
            return g_ms[g][k] if g >= 0 else 0
        for g in range(NG):
            r.w("sync", "pe", GM(g - 1, "p_mlp1"))
            for c in range(DC):
                src = hn2_d[4 * g:4 * g + 4, :, 128 * c:128 * c + 128].rearrange("g p d -> (g p) d")
                stl = ("shl", "sh2st")[c % 2]
                r.w("sync", stl, r.cnt[stl])
                g_ms[g][f"ld_t{c % 2}"] = r.do("sync", lambda h, c=c, s=src: h.dma_start_transpose(
                    hn2T[:, 512 * c:512 * c + 512], s), inc=stl)
            r.w("tensor", "shl", g_ms[g]["ld_t0"])
            r.w("tensor", "sh2st", g_ms[g]["ld_t1"])
            a_hid = {}
            for m in range(FFC):
                bank = pb[m % 2][:, :]
                if m >= 2:
                    r.w("tensor", "act", a_hid[m - 2])
                else:
                    r.w("tensor", "act", GM(g - 1, "a_hid"))
                pm = None
                for c in range(DC):
                    pm = r.do("tensor", lambda h, m=m, c=c, bank=bank: h.matmul(
                        bank, w1_bf[:, FF * c + 128 * m:FF * c + 128 * m + 128],
                        hn2T[:, 512 * c:512 * c + 512],
                        start=(c == 0), stop=(c == DC - 1)),
                        inc="pe" if c == DC - 1 else None)
                r.w("scalar", "pe", pm)
                a_hid[m] = r.do("scalar", lambda h, m=m, bank=bank: h.activation(
                    out=hid_b[:, 512 * m:512 * m + 512], in_=bank,
                    func=AF.Gelu_apprx_tanh, bias=b1r[:, m:m + 1]), inc="act")
            g_ms[g]["p_mlp1"] = r.cnt["pe"]
            g_ms[g]["a_hid"] = a_hid[FFC - 1]
            d_aev = 0
            for ps4 in range(4):
                ti = 4 * g + ps4
                r.w("gpsimd", "dve", d_a2.get(2 * (ti - 2) + 1, 0))
                ld_h2 = L(h2t_b[ti % 2], h2_d[ti], "pld")
                for eh in range(2):
                    bank = pb[2 + (2 * ps4 + eh) % 2][:, :]
                    r.w("tensor", "act", a_hid[FFC - 1])
                    r.w("tensor", "dve", d_a2.get(2 * ti + eh - 2, 0))
                    for m in range(FFC):
                        r.do("tensor", lambda h, m=m, ps4=ps4, eh=eh, bank=bank: h.matmul(
                            bank, hid_b[:, 512 * m + 128 * ps4:512 * m + 128 * ps4 + 128],
                            w2_bf[:, D * m + 512 * eh:D * m + 512 * eh + 512],
                            start=(m == 0), stop=False))
                    pm = r.do("tensor", lambda h, eh=eh, bank=bank: h.matmul(
                        bank, ones_row_b, b2row_b[0:1, 512 * eh:512 * eh + 512],
                        start=False, stop=True), inc="pe")
                    r.w("vector", "pe", pm)
                    r.w("vector", "pld", ld_h2)
                    r.w("vector", "pst", st_a.get(ti - 2, 0))
                    r.do("vector", lambda h, ps4=ps4, eh=eh, bank=bank, h2t=h2t_b[ti % 2]:
                         h.scalar_tensor_tensor(
                        out=aout_b[ps4 % 2][:, 512 * eh:512 * eh + 512], in0=bank, scalar=1.0,
                        in1=h2t[:, 512 * eh:512 * eh + 512],
                        op0=ALU.mult, op1=ALU.add, accum_out=rt[eh]))
                    d_a2[2 * ti + eh] = r.do(
                        "vector", lambda h: h.engine_nop(), inc="dve")
                r.do("vector", lambda h, o=rows["suma"][:, ti:ti + 1]:
                     h.tensor_add(out=o, in0=rt[0], in1=rt[1]), inc="dve")
                d_sum = r.cnt["dve"]
                r.w("scalar", "dve", d_sum)
                r.do("scalar", lambda h, ps4=ps4, a=rt[2]: h.activation(
                    out=qscr[:, 0:512], in_=aout_b[ps4 % 2][:, 0:512], func=AF.Square, accum_out=a))
                r.do("scalar", lambda h, ps4=ps4, a=rt[3]: h.activation(
                    out=qscr[:, 512:1024], in_=aout_b[ps4 % 2][:, 512:1024], func=AF.Square, accum_out=a), inc="act")
                a_ss = r.cnt["act"]
                r.w("vector", "act", a_ss)
                r.do("vector", lambda h, o=rows["ssqa"][:, ti:ti + 1]:
                     h.tensor_add(out=o, in0=rt[2], in1=rt[3]), inc="dve")
                if ti == 0:
                    ld_dot[0] = L(x0p3[0], di["x0"][0, 0:128, :], "pxt")
                if ti + 1 < NT:
                    r.w("gpsimd", "dve", d_dot.get(ti - 1, 0))
                    ld_dot[ti + 1] = L(x0p3[(ti + 1) % 2],
                                       di["x0"][(ti + 1) // seqt,
                                                128 * ((ti + 1) % seqt):128 * ((ti + 1) % seqt) + 128, :],
                                       "pxt")
                r.w("vector", "pxt", ld_dot[ti])
                d_dot[ti] = r.do("vector", lambda h, ti=ti, ps4=ps4: h.scalar_tensor_tensor(
                    out=qscr, in0=x0p3[ti % 2], scalar=1.0, in1=aout_b[ps4 % 2],
                    op0=ALU.mult, op1=ALU.mult, accum_out=rows["Sdot"][:, ti:ti + 1]))
                r.w("gpsimd", "act", a_ss)
                r.w("gpsimd", "dve", d_a2[2 * ti + 1])
                st_a[ti] = L(aout_d[ti], aout_b[ps4 % 2], "pst")
                for i2 in range(ti * len(tiles_w) // NT, (ti + 1) * len(tiles_w) // NT):
                    emit_quant(i2)
            g_ms[g]["d_aev"] = r.cnt["dve"]
        r.barrier()
        sc3.close()
        scW.close()
        if upi < 2:
            finish(r, nc, sems, out_d, NT)
            return nc, di

        sc4, AB, AFr = open_scope("p4", 94230, 2760)
        wqu_b = AB.alloc(DC * FF)
        wqd_b = AB.alloc(FFC * D)
        finT = AB.alloc(DC * 512)
        g_b = AB.alloc(FFC * 512)
        raw_b = [AB.alloc(D) for _ in range(4)]
        ao4_b = [AB.alloc(D) for _ in range(2)]
        fin4_b = ao4_b
        gsq_b = [AB.alloc(512) for _ in range(4)]
        ssq_row = AFr.alloc(512)[0:1, :]
        x0p4 = [AFr.alloc(D) for _ in range(2)]
        x1l_b = x0p4
        lad_q = AFr.alloc(2 * NT); lad_y = AFr.alloc(2 * NT); lad_t = AFr.alloc(2 * NT)

        def emit_rsqrt(qap, yap, tap, q0, levels, iters=8):
            base = (1.0 / math.sqrt(q0)) * (2.0 ** (-0.5 - levels))
            r.do("vector", lambda h, y=yap, b=base: h.memset(y, b))
            for k in range(1, levels + 1):
                thr = q0 * (4.0 ** k)
                r.do("vector", lambda h, q=qap, t=tap, thr=thr: h.tensor_scalar(
                    out=t, in0=q, scalar1=thr, scalar2=1.0, op0=ALU.is_le, op1=ALU.add))
                r.do("vector", lambda h, y=yap, t=tap: h.tensor_mul(out=y, in0=y, in1=t))
            for _ in range(iters):
                r.do("vector", lambda h, y=yap, t=tap: h.tensor_mul(out=t, in0=y, in1=y))
                r.do("vector", lambda h, q=qap, t=tap: h.scalar_tensor_tensor(
                    out=t, in0=q, scalar=-0.5, in1=t, op0=ALU.mult, op1=ALU.mult))
                r.do("vector", lambda h, t=tap: h.tensor_scalar(
                    out=t, in0=t, scalar1=1.5, scalar2=None, op0=ALU.add))
                r.do("vector", lambda h, y=yap, t=tap: h.tensor_mul(out=y, in0=y, in1=t))
            return r.cnt["dve"]

        r.do("scalar", preload_set(nc, [AF.Gelu, AF.Square]))
        for qq in range(4):
            L(wqu_b[:, 8192 * qq:8192 * qq + 8192].rearrange("p (c f) -> p c f", c=2),
              wqup_d[256 * qq:256 * qq + 256, :].rearrange("(c p) f -> p c f", p=128), "pw")
        for qq in range(4):
            L(wqd_b[:, 8192 * qq:8192 * qq + 8192].rearrange("p (m e) -> p m e", m=8),
              wqdn_d[1024 * qq:1024 * qq + 1024, :].rearrange("(m p) e -> p m e", p=128), "pw")
        lq = r.cnt["pw"]

        r.do("vector", lambda h: h.tensor_scalar_mul(out=rows["suma"], in0=rows["suma"], scalar1=1.0 / D))
        r.do("vector", lambda h: h.tensor_scalar_mul(out=rows["ssqa"], in0=rows["ssqa"], scalar1=1.0 / D))
        r.do("vector", lambda h: h.tensor_mul(out=rows["B3"], in0=rows["suma"], in1=rows["suma"]))
        r.do("vector", lambda h: h.tensor_sub(out=rows["ssqa"], in0=rows["ssqa"], in1=rows["B3"]))
        r.do("vector", lambda h: h.tensor_scalar(out=rows["mcln"], in0=rows["ssqa"], scalar1=1e-5,
                                                 scalar2=None, op0=ALU.add))
        emit_rsqrt(rows["mcln"], rows["A3"], rows["mccomb"], 1e-5, 17)
        r.do("vector", lambda h: h.tensor_mul(out=rows["B3"], in0=rows["suma"], in1=rows["A3"]))
        r.do("vector", lambda h: h.tensor_scalar_mul(out=rows["B3"], in0=rows["B3"], scalar1=-1.0))
        r.do("vector", lambda h: h.tensor_scalar_mul(out=rows["Sdot"], in0=rows["Sdot"], scalar1=1.0 / D))
        r.do("vector", lambda h: h.tensor_mul(out=rows["M4"], in0=rows["suma"], in1=rows["m"]))
        r.do("vector", lambda h: h.tensor_sub(out=rows["Sdot"], in0=rows["Sdot"], in1=rows["M4"]))
        r.do("vector", lambda h: h.tensor_mul(out=rows["Sdot"], in0=rows["Sdot"], in1=rows["A3"]))
        r.do("vector", lambda h: h.reciprocal(out=rows["mccomb"], in_=rows["mcln"]))
        r.do("vector", lambda h: h.tensor_mul(out=rows["mccomb"], in0=rows["ssqa"], in1=rows["mccomb"]))
        r.do("vector", lambda h: h.scalar_tensor_tensor(out=rows["mcln"], in0=rows["Sdot"], scalar=2.0,
                                                        in1=rows["mccomb"], op0=ALU.mult, op1=ALU.add))
        r.do("vector", lambda h: h.scalar_tensor_tensor(out=rows["mcln"], in0=rows["v0"],
                                                        scalar=-1.0 / D, in1=rows["mcln"],
                                                        op0=ALU.mult, op1=ALU.add))
        r.do("vector", lambda h: h.tensor_scalar(out=lad_q[:, 0:NT], in0=rows["mcln"], scalar1=1e-5,
                                                 scalar2=None, op0=ALU.add))
        r.do("vector", lambda h: h.tensor_scalar(out=lad_q[:, NT:2 * NT], in0=rows["mcln"], scalar1=1e-12,
                                                 scalar2=None, op0=ALU.add))
        emit_rsqrt(lad_q, lad_y, lad_t, 1e-12, 27)
        r.do("vector", lambda h: h.tensor_mul(out=lad_t[:, 0:NT], in0=rows["mcln"], in1=lad_y[:, NT:2 * NT]))
        r.do("vector", lambda h: h.tensor_mul(out=lad_t[:, 0:NT], in0=lad_t[:, 0:NT], in1=lad_y[:, 0:NT]))
        r.do("vector", lambda h: h.tensor_scalar(out=lad_t[:, 0:NT], in0=lad_t[:, 0:NT], scalar1=1e-8,
                                                 scalar2=None, op0=ALU.add))
        r.do("vector", lambda h: h.reciprocal(out=lad_t[:, 0:NT], in_=lad_t[:, 0:NT]))
        r.do("vector", lambda h: h.tensor_mul(out=rows["S4"], in0=lad_y[:, 0:NT], in1=lad_t[:, 0:NT]))
        r.do("vector", lambda h: h.tensor_sub(out=rows["M4"], in0=rows["B3"], in1=rows["m"]))
        r.do("vector", lambda h: h.tensor_mul(out=rows["M4"], in0=rows["M4"], in1=rows["S4"]))

        fold_ms = {}
        def FM(key, f):
            return fold_ms[(key, f)] if f >= 0 else 0
        def emit_fold(f):
            s_, t_ = f // seqt, f % seqt
            r.w("sync", "dve", FM("d_x1", f - 2))
            r.w("sync", "swst", FM("st_x1", f - 2))
            r.w("sync", "sh2st", out_ms.get(("st_o", f - (f % 4) - 6), 0))
            r.w("sync", "pst", out_ms.get(("st_o", f - (f % 4) - 5), 0))
            lx = L(x0p4[f % 2], di["x0"][s_, 128 * t_:128 * t_ + 128, :], "sx0")
            r.w("gpsimd", "dve", FM("d_t", f - 2))
            r.w("gpsimd", "pxt", FM("st_f", f - 2))
            la = L(ao4_b[f % 2], aout_d[f], "pld")
            r.w("vector", "sx0", lx)
            r.w("vector", "pld", la)
            fold_ms[("d_t", f)] = r.do("vector", lambda h, f=f: h.scalar_tensor_tensor(
                out=x0p4[f % 2], in0=ao4_b[f % 2], scalar=rows["A3"][:, f:f + 1],
                in1=x0p4[f % 2], op0=ALU.mult, op1=ALU.add))
            r.w("vector", "pxt", FM("st_f", f - 2))
            d_f = r.do("vector", lambda h, f=f: h.tensor_scalar(
                out=fin4_b[f % 2], in0=x0p4[f % 2], scalar1=rows["S4"][:, f:f + 1],
                scalar2=rows["M4"][:, f:f + 1], op0=ALU.mult, op1=ALU.add))
            r.w("vector", "swst", FM("st_x1", f - 2))
            d_x1 = r.do("vector", lambda h, f=f: h.tensor_scalar(
                out=x0p4[f % 2], in0=x0p4[f % 2], scalar1=rows["B3"][:, f:f + 1],
                scalar2=None, op0=ALU.add))
            fold_ms[("d_x1", f)] = d_x1
            r.w("sync", "dve", d_x1)
            fold_ms[("st_x1", f)] = L(x1_d[f], x0p4[f % 2], "swst")
            r.w("gpsimd", "dve", d_f)
            fold_ms[("st_f", f)] = L(fin_d[f], fin4_b[f % 2], "pxt")

        out_ms = {}
        def OM(key, f):
            return out_ms[(key, f)] if f >= 0 else 0
        def emit_out(tj):
            lastf = min(tj - (tj % 4) + 7, NT - 1)
            r.w("gpsimd", "swst", fold_ms[("st_x1", lastf)])
            r.w("gpsimd", "dve", FM("d_x1", lastf))
            r.w("gpsimd", ("sh2st", "pst")[tj % 2], OM("st_o", tj - 2))
            lx1 = L(x1l_b[tj % 2], x1_d[tj], "pw")
            r.w("vector", "pw", lx1)
            r.do("vector", lambda h, tj=tj: h.scalar_tensor_tensor(
                out=x1l_b[tj % 2], in0=raw_b[tj % 4], scalar=rows["A4"][:, tj:tj + 1],
                in1=x1l_b[tj % 2], op0=ALU.mult, op1=ALU.add))
            ost = ("sh2st", "pst")[tj % 2]
            r.w("vector", ost, OM("st_o", tj - 2))
            d_o = r.do("vector", lambda h, tj=tj: h.tensor_scalar(
                out=x1l_b[tj % 2], in0=x1l_b[tj % 2], scalar1=rows["B4"][:, tj:tj + 1],
                scalar2=None, op0=ALU.add))
            out_ms[("d_o", tj)] = d_o
            r.w("sync" if ost[0] == "s" else "gpsimd", "dve", d_o)
            out_ms[("st_o", tj)] = L(out_d[tj], x1l_b[tj % 2], ost)

        def emit_a4rows(g):
            c0 = 4 * g
            cs = slice(c0, c0 + 4)
            r.w("vector", "act", g4_ms[g]["d_ssqg"])
            r.do("vector", lambda h, cs=cs: h.tensor_scalar_mul(
                out=rows["sumr"][:, cs], in0=rows["sumr"][:, cs], scalar1=1.0 / D))
            r.do("vector", lambda h, cs=cs: h.tensor_scalar_mul(
                out=rows["ssqr"][:, cs], in0=rows["ssqr"][:, cs], scalar1=1.0 / D))
            r.do("vector", lambda h, cs=cs: h.tensor_mul(
                out=lad_t[:, 0:4], in0=rows["sumr"][:, cs], in1=rows["sumr"][:, cs]))
            r.do("vector", lambda h, cs=cs: h.tensor_sub(
                out=rows["ssqr"][:, cs], in0=rows["ssqr"][:, cs], in1=lad_t[:, 0:4]))
            r.do("vector", lambda h, cs=cs: h.tensor_scalar(
                out=lad_q[:, 0:4], in0=rows["ssqg"][:, cs], scalar1=1.0 / FF, scalar2=1e-16,
                op0=ALU.mult, op1=ALU.add))
            emit_rsqrt(lad_q[:, 0:4], lad_y[:, 0:4], lad_t[:, 0:4], 1e-16, 36)
            r.do("vector", lambda h, cs=cs: h.tensor_scalar_mul(
                out=lad_t[:, 0:4], in0=rows["ssqg"][:, cs], scalar1=1.0 / FF))
            r.do("vector", lambda h: h.tensor_mul(out=lad_y[:, 0:4], in0=lad_y[:, 0:4], in1=lad_t[:, 0:4]))
            r.do("vector", lambda h: h.tensor_scalar(out=lad_y[:, 0:4], in0=lad_y[:, 0:4],
                                                     scalar1=1e-8, scalar2=None, op0=ALU.add))
            r.do("vector", lambda h: h.reciprocal(out=lad_y[:, 0:4], in_=lad_y[:, 0:4]))
            r.do("vector", lambda h: h.tensor_mul(out=lad_t[:, 0:4], in0=lad_y[:, 0:4], in1=lad_y[:, 0:4]))
            r.do("vector", lambda h, cs=cs: h.tensor_mul(
                out=lad_q[:, 4:8], in0=lad_t[:, 0:4], in1=rows["ssqr"][:, cs]))
            r.do("vector", lambda h: h.tensor_scalar(out=lad_q[:, 4:8], in0=lad_q[:, 4:8],
                                                     scalar1=1e-5, scalar2=None, op0=ALU.add))
            emit_rsqrt(lad_q[:, 4:8], lad_y[:, 4:8], lad_t[:, 4:8], 1e-5, 19)
            r.do("vector", lambda h, cs=cs: h.tensor_mul(
                out=rows["A4"][:, cs], in0=lad_y[:, 0:4], in1=lad_y[:, 4:8]))
            r.do("vector", lambda h, cs=cs: h.scalar_tensor_tensor(
                out=rows["B4"][:, cs], in0=rows["sumr"][:, cs], scalar=-1.0,
                in1=rows["A4"][:, cs], op0=ALU.mult, op1=ALU.mult))

        for f in range(min(4, NT)):
            emit_fold(f)

        ssq4_ps = pb[4][0:1, 0:512]
        tr4_ps = pb[5][:, 0:4]
        g4_ms = [dict() for _ in range(NG)]
        d_rev, a_ssr = {}, {}
        def G4(g, k):
            return g4_ms[g][k] if g >= 0 else 0
        for g in range(NG):
            r.w("sync", "pe", G4(g - 1, "p_up"))
            r.w("sync", "pxt", fold_ms[("st_f", 4 * g + 3)])
            for c in range(DC):
                src = fin_d[4 * g:4 * g + 4, :, 128 * c:128 * c + 128].rearrange("g p d -> (g p) d")
                stl = ("shl", "sx0")[c % 2]
                r.w("sync", stl, r.cnt[stl])
                g4_ms[g][f"ld_t{c % 2}"] = r.do("sync", lambda h, c=c, s=src: h.dma_start_transpose(
                    finT[:, 512 * c:512 * c + 512], s), inc=stl)
            if g >= 1:
                for j in range(4):
                    emit_out(4 * (g - 1) + j)
            r.w("tensor", "shl", g4_ms[g]["ld_t0"])
            r.w("tensor", "sx0", g4_ms[g]["ld_t1"])
            if g == 0:
                r.w("tensor", "pw", lq)
            a_g = {}
            d_gsq = {}
            for m in range(FFC):
                bank = pb[m % 2][:, :]
                if m >= 2:
                    r.w("tensor", "act", a_g[m - 2])
                else:
                    r.w("tensor", "act", G4(g - 1, "a_glast"))
                    r.w("tensor", "dve", G4(g - 1, "d_srow"))
                pm = None
                for c in range(DC):
                    pm = r.do("tensor", lambda h, m=m, c=c, bank=bank: h.matmul(
                        bank, wqu_b[:, FF * c + 128 * m:FF * c + 128 * m + 128],
                        finT[:, 512 * c:512 * c + 512],
                        start=(c == 0), stop=(c == DC - 1)),
                        inc="pe" if c == DC - 1 else None)
                r.w("scalar", "pe", pm)
                a_g[m] = r.do("scalar", lambda h, m=m, bank=bank: h.activation(
                    out=g_b[:, 512 * m:512 * m + 512], in_=bank, func=AF.Gelu), inc="act")
                if m >= 1:
                    if m >= 5:
                        r.w("scalar", "pe", g4_ms[g].get(f"p_ssq_{m-5}", 0))
                    else:
                        r.w("scalar", "pe", G4(g - 1, f"p_ssq_{m+27}"))
                    d_gsq[m - 1] = r.do("scalar", lambda h, m1=m - 1: h.activation(
                        out=gsq_b[m1 % 4], in_=g_b[:, 512 * m1:512 * m1 + 512],
                        func=AF.Square), inc="act")
                if m % 8 == 4 and 4 * g + 4 + m // 8 < NT:
                    emit_fold(4 * g + 4 + m // 8)
                if m >= 2:
                    r.w("tensor", "act", d_gsq[m - 2])
                    g4_ms[g][f"p_ssq_{m-2}"] = r.do("tensor", lambda h, m1=m - 2: h.matmul(
                        ssq4_ps, ones_col_b, gsq_b[m1 % 4],
                        start=(m1 == 0), stop=False), inc="pe")
            r.w("scalar", "pe", g4_ms[g][f"p_ssq_{FFC-5}"])
            d_gsq[FFC - 1] = r.do("scalar", lambda h: h.activation(
                out=gsq_b[(FFC - 1) % 4], in_=g_b[:, 512 * (FFC - 1):512 * FFC],
                func=AF.Square), inc="act")
            for m1 in (FFC - 2, FFC - 1):
                r.w("tensor", "act", d_gsq[m1])
                g4_ms[g][f"p_ssq_{m1}"] = r.do("tensor", lambda h, m1=m1: h.matmul(
                    ssq4_ps, ones_col_b, gsq_b[m1 % 4],
                    start=False, stop=(m1 == FFC - 1)), inc="pe")
            g4_ms[g]["p_up"] = r.cnt["pe"]
            g4_ms[g]["a_glast"] = a_g[FFC - 1]
            r.w("scalar", "pe", g4_ms[g]["p_up"])
            g4_ms[g]["d_srow"] = r.do(
                "scalar", lambda h: h.copy(out=ssq_row, in_=ssq4_ps), inc="act")
            r.w("tensor", "act", g4_ms[g]["d_srow"])
            r.w("tensor", "act", G4(g - 1, "d_ssqg"))
            for j in range(4):
                r.do("tensor", lambda h, j=j: h.transpose(
                    tr4_ps[:, j:j + 1], ssq_row[0:1, 128 * j:128 * j + 128], ident_f[0:1, 0:1]),
                    inc="pe" if j == 3 else None)
            p_tr = r.cnt["pe"]
            r.w("scalar", "pe", p_tr)
            for j in range(4):
                r.do("scalar", lambda h, j=j, ti=4 * g + j: h.copy(
                    out=rows["ssqg"][:, ti:ti + 1], in_=tr4_ps[:, j:j + 1]),
                    inc="act" if j == 3 else None)
            g4_ms[g]["d_ssqg"] = r.cnt["act"]
            for ps4 in range(4):
                ti = 4 * g + ps4
                for eh in range(2):
                    bank = pb[2 + (2 * ps4 + eh) % 2][:, :]
                    r.w("tensor", "dve", d_rev.get(2 * ti + eh - 2, 0))
                    pm = None
                    for m in range(FFC):
                        pm = r.do("tensor", lambda h, m=m, ps4=ps4, eh=eh, bank=bank: h.matmul(
                            bank, g_b[:, 512 * m + 128 * ps4:512 * m + 128 * ps4 + 128],
                            wqd_b[:, D * m + 512 * eh:D * m + 512 * eh + 512],
                            start=(m == 0), stop=(m == FFC - 1)),
                            inc="pe" if m == FFC - 1 else None)
                    r.w("vector", "pe", pm)
                    r.w("vector", "act", a_ssr.get(ti - 2, 0))
                    r.do("vector", lambda h, ti=ti, eh=eh, bank=bank: h.tensor_scalar(
                        out=raw_b[ti % 4][:, 512 * eh:512 * eh + 512], in0=bank,
                        scalar1=1.0, scalar2=None, op0=ALU.mult, op1=ALU.add, accum_out=rt[eh]))
                    d_rev[2 * ti + eh] = r.do("vector", lambda h: h.engine_nop(), inc="dve")
                r.do("vector", lambda h, ti=ti: h.tensor_add(
                    out=rows["sumr"][:, ti:ti + 1], in0=rt[0], in1=rt[1]), inc="dve")
                d_sr2 = r.cnt["dve"]
                r.w("scalar", "dve", d_sr2)
                r.do("scalar", lambda h, ti=ti, a=rt[2]: h.activation(
                    out=gsq_b[0], in_=raw_b[ti % 4][:, 0:512], func=AF.Square, accum_out=a))
                r.do("scalar", lambda h, ti=ti, a=rt[3]: h.activation(
                    out=gsq_b[1], in_=raw_b[ti % 4][:, 512:1024], func=AF.Square, accum_out=a), inc="act")
                a_ssr[ti] = r.cnt["act"]
                r.w("vector", "act", a_ssr[ti])
                r.do("vector", lambda h, ti=ti: h.tensor_add(
                    out=rows["ssqr"][:, ti:ti + 1], in0=rt[2], in1=rt[3]), inc="dve")
                if "raw" in dbg:
                    r.w("gpsimd", "dve", d_rev[2 * ti + 1])
                    L(raw_d[ti], raw_b[ti % 4], "pst")
            emit_a4rows(g)
        for j in range(4):
            emit_out(4 * (NG - 1) + j)
        r.barrier()
        sc4.close()

        finish(r, nc, sems, out_d, NT)
        return nc, di


def finish(r, nc, sems, out_d, NT):
    snap = dict(r.cnt)
    for e in ENGINES:
        for s in DMA_SEMS:
            r.w(e, s, snap[s])
    r.replay(nc, sems)


_CACHE = {}


def kernel(**inputs):
    nseq, S = 2, 2048
    key = (nseq, S)
    if key not in _CACHE:
        _CACHE[key] = build(nseq, S)
    nc, _ = _CACHE[key]
    in_maps = [host_prep(c, inputs, nseq, S) for c in range(8)]
    res = run_bass_kernel_spmd(nc, in_maps, core_ids=list(range(8)))
    seqt = S // 128
    outs = []
    for c in range(8):
        o = np.asarray(res.results[c]["out"])
        outs.append(o.reshape(nseq, S, D))
    return np.concatenate(outs, 0).astype(np.float32)


# version 34
# speedup vs baseline: 1.0529x; 1.0529x over previous
import contextlib
import math
import numpy as np
import ml_dtypes

import concourse.bass as bass
import concourse.mybir as mybir
from concourse.bass_utils import run_bass_kernel_spmd

F32 = mybir.dt.float32
BF16 = mybir.dt.bfloat16
AF = mybir.ActivationFunctionType
ALU = mybir.AluOpType
AX = mybir.AxisListType
BFNP = ml_dtypes.bfloat16

D = 1024
DC = 8
R = 16
FF = 4096
FFC = 32
CONVK = 4
GAMMA_MIN, GAMMA_MAX = 0.15, 1.0

ENGINES = ("tensor", "vector", "scalar", "sync", "gpsimd")


def preload_set(nc, needed):
    from concourse.hw_specs import get_activation_tables
    tabs = list(get_activation_tables(nc.m.arch).values())
    for idx, funcs in enumerate(tabs):
        if all(f in funcs for f in needed):
            def fn(h, idx=idx):
                return h.add_instruction(mybir.InstLoadActFuncSet(
                    name=nc.get_next_instruction_name(),
                    act_func_set_id=idx, ins=[], outs=[]))
            return fn
    raise ValueError(f"no table set contains {needed}")
DMA_SEMS = ("sx0", "swst", "sh2st", "shl", "pxt", "pw", "pst", "pld")
SEMS = ("pe", "dve", "act", "pool") + DMA_SEMS
ESEM = {"tensor": "pe", "vector": "dve", "scalar": "act", "gpsimd": "pool"}


class Rec:

    def __init__(self):
        self.ops = {e: [] for e in ENGINES}
        self.cnt = {s: 0 for s in SEMS}

    def w(self, eng, sem, val):
        if val and val > 0:
            self.ops[eng].append(("wait", sem, val))

    def do(self, eng, fn, inc=None, sw=True):
        if eng in ("vector", "scalar") or (eng == "gpsimd" and inc not in DMA_SEMS):
            own = ESEM[eng]
            if sw:
                self.w(eng, own, self.cnt[own])
            self.cnt[own] += 1
            self.ops[eng].append(("op", fn, own))
            if inc is not None and inc != own:
                raise ValueError(f"{eng} ops auto-inc {own}; got inc={inc}")
            return self.cnt[own]
        if inc is not None:
            self.cnt[inc] += 16 if inc in DMA_SEMS else 1
        self.ops[eng].append(("op", fn, inc))
        return self.cnt[inc] if inc else None

    def barrier(self):
        engines = ("tensor", "vector", "scalar", "gpsimd")
        snap0 = dict(self.cnt)
        for e in engines:
            for s in DMA_SEMS:
                self.w(e, s, snap0[s])
        self.do("tensor", self.tensor_barrier_fn, inc="pe")
        self.do("vector", lambda h: h.engine_nop())
        self.do("scalar", self.scalar_barrier_fn)
        self.do("gpsimd", lambda h: h.engine_nop())
        snap = dict(self.cnt)
        for e in ENGINES:
            for e2 in engines:
                if e2 != e:
                    self.w(e, ESEM[e2], snap[ESEM[e2]])
            for s in DMA_SEMS:
                self.w(e, s, snap[s])

    def replay(self, nc, sems):
        def mk_body(ename):
            def body(h):
                for item in self.ops[ename]:
                    if item[0] == "wait":
                        h.wait_ge(sems[item[1]], item[2])
                    else:
                        inst = item[1](h)
                        if item[2] is not None:
                            inst.then_inc(sems[item[2]],
                                          16 if item[2] in DMA_SEMS else 1)
            return body
        with nc.Block() as block:
            block.tensor(mk_body("tensor"))
            block.vector(mk_body("vector"))
            block.scalar(mk_body("scalar"))
            block.sync(mk_body("sync"))
            block.gpsimd(mk_body("gpsimd"))


class Arena:
    def __init__(self, handle, cols):
        self.h, self.cols, self.off, self.marks = handle, cols, 0, []

    def alloc(self, ncols, align=16):
        o = -(-self.off // align) * align
        assert o + ncols <= self.cols, f"arena overflow {o}+{ncols}>{self.cols}"
        self.off = o + ncols
        return self.h[:, o:o + ncols]

    def push(self):
        self.marks.append(self.off)

    def pop(self):
        self.off = self.marks.pop()


def host_constants():
    c = {}
    c["ident_f"] = np.eye(128, dtype=np.float32)
    q = np.arange(128)[:, None]
    p = np.arange(128)[None, :]
    sq, sp = q // 32, p // 32
    c["bigmask"] = np.where((sp > sq) | ((sp == sq) & (p >= q)), 1.0, 0.0).astype(BFNP)
    for j in range(CONVK):
        c[f"econv{j}"] = ((p - q) == j).astype(np.float32)
    for j in range(1, CONVK):
        c[f"ebnd{j}"] = ((p + 128 - q) == j).astype(np.float32)
    pr = np.arange(128, dtype=np.float32)[None, :].repeat(R, 0)
    c["exp_qs"] = pr % 32
    for dlt in range(4):
        c[f"exp_kv{dlt}"] = 32.0 * dlt - (pr % 32)
    c["exp_qeff"] = pr + 1.0
    c["exp_keff"] = 127.0 - pr
    c["ones_row_b"] = np.ones((1, 128), dtype=BFNP)
    c["ones_row_f"] = np.ones((1, 128), dtype=np.float32)
    c["ones_col_f"] = np.ones((128, 1), dtype=np.float32)
    c["ones_col_b"] = np.ones((128, 1), dtype=BFNP)
    c["ones16_f"] = np.ones((R, 1), dtype=np.float32)
    c["ones16r_f"] = np.ones((1, R), dtype=np.float32)
    return c


def host_prep(core, inputs, nseq, S):
    f32 = np.float32
    h = np.asarray(inputs["L0_attn_h"], dtype=f32)
    x0 = np.ascontiguousarray(h[core * nseq:(core + 1) * nseq, :S])
    d = {"x0": x0, "x0t": np.ascontiguousarray(x0.transpose(0, 2, 1))}
    d["uv"] = np.concatenate([inputs["L0_attn_u"], inputs["L0_attn_v"]], axis=1)
    d["projr"] = np.asarray(inputs["L0_attn_proj_w"], f32).T
    d["w1"] = np.asarray(inputs["L0_attn_mlp_w1"], f32)
    d["w2"] = np.asarray(inputs["L0_attn_mlp_w2"], f32)
    d["b1r"] = np.asarray(inputs["L0_attn_mlp_b1"], f32).reshape(FFC, 128).T
    d["b2row"] = np.asarray(inputs["L0_attn_mlp_b2"], f32).reshape(1, D)
    d["pbrow"] = np.asarray(inputs["L0_attn_proj_b"], f32).reshape(1, D)
    d["wupT"] = np.asarray(inputs["L0_ffn_w_up"], f32).T
    d["wdnT"] = np.asarray(inputs["L0_ffn_w_down"], f32).T
    d["n1w"] = np.asarray(inputs["L0_attn_norm1_w"], f32).reshape(DC, 128).T
    d["n2w"] = np.asarray(inputs["L0_attn_norm2_w"], f32).reshape(DC, 128).T
    d["declog"] = np.asarray(inputs["L0_attn_decay_logit"], f32).reshape(R, 1)
    d["alplog"] = np.asarray(inputs["L0_attn_alpha_logit"], f32).reshape(R, 1)
    d["gatelog"] = np.asarray(inputs["L0_attn_k_base_gate_logit"], f32).reshape(1, 1)
    d["kern"] = np.asarray(inputs["L0_attn_k_base_kernel"], f32).reshape(1, CONVK)
    d.update(host_constants())
    return {k: np.ascontiguousarray(v) for k, v in d.items()}


def build(nseq=2, S=2048, dbg=(), upto="p4"):
    seqt = S // 128
    NT = nseq * seqt
    NG = NT // 4
    assert NT % 4 == 0 or upto == "p12" 
    PASSES = ["p12", "p3", "p4"]
    upi = PASSES.index(upto)
    nc = bass.Bass("TRN2", target_bir_lowering=False, debug=False)

    di = {}
    def inp(name, shape, dt=F32):
        di[name] = nc.dram_tensor(name, list(shape), dt, kind="ExternalInput").ap()
    inp("x0", (nseq, S, D)); inp("x0t", (nseq, D, S))
    inp("uv", (D, 2 * R)); inp("projr", (D, D))
    inp("w1", (D, FF)); inp("w2", (FF, D))
    inp("b1r", (128, FFC)); inp("b2row", (1, D)); inp("pbrow", (1, D))
    inp("wupT", (D, FF)); inp("wdnT", (FF, D))
    inp("n1w", (128, DC)); inp("n2w", (128, DC))
    inp("declog", (R, 1)); inp("alplog", (R, 1)); inp("gatelog", (1, 1))
    inp("kern", (1, CONVK))
    inp("ident_f", (128, 128)); inp("bigmask", (128, 128), BF16)
    for j in range(CONVK): inp(f"econv{j}", (128, 128))
    for j in range(1, CONVK): inp(f"ebnd{j}", (128, 128))
    inp("exp_qs", (R, 128))
    for dlt in range(4): inp(f"exp_kv{dlt}", (R, 128))
    inp("exp_qeff", (R, 128)); inp("exp_keff", (R, 128))
    inp("ones_row_b", (1, 128), BF16); inp("ones_row_f", (1, 128))
    inp("ones_col_f", (128, 1)); inp("ones_col_b", (128, 1), BF16)
    inp("ones16_f", (R, 1)); inp("ones16r_f", (1, R))

    out_d = nc.dram_tensor("out", [NT, 128, D], F32, kind="ExternalOutput").ap()
    def scr(name, shape, dt):
        kind = "ExternalOutput" if name in dbg else "Internal"
        return nc.dram_tensor(name, list(shape), dt, kind=kind).ap()
    h2_d = scr("h2", (NT, 128, D), BF16)
    hn2_d = scr("hn2", (NT, 128, D), BF16)
    aout_d = scr("aout", (NT, 128, D), BF16)
    x1_d = scr("x1", (NT, 128, D), F32)
    fin_d = scr("fin", (NT, 128, D), BF16)
    raw_d = scr("raw", (NT, 128, D), BF16)
    dbg_hn = scr("dbg_hn", (NT, 128, D), BF16)
    dbg_mo = scr("dbg_mo", (NT, 128, D), BF16)
    dbg_q = scr("dbg_q", (NT, R, 128), BF16)
    dbg_at = scr("dbg_at", (NT, 128, 128), BF16)
    dbg_rows = scr("dbg_rows", (128, 8 * NT), F32)
    wqup_d = scr("wqup", (D, FF), BF16)
    wqdn_d = scr("wqdn", (FF, D), BF16)

    r = Rec()
    es = contextlib.ExitStack()
    with es:
        sems = {s: es.enter_context(nc.semaphore(s)) for s in SEMS}
        ABp = Arena(es.enter_context(nc.sbuf_tensor("ab_p", [128, 1400], BF16)), 1400)
        AFp = Arena(es.enter_context(nc.sbuf_tensor("af_p", [128, 1510], F32)), 1510)
        pb = [es.enter_context(nc.psum_tensor(f"pb{i}", [128, 512], F32)) for i in range(8)]

        def open_scope(tag, nbf, nf):
            st = contextlib.ExitStack()
            ab = Arena(st.enter_context(
                nc.sbuf_tensor(f"ab_{tag}", [128, nbf], BF16)), nbf)
            af = Arena(st.enter_context(
                nc.sbuf_tensor(f"af_{tag}", [128, nf], F32)), nf)
            return st, ab, af

        ones_row_b = ABp.alloc(128)[0:1, :]
        ones_col_b = ABp.alloc(16)[:, 0:1]
        b2row_b = ABp.alloc(D)[0:1, :]
        ident_b = ABp.alloc(16)[0:1, 0:1]
        ident_f = AFp.alloc(128)
        ones_row_f = AFp.alloc(128)[0:1, :]
        ones_col_f = AFp.alloc(16)[:, 0:1]
        ones16_f = AFp.alloc(16)[0:R, 0:1]
        ones16r_f = AFp.alloc(16)[0:1, 0:R]
        n1w = AFp.alloc(DC); n2w = AFp.alloc(DC); b1r = AFp.alloc(FFC)
        rows = {nm: AFp.alloc(NT) for nm in
                ["m", "cln", "mcln", "ccomb", "mccomb", "rstd2",
                 "suma", "ssqa", "A3", "B3", "S4", "M4",
                 "ssqg", "sumr", "ssqr", "A4", "B4", "v0", "Sdot"]}
        rt = [AFp.alloc(16)[:, 0:1] for _ in range(10)]
        thr_up = AFp.alloc(16)[:, 0:1]; nthr_up = AFp.alloc(16)[:, 0:1]
        thr_dn = AFp.alloc(16)[:, 0:1]; nthr_dn = AFp.alloc(16)[:, 0:1]
        part_arr = AFp.alloc(64)
        rtq = AFp.alloc(16)[:, 0:1]; rtq2b = AFp.alloc(16)[:, 0:1]

        for cv in (1e-5, -0.5, 0.5, 1.000001, 1e-11, 1e-6, 1.0 / D, 1.0 / FF,
                   -1.0 / D, -1.000001 / D):
            cap = AFp.alloc(16)[:, 0:1]
            r.do("vector", lambda h, a=cap, v=cv: h.memset(a, v))
            nc.const_aps.register(F32, cv, cap)

        def L(dst, src, stream="swst"):
            eng = "sync" if stream.startswith("s") else "gpsimd"
            r.w(eng, stream, r.cnt[stream])
            return r.do(eng, lambda h, d=dst, s=src: h.dma_start(out=d, in_=s), inc=stream)

        L(ones_row_b, di["ones_row_b"]); L(ones_col_b, di["ones_col_b"])
        L(b2row_b, di["b2row"], "pw")
        L(ident_f, di["ident_f"]); L(ones_row_f, di["ones_row_f"])
        L(ones_col_f, di["ones_col_f"]); L(ones16_f, di["ones16_f"])
        L(ones16r_f, di["ones16r_f"])
        L(n1w, di["n1w"]); L(n2w, di["n2w"]); L(b1r, di["b1r"])
        r.do("vector", lambda h: h.memset(ident_b, 1.0))
        barr_row = AFp.alloc(16)[0:1, 0:1]
        r.tensor_barrier_fn = lambda h: h.matmul(
            pb[1][0:1, 456:457], ident_f[0:1, 0:1], ident_f[0:1, 0:1],
            start=True, stop=True)
        r.scalar_barrier_fn = lambda h: h.memzero(barr_row)

        scW, ABw, _AFw = open_scope("w1", DC * FF + 16, 16)
        w1_bf = ABw.alloc(DC * FF)

        sc12, AB, AFr = open_scope("p12", 38200, 15400)
        bigmask = AB.alloc(128)
        uvp = AB.alloc(2 * R * DC)
        convm_b = AB.alloc(128); cm3_b = AB.alloc(128)
        uvsum_nb = AB.alloc(2 * R)[0:1, :]
        pbrow_b = AB.alloc(D)[0:1, :]
        state_b = [AB.alloc(D)[0:R, :] for _ in range(nseq)]
        x0t_b = [AB.alloc(D) for _ in range(4)]
        HNR = 3 if nseq == 2 else 5
        hn_b = [[AB.alloc(D) for _ in range(HNR)] for _ in range(nseq)]
        mo_b = [AB.alloc(D) for _ in range(2)]
        sqscr_b = AB.alloc(D)
        h2_b = [AB.alloc(D) for _ in range(6)]
        hn2_b = [AB.alloc(D) for _ in range(2)]
        at_b = [AB.alloc(128) for _ in range(2)]
        qabf = [AB.alloc(128)[0:R, :] for _ in range(4)]
        knbf = [AB.alloc(128)[0:R, :] for _ in range(4)]
        qsbf = [AB.alloc(128)[0:R, :] for _ in range(4)]
        qeffbf = [AB.alloc(128)[0:R, :] for _ in range(4)]
        kvbf = [[AB.alloc(128)[0:R, :] for _ in range(4)] for _ in range(4)]
        kefft_b = [AB.alloc(16) for _ in range(4)]
        mrow_b = [AB.alloc(128)[0:1, :] for _ in range(2)]

        qpat = {nm: AFr.alloc(128)[0:R, :] for nm in
                ["qs", "kv0", "kv1", "kv2", "kv3", "qeff", "keff"]}
        alpha_row = AFr.alloc(16)[0:1, 0:R]
        g128 = AFr.alloc(16)[0:R, 0:1]
        gam = AFr.alloc(16)[0:R, 0:1]; lng = AFr.alloc(16)[0:R, 0:1]
        gl = AFr.alloc(16)[0:1, 0:1]; kr = AFr.alloc(16)[0:1, 0:CONVK]
        dl = AFr.alloc(16)[0:R, 0:1]; al = AFr.alloc(16)[0:R, 0:1]
        alp = AFr.alloc(16)[0:R, 0:1]
        gkrow = AFr.alloc(16)[0:1, 0:CONVK]
        gk_b = AFr.alloc(CONVK)
        state_f = [AFr.alloc(D)[0:R, :] for _ in range(nseq)]
        x0_t = [AFr.alloc(D) for _ in range(4)]
        wstg = [AFr.alloc(D) for _ in range(2)]
        emsk = {**{f"econv{j}": AFr.alloc(128) for j in range(CONVK)},
                **{f"ebnd{j}": AFr.alloc(128) for j in range(1, CONVK)}}
        cscr = AFr.alloc(128)
        qkf = [AFr.alloc(256)[0:R, :] for _ in range(4)]
        sqf = [AFr.alloc(256)[0:R, :] for _ in range(2)]
        keff_f = [AFr.alloc(128)[0:R, :] for _ in range(2)]
        inv_row = [AFr.alloc(256)[0:1, :] for _ in range(2)]
        wstg1 = [AFr.alloc(1024) for _ in range(2)]

        def q_ps(pi): return pb[pi][0:R, 0:128]
        def k_ps(pi): return pb[pi][0:R, 128:256]
        def qk_ps(pi): return pb[pi][0:R, 0:256]
        def mrow_ps(pi): return pb[pi][0:1, 256:384]
        def at_ps(pi): return pb[pi][:, 256:384]
        def kefft_ps(pi): return pb[pi][:, 384:400]
        def ssq_ps(pi): return pb[2 + pi][0:1, 0:256]
        def bc_ps(pi): return pb[2 + pi][0:R, 0:256]
        def dlt_ps(pi): return pb[2 + pi][0:R, 256:512]
        def o_ps(pi, c): return pb[4 + 2 * pi + c // 4][:, 128 * (c % 4):128 * (c % 4) + 128]
        def pj_ps(pi, eh): return pb[4 + 2 * pi + eh][:, :]
        arow_ps = pb[1][0:1, 400:416]; usum_ps = pb[1][0:1, 416:448]
        gkb_ps = pb[1][:, 448:452]

        L(bigmask, di["bigmask"])
        L(pbrow_b, di["pbrow"], "pw")
        for nm in qpat:
            L(qpat[nm], di["exp_" + nm])
        for nm in emsk:
            L(emsk[nm], di[nm])
        L(gl, di["gatelog"]); L(kr, di["kern"]); L(dl, di["declog"]); L(al, di["alplog"])
        uvstg = AFr.alloc(2 * R * DC)
        L(uvstg.rearrange("p (c r) -> p c r", c=DC),
          di["uv"].rearrange("(c p) r -> p c r", p=128))
        ld0 = r.cnt["swst"]; lp0 = r.cnt["pw"]

        r.w("scalar", "swst", ld0); r.w("scalar", "pw", lp0)
        r.do("scalar", preload_set(nc, [AF.Sigmoid]))
        r.do("scalar", lambda h: h.activation(out=gam, in_=dl, func=AF.Sigmoid))
        r.do("scalar", lambda h: h.activation(out=alp, in_=al, func=AF.Sigmoid))
        r.do("scalar", lambda h: h.activation(out=gl, in_=gl, func=AF.Sigmoid), inc="act")
        a_sig = r.cnt["act"]
        r.w("vector", "act", a_sig)
        r.do("vector", lambda h: h.tensor_scalar_max(out=gam, in0=gam, scalar1=GAMMA_MIN))
        r.do("vector", lambda h: h.tensor_scalar_min(out=gam, in0=gam, scalar1=GAMMA_MAX))
        r.do("vector", lambda h: h.tensor_scalar(out=gkrow, in0=kr, scalar1=gl,
                                                 scalar2=None, op0=ALU.mult), inc="dve")
        d_gk = r.cnt["dve"]
        r.do("scalar", preload_set(nc, [AF.Ln, AF.Exp, AF.Square]))
        r.w("scalar", "dve", d_gk)
        r.do("scalar", lambda h: h.activation(out=lng, in_=gam, func=AF.Ln), inc="act")
        a_ln = r.cnt["act"]
        r.w("vector", "act", a_ln)
        for nm in qpat:
            r.do("vector", lambda h, t=qpat[nm]: h.tensor_scalar_mul(out=t, in0=t, scalar1=lng))
        r.do("vector", lambda h: h.tensor_scalar_mul(out=g128, in0=lng, scalar1=128.0), inc="dve")
        d_pat = r.cnt["dve"]
        r.w("scalar", "dve", d_pat)
        for nm in qpat:
            r.do("scalar", lambda h, t=qpat[nm]: h.activation(out=t, in_=t, func=AF.Exp))
        r.do("scalar", lambda h: h.activation(out=g128, in_=g128, func=AF.Exp), inc="act")
        a_pat = r.cnt["act"]
        r.w("tensor", "act", a_sig)
        r.do("tensor", lambda h: h.transpose(arow_ps, alp, ident_f[0:R, 0:R]), inc="pe")
        p_ar = r.cnt["pe"]
        r.w("vector", "pe", p_ar)
        r.do("vector", lambda h: h.tensor_copy(out=alpha_row, in_=arow_ps), inc="dve")
        r.w("tensor", "dve", d_gk)
        r.do("tensor", lambda h: h.matmul(gkb_ps, ones_row_f, gkrow, start=True, stop=True), inc="pe")
        r.w("vector", "pe", r.cnt["pe"])
        r.do("vector", lambda h: h.tensor_copy(out=gk_b, in_=gkb_ps), inc="dve")
        d_gkb = r.cnt["dve"]
        r.w("vector", "swst", ld0)
        r.do("vector", lambda h: h.tensor_scalar_mul(out=cscr, in0=emsk["econv0"],
                                                     scalar1=gk_b[:, 0:1]))
        for j in range(1, CONVK):
            r.do("vector", lambda h, j=j: h.scalar_tensor_tensor(
                out=cscr, in0=emsk[f"econv{j}"], scalar=gk_b[:, j:j + 1], in1=cscr,
                op0=ALU.mult, op1=ALU.add))
        r.do("vector", lambda h: h.tensor_copy(out=convm_b, in_=cscr))
        r.do("vector", lambda h: h.tensor_scalar_mul(out=cscr, in0=emsk["ebnd1"],
                                                     scalar1=gk_b[:, 1:2]))
        for j in range(2, CONVK):
            r.do("vector", lambda h, j=j: h.scalar_tensor_tensor(
                out=cscr, in0=emsk[f"ebnd{j}"], scalar=gk_b[:, j:j + 1], in1=cscr,
                op0=ALU.mult, op1=ALU.add))
        r.do("vector", lambda h: h.tensor_copy(out=cm3_b, in_=cscr))
        for c in range(DC):
            r.do("vector", lambda h, c=c: h.tensor_scalar_mul(
                out=uvp[:, 32 * c:32 * c + 32], in0=uvstg[:, 32 * c:32 * c + 32],
                scalar1=n1w[:, c:c + 1]))
        r.do("vector", lambda h: h.engine_nop(), inc="dve")
        d_uvp = r.cnt["dve"]
        r.w("tensor", "dve", d_uvp)
        for c in range(DC):
            r.do("tensor", lambda h, c=c: h.matmul(
                usum_ps, ones_col_b, uvp[:, 32 * c:32 * c + 32],
                start=(c == 0), stop=(c == DC - 1)), inc="pe" if c == DC - 1 else None)
        p_us = r.cnt["pe"]
        r.w("vector", "pe", p_us)
        r.do("vector", lambda h: h.tensor_scalar_mul(out=uvsum_nb, in0=usum_ps, scalar1=-1.0 / D), inc="dve")
        projr_bf = AB.alloc(DC * D)
        dpj = {}
        for c in range(DC):
            r.w("sync", "dve", dpj.get(c - 2, 0))
            lw = L(wstg[c % 2], di["projr"][128 * c:128 * c + 128, :])
            r.w("vector", "swst", lw)
            dpj[c] = r.do("vector", lambda h, c=c: h.tensor_scalar_mul(
                out=projr_bf[:, D * c:D * (c + 1)], in0=wstg[c % 2],
                scalar1=n1w[:, c:c + 1]), inc="dve")
        for s in range(nseq):
            r.do("vector", lambda h, s=s: h.memset(state_f[s], 0.0))
            r.do("vector", lambda h, s=s: h.memset(state_b[s], 0.0))
            r.do("vector", lambda h, s=s: h.memset(hn_b[s][HNR - 1], 0.0))
        r.do("vector", lambda h: h.memset(pb[0][:, 256:384], 0.0))
        r.do("vector", lambda h: h.memset(pb[1][:, 256:384], 0.0))
        r.barrier()

        ms = [dict() for _ in range(NT)]
        order = [(s2, t2) for t2 in range(seqt) for s2 in range(nseq)]
        def M(gi, key):
            return ms[gi][key] if gi >= 0 else 0
        def EM(j, key):
            if j < 0:
                return 0
            s2, t2 = order[j]
            return ms[s2 * seqt + t2][key]
        tiles_w = [("up", di["wupT"], ai, bi) for ai in range(DC) for bi in range(4)] + \
                  [("dn", di["wdnT"], mi, 0) for mi in range(FFC)]
        alpha_sched = {k2: [] for k2 in range(NT)}
        for i2 in range(len(tiles_w)):
            alpha_sched[i2 * NT // len(tiles_w)].append(i2)
        lastred = {}
        d_w1prep = 0

        w1_sched = {k2: [] for k2 in range(NT)}
        for i2 in range(32):
            w1_sched[i2 * NT // 32].append(i2)
        lw_d = {}

        def emit_ld_xt(k2):
            if k2 >= NT:
                return
            s2, t2 = order[k2]
            gi2 = s2 * seqt + t2
            r.w("gpsimd", "pe", EM(k2 - 4, "p_qk"))
            ms[gi2]["ld_xt"] = L(
                x0t_b[k2 % 4].rearrange("p (c n) -> p c n", c=DC),
                di["x0t"][s2, :, 128 * t2:128 * t2 + 128].rearrange("(c p) n -> p c n", p=128),
                "pxt")
        emit_ld_xt(0); emit_ld_xt(1)

        def ctx(k):
            s, t = order[k]
            gi = s * seqt + t
            return s, t, gi, k % 2

        def S1(k):
            s, t, gi, par = ctx(k)
            x0c = x0_t[k % 4]
            hnc = hn_b[s][t % HNR]
            rt0 = rt[4 * par + 0]; rt1 = rt[4 * par + 1]; rt2 = rt[4 * par + 2]
            r.w("sync", "dve", EM(k - 4, "d_ain"))
            r.w("sync", "act", EM(k - 4, "a_sq"))
            ms[gi]["ld_x0"] = L(x0c, di["x0"][s, 128 * t:128 * t + 128, :], "sx0")
            emit_ld_xt(k + 2)
            r.w("scalar", "sx0", ms[gi]["ld_x0"])
            r.w("scalar", "dve", EM(k - 2, "d_v"))
            ms[gi]["a_sq"] = r.do("scalar", lambda h, o=sqscr_b, i=x0c, a=rt0:
                                  h.activation(out=o, in_=i, func=AF.Square, accum_out=a))
            r.w("vector", "sx0", ms[gi]["ld_x0"])
            r.do("vector", lambda h, i=x0c, o=rt1: h.reduce_sum(out=o, in_=i, axis=AX.X))
            r.w("vector", "act", ms[gi]["a_sq"])
            r.do("vector", lambda h, o=rows["m"][:, gi:gi + 1], i=rt1:
                 h.tensor_scalar_mul(out=o, in0=i, scalar1=1.0 / D))
            ms[gi]["d_v"] = r.do("vector", lambda h, o=rows["v0"][:, gi:gi + 1], i=rt1,
                                 m_=rows["m"][:, gi:gi + 1], ss=rt0: h.scalar_tensor_tensor(
                                 out=o, in0=i, scalar=m_, in1=ss, op0=ALU.mult, op1=ALU.subtract))
            if "dbg_rows" in dbg:
                r.do("vector", lambda h, o=rows["ssqa"][:, gi:gi + 1], i=rt0: h.tensor_copy(out=o, in_=i))
            r.w("scalar", "dve", ms[gi]["d_v"])
            r.do("scalar", lambda h, o=rt[4 * par + 3], i=rows["v0"][:, gi:gi + 1]:
                 h.activation(out=o, in_=i, func=AF.Ln, scale=-1.0 / D, bias=1e-5))
            r.do("scalar", lambda h, o=rows["cln"][:, gi:gi + 1], i=rt[4 * par + 3]:
                 h.activation(out=o, in_=i, func=AF.Exp, scale=-0.5))
            r.do("scalar", lambda h, o=rt[8 + par], i=rows["v0"][:, gi:gi + 1]:
                 h.activation(out=o, in_=i, func=AF.Ln, scale=-1.000001 / D, bias=1e-11), sw=False)
            a_rows = r.do("scalar", lambda h, o=rows["ccomb"][:, gi:gi + 1], i=rt[8 + par]:
                          h.activation(out=o, in_=i, func=AF.Exp, scale=-0.5))
            ms[gi]["a_rows"] = a_rows
            r.w("vector", "act", a_rows)
            if t >= HNR - 1:
                r.w("vector", "pe", M(gi - (HNR - 1), "p_o"))
            ms[gi]["d_hn"] = r.do("vector", lambda h, o=hnc, i=x0c,
                                  m_=rows["m"][:, gi:gi + 1], c=rows["ccomb"][:, gi:gi + 1]:
                                  h.tensor_scalar(out=o, in0=i, scalar1=m_, scalar2=c,
                                                  op0=ALU.subtract, op1=ALU.mult))
            r.w("vector", "sh2st", EM(k - 6, "st_h2"))
            ms[gi]["d_ain"] = r.do("vector", lambda h, o=h2_b[k % 6], i=x0c,
                                   m_=rows["m"][:, gi:gi + 1], c=rows["cln"][:, gi:gi + 1]:
                                   h.tensor_scalar(out=o, in0=i, scalar1=m_, scalar2=c,
                                                   op0=ALU.subtract, op1=ALU.mult), sw=False)
            if "dbg_hn" in dbg:
                r.w("sync", "dve", ms[gi]["d_hn"])
                L(dbg_hn[gi], hnc, "sh2st")

        def S2(k):
            s, t, gi, par = ctx(k)
            r.w("tensor", "pxt", ms[gi]["ld_xt"])
            r.w("tensor", "dve", EM(k - 2, "d_at"))
            r.w("tensor", "act", EM(k - 2, "a_mrow"))
            for c in range(DC):
                p_mrow = r.do("tensor", lambda h, c=c, x=x0t_b[k % 4], o=mrow_ps(par):
                              h.matmul(o[0:1, 0:128], ones_col_b, x[:, 128 * c:128 * c + 128],
                                       start=(c == 0), stop=(c == DC - 1)),
                              inc="pe" if c == DC - 1 else None)
            r.w("scalar", "pe", p_mrow)
            ms[gi]["a_mrow"] = r.do("scalar", lambda h, o=mrow_b[par], i=mrow_ps(par):
                                    h.copy(out=o, in_=i))
            r.w("tensor", "pxt", ms[gi]["ld_xt"])
            r.w("tensor", "act", ms[gi]["a_mrow"])
            r.w("tensor", "dve", EM(k - 2, "d_qkevac"))
            for c in range(DC):
                r.do("tensor", lambda h, c=c, x=x0t_b[k % 4], o=q_ps(par): h.matmul(
                    o, uvp[:, 32 * c:32 * c + 16], x[:, 128 * c:128 * c + 128],
                    start=(c == 0), stop=False))
            r.do("tensor", lambda h, mr=mrow_b[par], o=q_ps(par): h.matmul(
                o, uvsum_nb[0:1, 0:R], mr, start=False, stop=True))
            for c in range(DC):
                r.do("tensor", lambda h, c=c, x=x0t_b[k % 4], o=k_ps(par): h.matmul(
                    o, uvp[:, 32 * c + 16:32 * c + 32], x[:, 128 * c:128 * c + 128],
                    start=(c == 0), stop=False))
            ms[gi]["p_qk"] = r.do("tensor", lambda h, mr=mrow_b[par], o=k_ps(par): h.matmul(
                o, uvsum_nb[0:1, R:2 * R], mr, start=False, stop=True), inc="pe")
            r.w("vector", "pe", ms[gi]["p_qk"])
            r.w("vector", "pe", EM(k - 4, "pe_end"))
            r.do("vector", lambda h, o=qkf[k % 4], i=qk_ps(par): h.tensor_copy(out=o, in_=i))
            ms[gi]["d_qkevac"] = r.do("vector", lambda h, o=sqf[par], i=qkf[k % 4]:
                                      h.tensor_mul(out=o, in0=i, in1=i))
            for i2 in alpha_sched.get(k, []):
                nm2, src2, a2, b2 = tiles_w[i2]
                r.w("sync", "act", lastred.get(i2 - 2, 0))
                src_ap = src2[128 * a2:128 * a2 + 128, 1024 * b2:1024 * b2 + 1024] \
                    if nm2 == "up" else src2[128 * a2:128 * a2 + 128, :]
                lw2 = L(wstg[i2 % 2], src_ap, "swst")
                r.w("scalar", "swst", lw2)
                lastred[i2] = r.do("scalar", lambda h, i2=i2, w=wstg[i2 % 2]: h.activation(
                    out=w, in_=w, func=AF.Abs, accum_out=part_arr[:, i2:i2 + 1]))

        def S3(k):
            s, t, gi, par = ctx(k)
            r.w("tensor", "dve", ms[gi]["d_qkevac"])
            r.w("tensor", "dve", EM(k - 2, "d_var"))
            r.w("tensor", "dve", EM(k - 2, "d_st"))
            p_ssq = r.do("tensor", lambda h, i=sqf[par], o=ssq_ps(par):
                         h.matmul(o, ones16_f, i, start=True, stop=True), inc="pe")
            r.w("vector", "pe", p_ssq)
            ms[gi]["d_inv"] = r.do("vector", lambda h, o=inv_row[par], i=ssq_ps(par):
                                   h.tensor_scalar_max(out=o, in0=i, scalar1=1e-16))
            r.w("scalar", "dve", ms[gi]["d_inv"])
            r.do("scalar", lambda h, o=inv_row[par]: h.activation(out=o, in_=o, func=AF.Ln))
            a_inv = r.do("scalar", lambda h, o=inv_row[par]:
                         h.activation(out=o, in_=o, func=AF.Exp, scale=-0.5))
            r.w("tensor", "act", a_inv)
            r.w("tensor", "dve", ms[gi]["d_inv"])
            r.do("tensor", lambda h, i=inv_row[par], o=bc_ps(par): h.matmul(
                o[0:R, 0:128], alpha_row, i[0:1, 0:128], start=True, stop=True))
            p_bc = r.do("tensor", lambda h, i=inv_row[par], o=bc_ps(par): h.matmul(
                o[0:R, 128:256], ones16r_f, i[0:1, 128:256], start=True, stop=True), inc="pe")
            r.w("vector", "pe", p_bc)
            r.w("vector", "pe", EM(k - 4, "pe_end"))
            r.do("vector", lambda h, o=qabf[k % 4], i=qkf[k % 4], b=bc_ps(par):
                 h.tensor_mul(out=o, in0=i[:, 0:128], in1=b[0:R, 0:128]))
            r.do("vector", lambda h, o=knbf[k % 4], i=qkf[k % 4], b=bc_ps(par):
                 h.tensor_mul(out=o, in0=i[:, 128:256], in1=b[0:R, 128:256]))
            r.do("vector", lambda h, o=qsbf[k % 4], i=qabf[k % 4]: h.tensor_mul(out=o, in0=i, in1=qpat["qs"]))
            for dlt in range(4):
                r.do("vector", lambda h, o=kvbf[dlt][k % 4], i=knbf[k % 4], pat=qpat[f"kv{dlt}"]:
                     h.tensor_mul(out=o, in0=i, in1=pat), sw=(dlt == 0))
            r.do("vector", lambda h, o=qeffbf[k % 4], i=qabf[k % 4]:
                 h.tensor_mul(out=o, in0=i, in1=qpat["qeff"]), sw=False)
            ms[gi]["d_var"] = r.do("vector", lambda h, o=keff_f[par], i=knbf[k % 4]:
                                   h.tensor_mul(out=o, in0=i, in1=qpat["keff"]), sw=False)
            if "dbg_q" in dbg:
                r.w("sync", "dve", ms[gi]["d_var"])
                L(dbg_q[gi], qabf[k % 4][0:R, :], "sh2st")

        def S4(k):
            s, t, gi, par = ctx(k)
            r.w("tensor", "dve", ms[gi]["d_var"])
            r.w("tensor", "act", EM(k - 2, "a_kt"))
            p_kt = r.do("tensor", lambda h, i=keff_f[par], o=kefft_ps(par):
                        h.transpose(o, i, ident_f[0:R, 0:R]), inc="pe")
            r.w("scalar", "pe", p_kt)
            ms[gi]["a_kt"] = r.do("scalar", lambda h, o=kefft_b[k % 4], i=kefft_ps(par):
                                  h.copy(out=o, in_=i))
            r.w("tensor", "dve", EM(k - 2, "d_at"))
            for b in range(4):
                for a in range(b, 4):
                    dlt = a - b
                    last = (b == 3 and a == 3)
                    r.do("tensor", lambda h, b=b, a=a, kv=kvbf[dlt][k % 4], qs=qsbf[k % 4],
                         o=pb[par]: h.matmul(
                        o[32 * b:32 * b + 32, 256 + 32 * a:256 + 32 * a + 32],
                        kv[:, 32 * b:32 * b + 32],
                        qs[:, 32 * a:32 * a + 32],
                        start=True, stop=True, tile_position=(0, 32 * b)),
                        inc="pe" if last else None)
            ms[gi]["p_at"] = r.cnt["pe"]
            r.w("vector", "pe", ms[gi]["p_at"])
            r.w("vector", "pe", EM(k - 2, "p_o"))
            ms[gi]["d_at"] = r.do("vector", lambda h, o=at_b[par], i=at_ps(par):
                                  h.tensor_tensor(out=o, in0=i, in1=bigmask, op=ALU.mult))
            if "dbg_at" in dbg:
                r.w("sync", "dve", ms[gi]["d_at"])
                L(dbg_at[gi], at_b[par], "sh2st")
            for i3 in w1_sched.get(k, []):
                c3, j3 = i3 // 4, i3 % 4
                r.w("sync", "dve", lw_d.get(i3 - 2, 0))
                lw3 = L(wstg1[i3 % 2],
                        di["w1"][128 * c3:128 * c3 + 128, 1024 * j3:1024 * j3 + 1024], "swst")
                r.w("vector", "swst", lw3)
                lw_d[i3] = r.do("vector", lambda h, c3=c3, j3=j3, i=wstg1[i3 % 2]:
                                h.tensor_scalar_mul(
                                    out=w1_bf[:, FF * c3 + 1024 * j3:FF * c3 + 1024 * j3 + 1024],
                                    in0=i, scalar1=n2w[:, c3:c3 + 1]))

        def S5(k):
            s, t, gi, par = ctx(k)
            hnc = hn_b[s][t % HNR]
            hnp = hn_b[s][(t - 1) % HNR]
            r.w("tensor", "dve", ms[gi]["d_at"])
            r.w("tensor", "dve", ms[gi]["d_hn"])
            if t >= 1:
                r.w("tensor", "dve", M(gi - 1, "d_st"))
            r.w("tensor", "dve", EM(k - 2, "d_h2"))
            r.w("tensor", "dve", EM(k - 2, "d_moevac"))
            r.w("tensor", "act", EM(k - 2, "a_moevac"))
            for c in range(DC):
                last_c = (c == DC - 1)
                r.do("tensor", lambda h, c=c, hn=hnc, at=at_b[par], o=o_ps(par, c): h.matmul(
                    o, hn[:, 128 * c:128 * c + 128], at, start=True, stop=False))
                r.do("tensor", lambda h, c=c, st=state_b[s], qe=qeffbf[k % 4], o=o_ps(par, c): h.matmul(
                    o, st[:, 128 * c:128 * c + 128], qe, start=False, stop=False))
                ms[gi]["p_o"] = r.do("tensor", lambda h, c=c, hn=hnc, t0=(t == 0), o=o_ps(par, c): h.matmul(
                    o, hn[:, 128 * c:128 * c + 128], convm_b, start=False, stop=t0),
                    inc="pe" if (last_c and t == 0) else None)
                if t >= 1:
                    ms[gi]["p_o"] = r.do("tensor", lambda h, c=c, hp=hnp, o=o_ps(par, c): h.matmul(
                        o, hp[:, 128 * c:128 * c + 128], cm3_b, start=False, stop=True),
                        inc="pe" if last_c else None)
            d_q4 = EM(k - 2, "d_st")
            for qq4 in range(4):
                r.w("tensor", "act", ms[gi]["a_kt"])
                r.w("tensor", "dve", d_q4)
                p_dl = r.do("tensor", lambda h, qq4=qq4, kt=kefft_b[k % 4], hn=hnc, o=dlt_ps(par):
                            h.matmul(o, kt[:, 0:R], hn[:, 256 * qq4:256 * qq4 + 256],
                                     start=True, stop=True), inc="pe")
                r.w("vector", "pe", p_dl)
                d_q4 = r.do("vector", lambda h, qq4=qq4, sf=state_f[s], dl=dlt_ps(par): h.scalar_tensor_tensor(
                    out=sf[:, 256 * qq4:256 * qq4 + 256], in0=sf[:, 256 * qq4:256 * qq4 + 256],
                    scalar=g128, in1=dl, op0=ALU.mult, op1=ALU.add))
            ms[gi]["d_st"] = r.do("vector", lambda h, sf=state_f[s], sb=state_b[s]:
                                  h.tensor_copy(out=sb[:, :], in_=sf[:, :]))
            r.w("scalar", "pe", ms[gi]["p_o"])
            r.w("vector", "pe", ms[gi]["p_o"])
            r.w("scalar", "pe", EM(k - 2, "p_pj"))
            r.w("vector", "pe", EM(k - 2, "p_pj"))
            for c in range(4):
                r.do("scalar", lambda h, c=c, o=mo_b[par], i=o_ps(par, c): h.copy(
                    out=o[:, 128 * c:128 * c + 128], in_=i), sw=(c == 0))
            for c in range(4, DC):
                r.do("vector", lambda h, c=c, o=mo_b[par], i=o_ps(par, c): h.tensor_copy(
                    out=o[:, 128 * c:128 * c + 128], in_=i), sw=(c == 4))
            ms[gi]["a_moevac"] = r.cnt["act"]
            ms[gi]["d_moevac"] = r.cnt["dve"]
            if "dbg_mo" in dbg:
                r.w("sync", "act", ms[gi]["a_moevac"])
                r.w("sync", "dve", ms[gi]["d_moevac"])
                L(dbg_mo[gi], mo_b[par], "sh2st")

        def S6(k):
            s, t, gi, par = ctx(k)
            r.w("tensor", "act", ms[gi]["a_moevac"])
            r.w("tensor", "dve", ms[gi]["d_moevac"])
            r.w("tensor", "pw", lp0)
            for eh in range(2):
                for c in range(DC):
                    r.do("tensor", lambda h, eh=eh, c=c, o=mo_b[par], pj=pj_ps(par, eh): h.matmul(
                        pj, o[:, 128 * c:128 * c + 128],
                        projr_bf[:, D * c + 512 * eh:D * c + 512 * eh + 512],
                        start=(c == 0), stop=False))
                ms[gi]["p_pj"] = r.do("tensor", lambda h, eh=eh, pj=pj_ps(par, eh): h.matmul(
                    pj, ones_row_b, pbrow_b[0:1, 512 * eh:512 * eh + 512],
                    start=False, stop=True), inc="pe" if eh == 1 else None)
            r.w("vector", "pe", ms[gi]["p_pj"])
            r.w("vector", "act", EM(k - 2, "a_rstd2"))
            for eh in range(2):
                r.do("vector", lambda h, eh=eh, o=h2_b[k % 6], pj=pj_ps(par, eh):
                     h.scalar_tensor_tensor(
                    out=o[:, 512 * eh:512 * eh + 512], in0=o[:, 512 * eh:512 * eh + 512],
                    scalar=1.0, in1=pj, op0=ALU.mult, op1=ALU.add), sw=(eh == 0))
            ms[gi]["d_h2"] = r.cnt["dve"]
            r.w("scalar", "dve", ms[gi]["d_h2"])
            r.w("scalar", "sh2st", EM(k - 2, "st_h2"))
            r.do("scalar", lambda h, o=hn2_b[par], i=h2_b[k % 6], a=rt[6]:
                 h.activation(out=o, in_=i, func=AF.Square, accum_out=a))
            r.do("scalar", lambda h, i=rt[6], o=rt[7]:
                 h.activation(out=o, in_=i, func=AF.Ln, scale=1.0 / D, bias=1e-6))
            a_rstd2 = r.do("scalar", lambda h, i=rt[7], o=rows["rstd2"][:, gi:gi + 1]:
                           h.activation(out=o, in_=i, func=AF.Exp, scale=-0.5))
            ms[gi]["a_rstd2"] = a_rstd2
            r.w("vector", "act", a_rstd2)
            d_hn2 = r.do("vector", lambda h, o=hn2_b[par], i=h2_b[k % 6],
                         sc=rows["rstd2"][:, gi:gi + 1]:
                         h.tensor_scalar_mul(out=o, in0=i, scalar1=sc))
            r.w("sync", "dve", ms[gi]["d_h2"])
            L(h2_d[gi], h2_b[k % 6], "sh2st")
            r.w("sync", "dve", d_hn2)
            ms[gi]["st_h2"] = L(hn2_d[gi], hn2_b[par], "sh2st")
            ms[gi]["pe_end"] = ms[gi]["p_pj"]

        for kk in range(NT + 5):
            for ph, off in ((S6, 5), (S5, 4), (S4, 3), (S3, 2), (S2, 1), (S1, 0)):
                kp = kk - off
                if 0 <= kp < NT:
                    ph(kp)

        if "dbg_rows" in dbg:
            for k2, nm2 in enumerate(["m", "cln", "ccomb", "mcln", "mccomb", "rstd2", "suma", "ssqa"]):
                r.w("sync", "dve", r.cnt["dve"])
                r.w("sync", "act", r.cnt["act"])
                L(dbg_rows[:, k2 * NT:(k2 + 1) * NT], rows[nm2], "sh2st")
        r.barrier()
        sc12.close()
        if upi < 1:
            finish(r, nc, sems, out_d, NT)
            return nc, di

        sc3, AB, AFr = open_scope("p3", 60550, 4200)
        w2_bf = AB.alloc(FFC * D)
        hn2T = AB.alloc(DC * 512)
        hid_b = AB.alloc(FFC * 512)
        h2t_b = [AB.alloc(D) for _ in range(2)]
        aout_b = [AB.alloc(D) for _ in range(2)]
        wqstg = [AB.alloc(D) for _ in range(2)]
        wstg3 = [AFr.alloc(D) for _ in range(2)]
        qscr = AB.alloc(D)
        x0p3 = [AFr.alloc(D) for _ in range(2)]

        for qq in range(4):
            L(w2_bf[:, 8192 * qq:8192 * qq + 8192].rearrange("p (m e) -> p m e", m=8),
              di["w2"][1024 * qq:1024 * qq + 1024, :].rearrange("(m p) e -> p m e", p=128),
              "pw")
        r.do("vector", lambda h: h.reduce_sum(out=rtq, in_=part_arr[:, 0:32], axis=AX.X))
        d_pa = r.do("vector", lambda h: h.reduce_sum(out=rtq2b, in_=part_arr[:, 32:64], axis=AX.X))
        r.w("tensor", "dve", d_pa)
        thr_sc = pb[1][0:1, 460:462]
        bcq_ps = pb[1][:, 462:464]
        for i, pp in enumerate((rtq, rtq2b)):
            r.do("tensor", lambda h, pp=pp, i=i: h.matmul(
                thr_sc[0:1, i:i + 1], ones_col_f, pp, start=True, stop=True), inc="pe")
        p_thr = r.cnt["pe"]
        rtq2 = AFr.alloc(16)[0:1, 0:2]
        r.w("vector", "pe", p_thr)
        r.do("vector", lambda h: h.tensor_scalar_mul(out=rtq2, in0=thr_sc, scalar1=0.5 / (D * FF)), inc="dve")
        d_thr = r.cnt["dve"]
        r.w("tensor", "dve", d_thr)
        r.do("tensor", lambda h: h.matmul(bcq_ps, ones_row_f, rtq2, start=True, stop=True), inc="pe")
        p_thrb = r.cnt["pe"]
        r.w("vector", "pe", p_thrb)
        r.do("vector", lambda h: h.tensor_copy(out=thr_up, in_=bcq_ps[:, 0:1]))
        r.do("vector", lambda h: h.tensor_copy(out=thr_dn, in_=bcq_ps[:, 1:2]))
        r.do("vector", lambda h: h.tensor_scalar_mul(out=nthr_up, in0=thr_up, scalar1=-1.0))
        r.do("vector", lambda h: h.tensor_scalar_mul(out=nthr_dn, in0=thr_dn, scalar1=-1.0), inc="dve")

        d_alpha = r.cnt["dve"]
        lastq = {}

        def emit_quant(i):
            nm, srct, a, b = tiles_w[i]
            r.w("sync", "dve", lastq.get(i - 2, d_alpha))
            lw = L(wstg3[i % 2],
                   srct[128 * a:128 * a + 128, 1024 * b:1024 * b + 1024] if nm == "up"
                   else srct[128 * a:128 * a + 128, :])
            thr, nthr = (thr_up, nthr_up) if nm == "up" else (thr_dn, nthr_dn)
            dstd = wqup_d if nm == "up" else wqdn_d
            dst = dstd[128 * a:128 * a + 128, 1024 * b:1024 * b + 1024] if nm == "up" \
                else dstd[128 * a:128 * a + 128, :]
            r.w("vector", "swst", lw)
            r.w("vector", "pst", lastq.get(("st", i - 2), 0))
            r.do("vector", lambda h, i=i, nthr=nthr: h.tensor_scalar(
                out=qscr, in0=wstg3[i % 2], scalar1=nthr, scalar2=None, op0=ALU.is_lt))
            lastq[i] = r.do("vector", lambda h, i=i, thr=thr: h.scalar_tensor_tensor(
                out=wqstg[i % 2], in0=wstg3[i % 2], scalar=thr, in1=qscr,
                op0=ALU.is_gt, op1=ALU.subtract), inc="dve")
            r.w("gpsimd", "dve", lastq[i])
            lastq[("st", i)] = L(dst, wqstg[i % 2], "pst")
        r.barrier()

        r.do("scalar", preload_set(nc, [AF.Gelu_apprx_tanh, AF.Square]))
        st_a = {}
        d_a2 = {}
        d_dot = {}
        ld_dot = {}
        g_ms = [dict() for _ in range(NG)]
        def GM(g, k):
            return g_ms[g][k] if g >= 0 else 0
        for g in range(NG):
            r.w("sync", "pe", GM(g - 1, "p_mlp1"))
            for c in range(DC):
                src = hn2_d[4 * g:4 * g + 4, :, 128 * c:128 * c + 128].rearrange("g p d -> (g p) d")
                stl = ("shl", "sh2st")[c % 2]
                r.w("sync", stl, r.cnt[stl])
                g_ms[g][f"ld_t{c % 2}"] = r.do("sync", lambda h, c=c, s=src: h.dma_start_transpose(
                    hn2T[:, 512 * c:512 * c + 512], s), inc=stl)
            r.w("tensor", "shl", g_ms[g]["ld_t0"])
            r.w("tensor", "sh2st", g_ms[g]["ld_t1"])
            a_hid = {}
            for m in range(FFC):
                bank = pb[m % 2][:, :]
                if m >= 2:
                    r.w("tensor", "act", a_hid[m - 2])
                else:
                    r.w("tensor", "act", GM(g - 1, "a_hid"))
                pm = None
                for c in range(DC):
                    pm = r.do("tensor", lambda h, m=m, c=c, bank=bank: h.matmul(
                        bank, w1_bf[:, FF * c + 128 * m:FF * c + 128 * m + 128],
                        hn2T[:, 512 * c:512 * c + 512],
                        start=(c == 0), stop=(c == DC - 1)),
                        inc="pe" if c == DC - 1 else None)
                r.w("scalar", "pe", pm)
                a_hid[m] = r.do("scalar", lambda h, m=m, bank=bank: h.activation(
                    out=hid_b[:, 512 * m:512 * m + 512], in_=bank,
                    func=AF.Gelu_apprx_tanh, bias=b1r[:, m:m + 1]), inc="act")
            g_ms[g]["p_mlp1"] = r.cnt["pe"]
            g_ms[g]["a_hid"] = a_hid[FFC - 1]
            d_aev = 0
            for ps4 in range(4):
                ti = 4 * g + ps4
                r.w("sync", "dve", d_a2.get(2 * (ti - 2) + 1, 0))
                ld_h2 = L(h2t_b[ti % 2], h2_d[ti], "shl")
                for eh in range(2):
                    bank = pb[2 + (2 * ps4 + eh) % 2][:, :]
                    r.w("tensor", "act", a_hid[FFC - 1])
                    r.w("tensor", "dve", d_a2.get(2 * ti + eh - 2, 0))
                    for m in range(FFC):
                        r.do("tensor", lambda h, m=m, ps4=ps4, eh=eh, bank=bank: h.matmul(
                            bank, hid_b[:, 512 * m + 128 * ps4:512 * m + 128 * ps4 + 128],
                            w2_bf[:, D * m + 512 * eh:D * m + 512 * eh + 512],
                            start=(m == 0), stop=False))
                    pm = r.do("tensor", lambda h, eh=eh, bank=bank: h.matmul(
                        bank, ones_row_b, b2row_b[0:1, 512 * eh:512 * eh + 512],
                        start=False, stop=True), inc="pe")
                    r.w("vector", "pe", pm)
                    r.w("vector", "shl", ld_h2)
                    r.w("vector", "pst", st_a.get(ti - 2, 0))
                    r.do("vector", lambda h, ps4=ps4, eh=eh, bank=bank, h2t=h2t_b[ti % 2]:
                         h.scalar_tensor_tensor(
                        out=aout_b[ps4 % 2][:, 512 * eh:512 * eh + 512], in0=bank, scalar=1.0,
                        in1=h2t[:, 512 * eh:512 * eh + 512],
                        op0=ALU.mult, op1=ALU.add, accum_out=rt[eh]))
                    d_a2[2 * ti + eh] = r.do(
                        "vector", lambda h: h.engine_nop(), inc="dve")
                r.do("vector", lambda h, o=rows["suma"][:, ti:ti + 1]:
                     h.tensor_add(out=o, in0=rt[0], in1=rt[1]), inc="dve")
                d_sum = r.cnt["dve"]
                r.w("scalar", "dve", d_sum)
                r.do("scalar", lambda h, ps4=ps4, a=rt[2]: h.activation(
                    out=qscr[:, 0:512], in_=aout_b[ps4 % 2][:, 0:512], func=AF.Square, accum_out=a))
                r.do("scalar", lambda h, ps4=ps4, a=rt[3]: h.activation(
                    out=qscr[:, 512:1024], in_=aout_b[ps4 % 2][:, 512:1024], func=AF.Square, accum_out=a), inc="act")
                a_ss = r.cnt["act"]
                r.w("vector", "act", a_ss)
                r.do("vector", lambda h, o=rows["ssqa"][:, ti:ti + 1]:
                     h.tensor_add(out=o, in0=rt[2], in1=rt[3]), inc="dve")
                if ti == 0:
                    ld_dot[0] = L(x0p3[0], di["x0"][0, 0:128, :], "sx0")
                if ti + 1 < NT:
                    r.w("sync", "dve", d_dot.get(ti - 1, 0))
                    ld_dot[ti + 1] = L(x0p3[(ti + 1) % 2],
                                       di["x0"][(ti + 1) // seqt,
                                                128 * ((ti + 1) % seqt):128 * ((ti + 1) % seqt) + 128, :],
                                       "sx0")
                r.w("vector", "sx0", ld_dot[ti])
                d_dot[ti] = r.do("vector", lambda h, ti=ti, ps4=ps4: h.scalar_tensor_tensor(
                    out=qscr, in0=x0p3[ti % 2], scalar=1.0, in1=aout_b[ps4 % 2],
                    op0=ALU.mult, op1=ALU.mult, accum_out=rows["Sdot"][:, ti:ti + 1]))
                r.w("gpsimd", "act", a_ss)
                r.w("gpsimd", "dve", d_a2[2 * ti + 1])
                st_a[ti] = L(aout_d[ti], aout_b[ps4 % 2], "pst")
                for i2 in range(ti * len(tiles_w) // NT, (ti + 1) * len(tiles_w) // NT):
                    emit_quant(i2)
            g_ms[g]["d_aev"] = r.cnt["dve"]
        r.barrier()
        sc3.close()
        scW.close()
        if upi < 2:
            finish(r, nc, sems, out_d, NT)
            return nc, di

        sc4, AB, AFr = open_scope("p4", 94230, 2760)
        wqu_b = AB.alloc(DC * FF)
        wqd_b = AB.alloc(FFC * D)
        finT = AB.alloc(DC * 512)
        g_b = AB.alloc(FFC * 512)
        raw_b = [AB.alloc(D) for _ in range(4)]
        ao4_b = [AB.alloc(D) for _ in range(2)]
        fin4_b = ao4_b
        gsq_b = [AB.alloc(512) for _ in range(4)]
        ssq_row = AFr.alloc(512)[0:1, :]
        x0p4 = [AFr.alloc(D) for _ in range(2)]
        x1l_b = x0p4
        lad_q = AFr.alloc(2 * NT); lad_y = AFr.alloc(2 * NT); lad_t = AFr.alloc(2 * NT)

        def emit_rsqrt(qap, yap, tap, q0, levels, iters=8):
            base = (1.0 / math.sqrt(q0)) * (2.0 ** (-0.5 - levels))
            r.do("vector", lambda h, y=yap, b=base: h.memset(y, b))
            for k in range(1, levels + 1):
                thr = q0 * (4.0 ** k)
                r.do("vector", lambda h, q=qap, t=tap, thr=thr: h.tensor_scalar(
                    out=t, in0=q, scalar1=thr, scalar2=1.0, op0=ALU.is_le, op1=ALU.add))
                r.do("vector", lambda h, y=yap, t=tap: h.tensor_mul(out=y, in0=y, in1=t))
            for _ in range(iters):
                r.do("vector", lambda h, y=yap, t=tap: h.tensor_mul(out=t, in0=y, in1=y))
                r.do("vector", lambda h, q=qap, t=tap: h.scalar_tensor_tensor(
                    out=t, in0=q, scalar=-0.5, in1=t, op0=ALU.mult, op1=ALU.mult))
                r.do("vector", lambda h, t=tap: h.tensor_scalar(
                    out=t, in0=t, scalar1=1.5, scalar2=None, op0=ALU.add))
                r.do("vector", lambda h, y=yap, t=tap: h.tensor_mul(out=y, in0=y, in1=t))
            return r.cnt["dve"]

        r.do("scalar", preload_set(nc, [AF.Gelu, AF.Square]))
        for qq in range(4):
            L(wqu_b[:, 8192 * qq:8192 * qq + 8192].rearrange("p (c f) -> p c f", c=2),
              wqup_d[256 * qq:256 * qq + 256, :].rearrange("(c p) f -> p c f", p=128), "pw")
        for qq in range(4):
            L(wqd_b[:, 8192 * qq:8192 * qq + 8192].rearrange("p (m e) -> p m e", m=8),
              wqdn_d[1024 * qq:1024 * qq + 1024, :].rearrange("(m p) e -> p m e", p=128), "pw")
        lq = r.cnt["pw"]

        r.do("vector", lambda h: h.tensor_scalar_mul(out=rows["suma"], in0=rows["suma"], scalar1=1.0 / D))
        r.do("vector", lambda h: h.tensor_scalar_mul(out=rows["ssqa"], in0=rows["ssqa"], scalar1=1.0 / D))
        r.do("vector", lambda h: h.tensor_mul(out=rows["B3"], in0=rows["suma"], in1=rows["suma"]))
        r.do("vector", lambda h: h.tensor_sub(out=rows["ssqa"], in0=rows["ssqa"], in1=rows["B3"]))
        r.do("vector", lambda h: h.tensor_scalar(out=rows["mcln"], in0=rows["ssqa"], scalar1=1e-5,
                                                 scalar2=None, op0=ALU.add))
        emit_rsqrt(rows["mcln"], rows["A3"], rows["mccomb"], 1e-5, 17)
        r.do("vector", lambda h: h.tensor_mul(out=rows["B3"], in0=rows["suma"], in1=rows["A3"]))
        r.do("vector", lambda h: h.tensor_scalar_mul(out=rows["B3"], in0=rows["B3"], scalar1=-1.0))
        r.do("vector", lambda h: h.tensor_scalar_mul(out=rows["Sdot"], in0=rows["Sdot"], scalar1=1.0 / D))
        r.do("vector", lambda h: h.tensor_mul(out=rows["M4"], in0=rows["suma"], in1=rows["m"]))
        r.do("vector", lambda h: h.tensor_sub(out=rows["Sdot"], in0=rows["Sdot"], in1=rows["M4"]))
        r.do("vector", lambda h: h.tensor_mul(out=rows["Sdot"], in0=rows["Sdot"], in1=rows["A3"]))
        r.do("vector", lambda h: h.reciprocal(out=rows["mccomb"], in_=rows["mcln"]))
        r.do("vector", lambda h: h.tensor_mul(out=rows["mccomb"], in0=rows["ssqa"], in1=rows["mccomb"]))
        r.do("vector", lambda h: h.scalar_tensor_tensor(out=rows["mcln"], in0=rows["Sdot"], scalar=2.0,
                                                        in1=rows["mccomb"], op0=ALU.mult, op1=ALU.add))
        r.do("vector", lambda h: h.scalar_tensor_tensor(out=rows["mcln"], in0=rows["v0"],
                                                        scalar=-1.0 / D, in1=rows["mcln"],
                                                        op0=ALU.mult, op1=ALU.add))
        r.do("vector", lambda h: h.tensor_scalar(out=lad_q[:, 0:NT], in0=rows["mcln"], scalar1=1e-5,
                                                 scalar2=None, op0=ALU.add))
        r.do("vector", lambda h: h.tensor_scalar(out=lad_q[:, NT:2 * NT], in0=rows["mcln"], scalar1=1e-12,
                                                 scalar2=None, op0=ALU.add))
        emit_rsqrt(lad_q, lad_y, lad_t, 1e-12, 27)
        r.do("vector", lambda h: h.tensor_mul(out=lad_t[:, 0:NT], in0=rows["mcln"], in1=lad_y[:, NT:2 * NT]))
        r.do("vector", lambda h: h.tensor_mul(out=lad_t[:, 0:NT], in0=lad_t[:, 0:NT], in1=lad_y[:, 0:NT]))
        r.do("vector", lambda h: h.tensor_scalar(out=lad_t[:, 0:NT], in0=lad_t[:, 0:NT], scalar1=1e-8,
                                                 scalar2=None, op0=ALU.add))
        r.do("vector", lambda h: h.reciprocal(out=lad_t[:, 0:NT], in_=lad_t[:, 0:NT]))
        r.do("vector", lambda h: h.tensor_mul(out=rows["S4"], in0=lad_y[:, 0:NT], in1=lad_t[:, 0:NT]))
        r.do("vector", lambda h: h.tensor_sub(out=rows["M4"], in0=rows["B3"], in1=rows["m"]))
        r.do("vector", lambda h: h.tensor_mul(out=rows["M4"], in0=rows["M4"], in1=rows["S4"]))

        fold_ms = {}
        def FM(key, f):
            return fold_ms[(key, f)] if f >= 0 else 0
        def emit_fold(f):
            s_, t_ = f // seqt, f % seqt
            r.w("sync", "dve", FM("d_x1", f - 2))
            r.w("sync", "swst", FM("st_x1", f - 2))
            r.w("sync", "sh2st", out_ms.get(("st_o", f - (f % 4) - 6), 0))
            r.w("sync", "pst", out_ms.get(("st_o", f - (f % 4) - 5), 0))
            lx = L(x0p4[f % 2], di["x0"][s_, 128 * t_:128 * t_ + 128, :], "sx0")
            r.w("gpsimd", "dve", FM("d_t", f - 2))
            r.w("gpsimd", "pxt", FM("st_f", f - 2))
            la = L(ao4_b[f % 2], aout_d[f], "pld")
            r.w("vector", "sx0", lx)
            r.w("vector", "pld", la)
            fold_ms[("d_t", f)] = r.do("vector", lambda h, f=f: h.scalar_tensor_tensor(
                out=x0p4[f % 2], in0=ao4_b[f % 2], scalar=rows["A3"][:, f:f + 1],
                in1=x0p4[f % 2], op0=ALU.mult, op1=ALU.add))
            r.w("vector", "pxt", FM("st_f", f - 2))
            d_f = r.do("vector", lambda h, f=f: h.tensor_scalar(
                out=fin4_b[f % 2], in0=x0p4[f % 2], scalar1=rows["S4"][:, f:f + 1],
                scalar2=rows["M4"][:, f:f + 1], op0=ALU.mult, op1=ALU.add))
            r.w("vector", "swst", FM("st_x1", f - 2))
            d_x1 = r.do("vector", lambda h, f=f: h.tensor_scalar(
                out=x0p4[f % 2], in0=x0p4[f % 2], scalar1=rows["B3"][:, f:f + 1],
                scalar2=None, op0=ALU.add))
            fold_ms[("d_x1", f)] = d_x1
            r.w("sync", "dve", d_x1)
            fold_ms[("st_x1", f)] = L(x1_d[f], x0p4[f % 2], "swst")
            r.w("gpsimd", "dve", d_f)
            fold_ms[("st_f", f)] = L(fin_d[f], fin4_b[f % 2], "pxt")

        out_ms = {}
        def OM(key, f):
            return out_ms[(key, f)] if f >= 0 else 0
        def emit_out(tj):
            lastf = min(tj - (tj % 4) + 7, NT - 1)
            r.w("gpsimd", "swst", fold_ms[("st_x1", lastf)])
            r.w("gpsimd", "dve", FM("d_x1", lastf))
            r.w("gpsimd", ("sh2st", "pst")[tj % 2], OM("st_o", tj - 2))
            lx1 = L(x1l_b[tj % 2], x1_d[tj], "pw")
            r.w("vector", "pw", lx1)
            r.do("vector", lambda h, tj=tj: h.scalar_tensor_tensor(
                out=x1l_b[tj % 2], in0=raw_b[tj % 4], scalar=rows["A4"][:, tj:tj + 1],
                in1=x1l_b[tj % 2], op0=ALU.mult, op1=ALU.add))
            ost = ("sh2st", "pst")[tj % 2]
            r.w("vector", ost, OM("st_o", tj - 2))
            d_o = r.do("vector", lambda h, tj=tj: h.tensor_scalar(
                out=x1l_b[tj % 2], in0=x1l_b[tj % 2], scalar1=rows["B4"][:, tj:tj + 1],
                scalar2=None, op0=ALU.add))
            out_ms[("d_o", tj)] = d_o
            r.w("sync" if ost[0] == "s" else "gpsimd", "dve", d_o)
            out_ms[("st_o", tj)] = L(out_d[tj], x1l_b[tj % 2], ost)

        def emit_a4rows(g):
            c0 = 4 * g
            cs = slice(c0, c0 + 4)
            r.w("vector", "act", g4_ms[g]["d_ssqg"])
            r.do("vector", lambda h, cs=cs: h.tensor_scalar_mul(
                out=rows["sumr"][:, cs], in0=rows["sumr"][:, cs], scalar1=1.0 / D))
            r.do("vector", lambda h, cs=cs: h.tensor_scalar_mul(
                out=rows["ssqr"][:, cs], in0=rows["ssqr"][:, cs], scalar1=1.0 / D))
            r.do("vector", lambda h, cs=cs: h.tensor_mul(
                out=lad_t[:, 0:4], in0=rows["sumr"][:, cs], in1=rows["sumr"][:, cs]))
            r.do("vector", lambda h, cs=cs: h.tensor_sub(
                out=rows["ssqr"][:, cs], in0=rows["ssqr"][:, cs], in1=lad_t[:, 0:4]))
            r.do("vector", lambda h, cs=cs: h.tensor_scalar(
                out=lad_q[:, 0:4], in0=rows["ssqg"][:, cs], scalar1=1.0 / FF, scalar2=1e-16,
                op0=ALU.mult, op1=ALU.add))
            emit_rsqrt(lad_q[:, 0:4], lad_y[:, 0:4], lad_t[:, 0:4], 1e-16, 36)
            r.do("vector", lambda h, cs=cs: h.tensor_scalar_mul(
                out=lad_t[:, 0:4], in0=rows["ssqg"][:, cs], scalar1=1.0 / FF))
            r.do("vector", lambda h: h.tensor_mul(out=lad_y[:, 0:4], in0=lad_y[:, 0:4], in1=lad_t[:, 0:4]))
            r.do("vector", lambda h: h.tensor_scalar(out=lad_y[:, 0:4], in0=lad_y[:, 0:4],
                                                     scalar1=1e-8, scalar2=None, op0=ALU.add))
            r.do("vector", lambda h: h.reciprocal(out=lad_y[:, 0:4], in_=lad_y[:, 0:4]))
            r.do("vector", lambda h: h.tensor_mul(out=lad_t[:, 0:4], in0=lad_y[:, 0:4], in1=lad_y[:, 0:4]))
            r.do("vector", lambda h, cs=cs: h.tensor_mul(
                out=lad_q[:, 4:8], in0=lad_t[:, 0:4], in1=rows["ssqr"][:, cs]))
            r.do("vector", lambda h: h.tensor_scalar(out=lad_q[:, 4:8], in0=lad_q[:, 4:8],
                                                     scalar1=1e-5, scalar2=None, op0=ALU.add))
            emit_rsqrt(lad_q[:, 4:8], lad_y[:, 4:8], lad_t[:, 4:8], 1e-5, 19)
            r.do("vector", lambda h, cs=cs: h.tensor_mul(
                out=rows["A4"][:, cs], in0=lad_y[:, 0:4], in1=lad_y[:, 4:8]))
            r.do("vector", lambda h, cs=cs: h.scalar_tensor_tensor(
                out=rows["B4"][:, cs], in0=rows["sumr"][:, cs], scalar=-1.0,
                in1=rows["A4"][:, cs], op0=ALU.mult, op1=ALU.mult))

        for f in range(min(4, NT)):
            emit_fold(f)

        ssq4_ps = pb[4][0:1, 0:512]
        tr4_ps = pb[5][:, 0:4]
        g4_ms = [dict() for _ in range(NG)]
        d_rev, a_ssr = {}, {}
        def G4(g, k):
            return g4_ms[g][k] if g >= 0 else 0
        for g in range(NG):
            r.w("sync", "pe", G4(g - 1, "p_up"))
            r.w("sync", "pxt", fold_ms[("st_f", 4 * g + 3)])
            for c in range(DC):
                src = fin_d[4 * g:4 * g + 4, :, 128 * c:128 * c + 128].rearrange("g p d -> (g p) d")
                stl = ("shl", "sx0")[c % 2]
                r.w("sync", stl, r.cnt[stl])
                g4_ms[g][f"ld_t{c % 2}"] = r.do("sync", lambda h, c=c, s=src: h.dma_start_transpose(
                    finT[:, 512 * c:512 * c + 512], s), inc=stl)
            if g >= 1:
                for j in range(4):
                    emit_out(4 * (g - 1) + j)
            r.w("tensor", "shl", g4_ms[g]["ld_t0"])
            r.w("tensor", "sx0", g4_ms[g]["ld_t1"])
            if g == 0:
                r.w("tensor", "pw", lq)
            a_g = {}
            d_gsq = {}
            for m in range(FFC):
                bank = pb[m % 2][:, :]
                if m >= 2:
                    r.w("tensor", "act", a_g[m - 2])
                else:
                    r.w("tensor", "act", G4(g - 1, "a_glast"))
                    r.w("tensor", "dve", G4(g - 1, "d_srow"))
                pm = None
                for c in range(DC):
                    pm = r.do("tensor", lambda h, m=m, c=c, bank=bank: h.matmul(
                        bank, wqu_b[:, FF * c + 128 * m:FF * c + 128 * m + 128],
                        finT[:, 512 * c:512 * c + 512],
                        start=(c == 0), stop=(c == DC - 1)),
                        inc="pe" if c == DC - 1 else None)
                r.w("scalar", "pe", pm)
                a_g[m] = r.do("scalar", lambda h, m=m, bank=bank: h.activation(
                    out=g_b[:, 512 * m:512 * m + 512], in_=bank, func=AF.Gelu), inc="act")
                if m >= 1:
                    if m >= 5:
                        r.w("scalar", "pe", g4_ms[g].get(f"p_ssq_{m-5}", 0))
                    else:
                        r.w("scalar", "pe", G4(g - 1, f"p_ssq_{m+27}"))
                    d_gsq[m - 1] = r.do("scalar", lambda h, m1=m - 1: h.activation(
                        out=gsq_b[m1 % 4], in_=g_b[:, 512 * m1:512 * m1 + 512],
                        func=AF.Square), inc="act")
                if m % 8 == 4 and 4 * g + 4 + m // 8 < NT:
                    emit_fold(4 * g + 4 + m // 8)
                if m >= 2:
                    r.w("tensor", "act", d_gsq[m - 2])
                    g4_ms[g][f"p_ssq_{m-2}"] = r.do("tensor", lambda h, m1=m - 2: h.matmul(
                        ssq4_ps, ones_col_b, gsq_b[m1 % 4],
                        start=(m1 == 0), stop=False), inc="pe")
            r.w("scalar", "pe", g4_ms[g][f"p_ssq_{FFC-5}"])
            d_gsq[FFC - 1] = r.do("scalar", lambda h: h.activation(
                out=gsq_b[(FFC - 1) % 4], in_=g_b[:, 512 * (FFC - 1):512 * FFC],
                func=AF.Square), inc="act")
            for m1 in (FFC - 2, FFC - 1):
                r.w("tensor", "act", d_gsq[m1])
                g4_ms[g][f"p_ssq_{m1}"] = r.do("tensor", lambda h, m1=m1: h.matmul(
                    ssq4_ps, ones_col_b, gsq_b[m1 % 4],
                    start=False, stop=(m1 == FFC - 1)), inc="pe")
            g4_ms[g]["p_up"] = r.cnt["pe"]
            g4_ms[g]["a_glast"] = a_g[FFC - 1]
            r.w("scalar", "pe", g4_ms[g]["p_up"])
            g4_ms[g]["d_srow"] = r.do(
                "scalar", lambda h: h.copy(out=ssq_row, in_=ssq4_ps), inc="act")
            r.w("tensor", "act", g4_ms[g]["d_srow"])
            r.w("tensor", "act", G4(g - 1, "d_ssqg"))
            for j in range(4):
                r.do("tensor", lambda h, j=j: h.transpose(
                    tr4_ps[:, j:j + 1], ssq_row[0:1, 128 * j:128 * j + 128], ident_f[0:1, 0:1]),
                    inc="pe" if j == 3 else None)
            p_tr = r.cnt["pe"]
            r.w("scalar", "pe", p_tr)
            for j in range(4):
                r.do("scalar", lambda h, j=j, ti=4 * g + j: h.copy(
                    out=rows["ssqg"][:, ti:ti + 1], in_=tr4_ps[:, j:j + 1]),
                    inc="act" if j == 3 else None)
            g4_ms[g]["d_ssqg"] = r.cnt["act"]
            for ps4 in range(4):
                ti = 4 * g + ps4
                for eh in range(2):
                    bank = pb[2 + (2 * ps4 + eh) % 2][:, :]
                    r.w("tensor", "dve", d_rev.get(2 * ti + eh - 2, 0))
                    pm = None
                    for m in range(FFC):
                        pm = r.do("tensor", lambda h, m=m, ps4=ps4, eh=eh, bank=bank: h.matmul(
                            bank, g_b[:, 512 * m + 128 * ps4:512 * m + 128 * ps4 + 128],
                            wqd_b[:, D * m + 512 * eh:D * m + 512 * eh + 512],
                            start=(m == 0), stop=(m == FFC - 1)),
                            inc="pe" if m == FFC - 1 else None)
                    r.w("vector", "pe", pm)
                    r.w("vector", "act", a_ssr.get(ti - 2, 0))
                    r.do("vector", lambda h, ti=ti, eh=eh, bank=bank: h.tensor_scalar(
                        out=raw_b[ti % 4][:, 512 * eh:512 * eh + 512], in0=bank,
                        scalar1=1.0, scalar2=None, op0=ALU.mult, op1=ALU.add, accum_out=rt[eh]))
                    d_rev[2 * ti + eh] = r.do("vector", lambda h: h.engine_nop(), inc="dve")
                r.do("vector", lambda h, ti=ti: h.tensor_add(
                    out=rows["sumr"][:, ti:ti + 1], in0=rt[0], in1=rt[1]), inc="dve")
                d_sr2 = r.cnt["dve"]
                r.w("scalar", "dve", d_sr2)
                r.do("scalar", lambda h, ti=ti, a=rt[2]: h.activation(
                    out=gsq_b[0], in_=raw_b[ti % 4][:, 0:512], func=AF.Square, accum_out=a))
                r.do("scalar", lambda h, ti=ti, a=rt[3]: h.activation(
                    out=gsq_b[1], in_=raw_b[ti % 4][:, 512:1024], func=AF.Square, accum_out=a), inc="act")
                a_ssr[ti] = r.cnt["act"]
                r.w("vector", "act", a_ssr[ti])
                r.do("vector", lambda h, ti=ti: h.tensor_add(
                    out=rows["ssqr"][:, ti:ti + 1], in0=rt[2], in1=rt[3]), inc="dve")
                if "raw" in dbg:
                    r.w("gpsimd", "dve", d_rev[2 * ti + 1])
                    L(raw_d[ti], raw_b[ti % 4], "pst")
            emit_a4rows(g)
        for j in range(4):
            emit_out(4 * (NG - 1) + j)
        r.barrier()
        sc4.close()

        finish(r, nc, sems, out_d, NT)
        return nc, di


def finish(r, nc, sems, out_d, NT):
    snap = dict(r.cnt)
    for e in ENGINES:
        for s in DMA_SEMS:
            r.w(e, s, snap[s])
    r.replay(nc, sems)


_CACHE = {}


def kernel(**inputs):
    nseq, S = 2, 2048
    key = (nseq, S)
    if key not in _CACHE:
        _CACHE[key] = build(nseq, S)
    nc, _ = _CACHE[key]
    in_maps = [host_prep(c, inputs, nseq, S) for c in range(8)]
    res = run_bass_kernel_spmd(nc, in_maps, core_ids=list(range(8)))
    seqt = S // 128
    outs = []
    for c in range(8):
        o = np.asarray(res.results[c]["out"])
        outs.append(o.reshape(nseq, S, D))
    return np.concatenate(outs, 0).astype(np.float32)


# version 35
# speedup vs baseline: 1.0681x; 1.0144x over previous
import contextlib
import math
import numpy as np
import ml_dtypes

import concourse.bass as bass
import concourse.mybir as mybir
from concourse.bass_utils import run_bass_kernel_spmd

F32 = mybir.dt.float32
BF16 = mybir.dt.bfloat16
AF = mybir.ActivationFunctionType
ALU = mybir.AluOpType
AX = mybir.AxisListType
BFNP = ml_dtypes.bfloat16

D = 1024
DC = 8
R = 16
FF = 4096
FFC = 32
CONVK = 4
GAMMA_MIN, GAMMA_MAX = 0.15, 1.0

ENGINES = ("tensor", "vector", "scalar", "sync", "gpsimd")


def preload_set(nc, needed):
    from concourse.hw_specs import get_activation_tables
    tabs = list(get_activation_tables(nc.m.arch).values())
    for idx, funcs in enumerate(tabs):
        if all(f in funcs for f in needed):
            def fn(h, idx=idx):
                return h.add_instruction(mybir.InstLoadActFuncSet(
                    name=nc.get_next_instruction_name(),
                    act_func_set_id=idx, ins=[], outs=[]))
            return fn
    raise ValueError(f"no table set contains {needed}")
DMA_SEMS = ("sx0", "swst", "sh2st", "shl", "pxt", "pw", "pst", "pld")
SEMS = ("pe", "dve", "act", "pool") + DMA_SEMS
ESEM = {"tensor": "pe", "vector": "dve", "scalar": "act", "gpsimd": "pool"}


class Rec:

    def __init__(self):
        self.ops = {e: [] for e in ENGINES}
        self.cnt = {s: 0 for s in SEMS}

    def w(self, eng, sem, val):
        if val and val > 0:
            self.ops[eng].append(("wait", sem, val))

    def do(self, eng, fn, inc=None, sw=True):
        if eng in ("vector", "scalar") or (eng == "gpsimd" and inc not in DMA_SEMS):
            own = ESEM[eng]
            if sw:
                self.w(eng, own, self.cnt[own])
            self.cnt[own] += 1
            self.ops[eng].append(("op", fn, own))
            if inc is not None and inc != own:
                raise ValueError(f"{eng} ops auto-inc {own}; got inc={inc}")
            return self.cnt[own]
        if inc is not None:
            self.cnt[inc] += 16 if inc in DMA_SEMS else 1
        self.ops[eng].append(("op", fn, inc))
        return self.cnt[inc] if inc else None

    def barrier(self):
        engines = ("tensor", "vector", "scalar", "gpsimd")
        snap0 = dict(self.cnt)
        for e in engines:
            for s in DMA_SEMS:
                self.w(e, s, snap0[s])
        self.do("tensor", self.tensor_barrier_fn, inc="pe")
        self.do("vector", lambda h: h.engine_nop())
        self.do("scalar", self.scalar_barrier_fn)
        self.do("gpsimd", lambda h: h.engine_nop())
        snap = dict(self.cnt)
        for e in ENGINES:
            for e2 in engines:
                if e2 != e:
                    self.w(e, ESEM[e2], snap[ESEM[e2]])
            for s in DMA_SEMS:
                self.w(e, s, snap[s])

    def replay(self, nc, sems):
        def mk_body(ename):
            def body(h):
                for item in self.ops[ename]:
                    if item[0] == "wait":
                        h.wait_ge(sems[item[1]], item[2])
                    else:
                        inst = item[1](h)
                        if item[2] is not None:
                            inst.then_inc(sems[item[2]],
                                          16 if item[2] in DMA_SEMS else 1)
            return body
        with nc.Block() as block:
            block.tensor(mk_body("tensor"))
            block.vector(mk_body("vector"))
            block.scalar(mk_body("scalar"))
            block.sync(mk_body("sync"))
            block.gpsimd(mk_body("gpsimd"))


class Arena:
    def __init__(self, handle, cols):
        self.h, self.cols, self.off, self.marks = handle, cols, 0, []

    def alloc(self, ncols, align=16):
        o = -(-self.off // align) * align
        assert o + ncols <= self.cols, f"arena overflow {o}+{ncols}>{self.cols}"
        self.off = o + ncols
        return self.h[:, o:o + ncols]

    def push(self):
        self.marks.append(self.off)

    def pop(self):
        self.off = self.marks.pop()


def host_constants():
    c = {}
    c["ident_f"] = np.eye(128, dtype=np.float32)
    q = np.arange(128)[:, None]
    p = np.arange(128)[None, :]
    sq, sp = q // 32, p // 32
    c["bigmask"] = np.where((sp > sq) | ((sp == sq) & (p >= q)), 1.0, 0.0).astype(BFNP)
    for j in range(CONVK):
        c[f"econv{j}"] = ((p - q) == j).astype(np.float32)
    for j in range(1, CONVK):
        c[f"ebnd{j}"] = ((p + 128 - q) == j).astype(np.float32)
    pr = np.arange(128, dtype=np.float32)[None, :].repeat(R, 0)
    c["exp_qs"] = pr % 32
    for dlt in range(4):
        c[f"exp_kv{dlt}"] = 32.0 * dlt - (pr % 32)
    c["exp_qeff"] = pr + 1.0
    c["exp_keff"] = 127.0 - pr
    c["ones_row_b"] = np.ones((1, 128), dtype=BFNP)
    c["ones_row_f"] = np.ones((1, 128), dtype=np.float32)
    c["ones_col_f"] = np.ones((128, 1), dtype=np.float32)
    c["ones_col_b"] = np.ones((128, 1), dtype=BFNP)
    c["ones16_f"] = np.ones((R, 1), dtype=np.float32)
    c["ones16r_f"] = np.ones((1, R), dtype=np.float32)
    return c


def host_prep(core, inputs, nseq, S):
    f32 = np.float32
    h = np.asarray(inputs["L0_attn_h"], dtype=f32)
    x0 = np.ascontiguousarray(h[core * nseq:(core + 1) * nseq, :S])
    d = {"x0": x0, "x0t": np.ascontiguousarray(x0.transpose(0, 2, 1))}
    d["uv"] = np.concatenate([inputs["L0_attn_u"], inputs["L0_attn_v"]], axis=1)
    d["projr"] = np.asarray(inputs["L0_attn_proj_w"], f32).T
    d["w1"] = np.asarray(inputs["L0_attn_mlp_w1"], f32)
    d["w2"] = np.asarray(inputs["L0_attn_mlp_w2"], f32)
    d["b1r"] = np.asarray(inputs["L0_attn_mlp_b1"], f32).reshape(FFC, 128).T
    d["b2row"] = np.asarray(inputs["L0_attn_mlp_b2"], f32).reshape(1, D)
    d["pbrow"] = np.asarray(inputs["L0_attn_proj_b"], f32).reshape(1, D)
    d["wupT"] = np.asarray(inputs["L0_ffn_w_up"], f32).T
    d["wdnT"] = np.asarray(inputs["L0_ffn_w_down"], f32).T
    d["n1w"] = np.asarray(inputs["L0_attn_norm1_w"], f32).reshape(DC, 128).T
    d["n2w"] = np.asarray(inputs["L0_attn_norm2_w"], f32).reshape(DC, 128).T
    d["declog"] = np.asarray(inputs["L0_attn_decay_logit"], f32).reshape(R, 1)
    d["alplog"] = np.asarray(inputs["L0_attn_alpha_logit"], f32).reshape(R, 1)
    d["gatelog"] = np.asarray(inputs["L0_attn_k_base_gate_logit"], f32).reshape(1, 1)
    d["kern"] = np.asarray(inputs["L0_attn_k_base_kernel"], f32).reshape(1, CONVK)
    d.update(host_constants())
    return {k: np.ascontiguousarray(v) for k, v in d.items()}


def build(nseq=2, S=2048, dbg=(), upto="p4"):
    seqt = S // 128
    NT = nseq * seqt
    NG = NT // 4
    assert NT % 4 == 0 or upto == "p12" 
    PASSES = ["p12", "p3", "p4"]
    upi = PASSES.index(upto)
    nc = bass.Bass("TRN2", target_bir_lowering=False, debug=False)

    di = {}
    def inp(name, shape, dt=F32):
        di[name] = nc.dram_tensor(name, list(shape), dt, kind="ExternalInput").ap()
    inp("x0", (nseq, S, D)); inp("x0t", (nseq, D, S))
    inp("uv", (D, 2 * R)); inp("projr", (D, D))
    inp("w1", (D, FF)); inp("w2", (FF, D))
    inp("b1r", (128, FFC)); inp("b2row", (1, D)); inp("pbrow", (1, D))
    inp("wupT", (D, FF)); inp("wdnT", (FF, D))
    inp("n1w", (128, DC)); inp("n2w", (128, DC))
    inp("declog", (R, 1)); inp("alplog", (R, 1)); inp("gatelog", (1, 1))
    inp("kern", (1, CONVK))
    inp("ident_f", (128, 128)); inp("bigmask", (128, 128), BF16)
    for j in range(CONVK): inp(f"econv{j}", (128, 128))
    for j in range(1, CONVK): inp(f"ebnd{j}", (128, 128))
    inp("exp_qs", (R, 128))
    for dlt in range(4): inp(f"exp_kv{dlt}", (R, 128))
    inp("exp_qeff", (R, 128)); inp("exp_keff", (R, 128))
    inp("ones_row_b", (1, 128), BF16); inp("ones_row_f", (1, 128))
    inp("ones_col_f", (128, 1)); inp("ones_col_b", (128, 1), BF16)
    inp("ones16_f", (R, 1)); inp("ones16r_f", (1, R))

    out_d = nc.dram_tensor("out", [NT, 128, D], F32, kind="ExternalOutput").ap()
    def scr(name, shape, dt):
        kind = "ExternalOutput" if name in dbg else "Internal"
        return nc.dram_tensor(name, list(shape), dt, kind=kind).ap()
    h2_d = scr("h2", (NT, 128, D), BF16)
    hn2_d = scr("hn2", (NT, 128, D), BF16)
    aout_d = scr("aout", (NT, 128, D), BF16)
    x1_d = scr("x1", (NT, 128, D), F32)
    fin_d = scr("fin", (NT, 128, D), BF16)
    raw_d = scr("raw", (NT, 128, D), BF16)
    dbg_hn = scr("dbg_hn", (NT, 128, D), BF16)
    dbg_mo = scr("dbg_mo", (NT, 128, D), BF16)
    dbg_q = scr("dbg_q", (NT, R, 128), BF16)
    dbg_at = scr("dbg_at", (NT, 128, 128), BF16)
    dbg_rows = scr("dbg_rows", (128, 8 * NT), F32)
    wqup_d = scr("wqup", (D, FF), BF16)
    wqdn_d = scr("wqdn", (FF, D), BF16)

    r = Rec()
    es = contextlib.ExitStack()
    with es:
        sems = {s: es.enter_context(nc.semaphore(s)) for s in SEMS}
        ABp = Arena(es.enter_context(nc.sbuf_tensor("ab_p", [128, 1400], BF16)), 1400)
        AFp = Arena(es.enter_context(nc.sbuf_tensor("af_p", [128, 1510], F32)), 1510)
        pb = [es.enter_context(nc.psum_tensor(f"pb{i}", [128, 512], F32)) for i in range(8)]

        def open_scope(tag, nbf, nf):
            st = contextlib.ExitStack()
            ab = Arena(st.enter_context(
                nc.sbuf_tensor(f"ab_{tag}", [128, nbf], BF16)), nbf)
            af = Arena(st.enter_context(
                nc.sbuf_tensor(f"af_{tag}", [128, nf], F32)), nf)
            return st, ab, af

        ones_row_b = ABp.alloc(128)[0:1, :]
        ones_col_b = ABp.alloc(16)[:, 0:1]
        b2row_b = ABp.alloc(D)[0:1, :]
        ident_b = ABp.alloc(16)[0:1, 0:1]
        ident_f = AFp.alloc(128)
        ones_row_f = AFp.alloc(128)[0:1, :]
        ones_col_f = AFp.alloc(16)[:, 0:1]
        ones16_f = AFp.alloc(16)[0:R, 0:1]
        ones16r_f = AFp.alloc(16)[0:1, 0:R]
        n1w = AFp.alloc(DC); n2w = AFp.alloc(DC); b1r = AFp.alloc(FFC)
        rows = {nm: AFp.alloc(NT) for nm in
                ["m", "cln", "mcln", "ccomb", "mccomb", "rstd2",
                 "suma", "ssqa", "A3", "B3", "S4", "M4",
                 "ssqg", "sumr", "ssqr", "A4", "B4", "v0", "Sdot"]}
        rt = [AFp.alloc(16)[:, 0:1] for _ in range(10)]
        thr_up = AFp.alloc(16)[:, 0:1]; nthr_up = AFp.alloc(16)[:, 0:1]
        thr_dn = AFp.alloc(16)[:, 0:1]; nthr_dn = AFp.alloc(16)[:, 0:1]
        part_arr = AFp.alloc(64)
        rtq = AFp.alloc(16)[:, 0:1]; rtq2b = AFp.alloc(16)[:, 0:1]

        for cv in (1e-5, -0.5, 0.5, 1.000001, 1e-11, 1e-6, 1.0 / D, 1.0 / FF,
                   -1.0 / D, -1.000001 / D):
            cap = AFp.alloc(16)[:, 0:1]
            r.do("vector", lambda h, a=cap, v=cv: h.memset(a, v))
            nc.const_aps.register(F32, cv, cap)

        def L(dst, src, stream="swst"):
            eng = "sync" if stream.startswith("s") else "gpsimd"
            r.w(eng, stream, r.cnt[stream])
            return r.do(eng, lambda h, d=dst, s=src: h.dma_start(out=d, in_=s), inc=stream)

        L(ones_row_b, di["ones_row_b"]); L(ones_col_b, di["ones_col_b"])
        L(b2row_b, di["b2row"], "pw")
        L(ident_f, di["ident_f"]); L(ones_row_f, di["ones_row_f"])
        L(ones_col_f, di["ones_col_f"]); L(ones16_f, di["ones16_f"])
        L(ones16r_f, di["ones16r_f"])
        L(n1w, di["n1w"]); L(n2w, di["n2w"]); L(b1r, di["b1r"])
        r.do("vector", lambda h: h.memset(ident_b, 1.0))
        barr_row = AFp.alloc(16)[0:1, 0:1]
        r.tensor_barrier_fn = lambda h: h.matmul(
            pb[1][0:1, 456:457], ident_f[0:1, 0:1], ident_f[0:1, 0:1],
            start=True, stop=True)
        r.scalar_barrier_fn = lambda h: h.memzero(barr_row)

        scW, ABw, _AFw = open_scope("w1", DC * FF + 16, 16)
        w1_bf = ABw.alloc(DC * FF)

        sc12, AB, AFr = open_scope("p12", 38200, 15400)
        bigmask = AB.alloc(128)
        uvp = AB.alloc(2 * R * DC)
        convm_b = AB.alloc(128); cm3_b = AB.alloc(128)
        uvsum_nb = AB.alloc(2 * R)[0:1, :]
        pbrow_b = AB.alloc(D)[0:1, :]
        state_b = [AB.alloc(D)[0:R, :] for _ in range(nseq)]
        x0t_b = [AB.alloc(D) for _ in range(4)]
        HNR = 3 if nseq == 2 else 5
        hn_b = [[AB.alloc(D) for _ in range(HNR)] for _ in range(nseq)]
        mo_b = [AB.alloc(D) for _ in range(2)]
        sqscr_b = AB.alloc(D)
        h2_b = [AB.alloc(D) for _ in range(6)]
        hn2_b = [AB.alloc(D) for _ in range(2)]
        at_b = [AB.alloc(128) for _ in range(2)]
        qabf = [AB.alloc(128)[0:R, :] for _ in range(4)]
        knbf = [AB.alloc(128)[0:R, :] for _ in range(4)]
        qsbf = [AB.alloc(128)[0:R, :] for _ in range(4)]
        qeffbf = [AB.alloc(128)[0:R, :] for _ in range(4)]
        kvbf = [[AB.alloc(128)[0:R, :] for _ in range(4)] for _ in range(4)]
        kefft_b = [AB.alloc(16) for _ in range(4)]
        mrow_b = [AB.alloc(128)[0:1, :] for _ in range(2)]

        qpat = {nm: AFr.alloc(128)[0:R, :] for nm in
                ["qs", "kv0", "kv1", "kv2", "kv3", "qeff", "keff"]}
        alpha_row = AFr.alloc(16)[0:1, 0:R]
        g128 = AFr.alloc(16)[0:R, 0:1]
        gam = AFr.alloc(16)[0:R, 0:1]; lng = AFr.alloc(16)[0:R, 0:1]
        gl = AFr.alloc(16)[0:1, 0:1]; kr = AFr.alloc(16)[0:1, 0:CONVK]
        dl = AFr.alloc(16)[0:R, 0:1]; al = AFr.alloc(16)[0:R, 0:1]
        alp = AFr.alloc(16)[0:R, 0:1]
        gkrow = AFr.alloc(16)[0:1, 0:CONVK]
        gk_b = AFr.alloc(CONVK)
        state_f = [AFr.alloc(D)[0:R, :] for _ in range(nseq)]
        x0_t = [AFr.alloc(D) for _ in range(4)]
        wstg = [AFr.alloc(D) for _ in range(2)]
        emsk = {**{f"econv{j}": AFr.alloc(128) for j in range(CONVK)},
                **{f"ebnd{j}": AFr.alloc(128) for j in range(1, CONVK)}}
        cscr = AFr.alloc(128)
        qkf = [AFr.alloc(256)[0:R, :] for _ in range(4)]
        sqf = [AFr.alloc(256)[0:R, :] for _ in range(2)]
        keff_f = [AFr.alloc(128)[0:R, :] for _ in range(2)]
        inv_row = [AFr.alloc(256)[0:1, :] for _ in range(2)]
        wstg1 = [AFr.alloc(1024) for _ in range(2)]

        def q_ps(pi): return pb[pi][0:R, 0:128]
        def k_ps(pi): return pb[pi][0:R, 128:256]
        def qk_ps(pi): return pb[pi][0:R, 0:256]
        def mrow_ps(pi): return pb[pi][0:1, 256:384]
        def at_ps(pi): return pb[pi][:, 256:384]
        def kefft_ps(pi): return pb[pi][:, 384:400]
        def ssq_ps(pi): return pb[2 + pi][0:1, 0:256]
        def bc_ps(pi): return pb[2 + pi][0:R, 0:256]
        def dlt_ps(pi): return pb[2 + pi][0:R, 256:512]
        def o_ps(pi, c): return pb[4 + 2 * pi + c // 4][:, 128 * (c % 4):128 * (c % 4) + 128]
        def pj_ps(pi, eh): return pb[4 + 2 * pi + eh][:, :]
        arow_ps = pb[1][0:1, 400:416]; usum_ps = pb[1][0:1, 416:448]
        gkb_ps = pb[1][:, 448:452]

        L(bigmask, di["bigmask"])
        L(pbrow_b, di["pbrow"], "pw")
        for nm in qpat:
            L(qpat[nm], di["exp_" + nm])
        for nm in emsk:
            L(emsk[nm], di[nm])
        L(gl, di["gatelog"]); L(kr, di["kern"]); L(dl, di["declog"]); L(al, di["alplog"])
        uvstg = AFr.alloc(2 * R * DC)
        L(uvstg.rearrange("p (c r) -> p c r", c=DC),
          di["uv"].rearrange("(c p) r -> p c r", p=128))
        ld0 = r.cnt["swst"]; lp0 = r.cnt["pw"]

        r.w("scalar", "swst", ld0); r.w("scalar", "pw", lp0)
        r.do("scalar", preload_set(nc, [AF.Sigmoid]))
        r.do("scalar", lambda h: h.activation(out=gam, in_=dl, func=AF.Sigmoid))
        r.do("scalar", lambda h: h.activation(out=alp, in_=al, func=AF.Sigmoid))
        r.do("scalar", lambda h: h.activation(out=gl, in_=gl, func=AF.Sigmoid), inc="act")
        a_sig = r.cnt["act"]
        r.w("vector", "act", a_sig)
        r.do("vector", lambda h: h.tensor_scalar_max(out=gam, in0=gam, scalar1=GAMMA_MIN))
        r.do("vector", lambda h: h.tensor_scalar_min(out=gam, in0=gam, scalar1=GAMMA_MAX))
        r.do("vector", lambda h: h.tensor_scalar(out=gkrow, in0=kr, scalar1=gl,
                                                 scalar2=None, op0=ALU.mult), inc="dve")
        d_gk = r.cnt["dve"]
        r.do("scalar", preload_set(nc, [AF.Ln, AF.Exp, AF.Square]))
        r.w("scalar", "dve", d_gk)
        r.do("scalar", lambda h: h.activation(out=lng, in_=gam, func=AF.Ln), inc="act")
        a_ln = r.cnt["act"]
        r.w("vector", "act", a_ln)
        for nm in qpat:
            r.do("vector", lambda h, t=qpat[nm]: h.tensor_scalar_mul(out=t, in0=t, scalar1=lng))
        r.do("vector", lambda h: h.tensor_scalar_mul(out=g128, in0=lng, scalar1=128.0), inc="dve")
        d_pat = r.cnt["dve"]
        r.w("scalar", "dve", d_pat)
        for nm in qpat:
            r.do("scalar", lambda h, t=qpat[nm]: h.activation(out=t, in_=t, func=AF.Exp))
        r.do("scalar", lambda h: h.activation(out=g128, in_=g128, func=AF.Exp), inc="act")
        a_pat = r.cnt["act"]
        r.w("tensor", "act", a_sig)
        r.do("tensor", lambda h: h.transpose(arow_ps, alp, ident_f[0:R, 0:R]), inc="pe")
        p_ar = r.cnt["pe"]
        r.w("vector", "pe", p_ar)
        r.do("vector", lambda h: h.tensor_copy(out=alpha_row, in_=arow_ps), inc="dve")
        r.w("tensor", "dve", d_gk)
        r.do("tensor", lambda h: h.matmul(gkb_ps, ones_row_f, gkrow, start=True, stop=True), inc="pe")
        r.w("vector", "pe", r.cnt["pe"])
        r.do("vector", lambda h: h.tensor_copy(out=gk_b, in_=gkb_ps), inc="dve")
        d_gkb = r.cnt["dve"]
        r.w("vector", "swst", ld0)
        r.do("vector", lambda h: h.tensor_scalar_mul(out=cscr, in0=emsk["econv0"],
                                                     scalar1=gk_b[:, 0:1]))
        for j in range(1, CONVK):
            r.do("vector", lambda h, j=j: h.scalar_tensor_tensor(
                out=cscr, in0=emsk[f"econv{j}"], scalar=gk_b[:, j:j + 1], in1=cscr,
                op0=ALU.mult, op1=ALU.add))
        r.do("vector", lambda h: h.tensor_copy(out=convm_b, in_=cscr))
        r.do("vector", lambda h: h.tensor_scalar_mul(out=cscr, in0=emsk["ebnd1"],
                                                     scalar1=gk_b[:, 1:2]))
        for j in range(2, CONVK):
            r.do("vector", lambda h, j=j: h.scalar_tensor_tensor(
                out=cscr, in0=emsk[f"ebnd{j}"], scalar=gk_b[:, j:j + 1], in1=cscr,
                op0=ALU.mult, op1=ALU.add))
        r.do("vector", lambda h: h.tensor_copy(out=cm3_b, in_=cscr))
        for c in range(DC):
            r.do("vector", lambda h, c=c: h.tensor_scalar_mul(
                out=uvp[:, 32 * c:32 * c + 32], in0=uvstg[:, 32 * c:32 * c + 32],
                scalar1=n1w[:, c:c + 1]))
        r.do("vector", lambda h: h.engine_nop(), inc="dve")
        d_uvp = r.cnt["dve"]
        r.w("tensor", "dve", d_uvp)
        for c in range(DC):
            r.do("tensor", lambda h, c=c: h.matmul(
                usum_ps, ones_col_b, uvp[:, 32 * c:32 * c + 32],
                start=(c == 0), stop=(c == DC - 1)), inc="pe" if c == DC - 1 else None)
        p_us = r.cnt["pe"]
        r.w("vector", "pe", p_us)
        r.do("vector", lambda h: h.tensor_scalar_mul(out=uvsum_nb, in0=usum_ps, scalar1=-1.0 / D), inc="dve")
        projr_bf = AB.alloc(DC * D)
        dpj = {}
        for c in range(DC):
            r.w("sync", "dve", dpj.get(c - 2, 0))
            lw = L(wstg[c % 2], di["projr"][128 * c:128 * c + 128, :])
            r.w("vector", "swst", lw)
            dpj[c] = r.do("vector", lambda h, c=c: h.tensor_scalar_mul(
                out=projr_bf[:, D * c:D * (c + 1)], in0=wstg[c % 2],
                scalar1=n1w[:, c:c + 1]), inc="dve")
        for s in range(nseq):
            r.do("vector", lambda h, s=s: h.memset(state_f[s], 0.0))
            r.do("vector", lambda h, s=s: h.memset(state_b[s], 0.0))
            r.do("vector", lambda h, s=s: h.memset(hn_b[s][HNR - 1], 0.0))
        r.do("vector", lambda h: h.memset(pb[0][:, 256:384], 0.0))
        r.do("vector", lambda h: h.memset(pb[1][:, 256:384], 0.0))
        r.barrier()

        ms = [dict() for _ in range(NT)]
        order = [(s2, t2) for t2 in range(seqt) for s2 in range(nseq)]
        def M(gi, key):
            return ms[gi][key] if gi >= 0 else 0
        def EM(j, key):
            if j < 0:
                return 0
            s2, t2 = order[j]
            return ms[s2 * seqt + t2][key]
        tiles_w = [("up", di["wupT"], ai, bi) for ai in range(DC) for bi in range(4)] + \
                  [("dn", di["wdnT"], mi, 0) for mi in range(FFC)]
        alpha_sched = {k2: [] for k2 in range(NT)}
        for i2 in range(len(tiles_w)):
            alpha_sched[i2 * NT // len(tiles_w)].append(i2)
        lastred = {}
        d_w1prep = 0

        w1_sched = {k2: [] for k2 in range(NT)}
        for i2 in range(32):
            w1_sched[i2 * NT // 32].append(i2)
        lw_d = {}

        def emit_ld_xt(k2):
            if k2 >= NT:
                return
            s2, t2 = order[k2]
            gi2 = s2 * seqt + t2
            r.w("gpsimd", "pe", EM(k2 - 4, "p_qk"))
            ms[gi2]["ld_xt"] = L(
                x0t_b[k2 % 4].rearrange("p (c n) -> p c n", c=DC),
                di["x0t"][s2, :, 128 * t2:128 * t2 + 128].rearrange("(c p) n -> p c n", p=128),
                "pxt")
        emit_ld_xt(0); emit_ld_xt(1)

        def ctx(k):
            s, t = order[k]
            gi = s * seqt + t
            return s, t, gi, k % 2

        def S1(k):
            s, t, gi, par = ctx(k)
            x0c = x0_t[k % 4]
            hnc = hn_b[s][t % HNR]
            rt0 = rt[4 * par + 0]; rt1 = rt[4 * par + 1]; rt2 = rt[4 * par + 2]
            r.w("sync", "dve", EM(k - 4, "d_ain"))
            r.w("sync", "act", EM(k - 4, "a_sq"))
            ms[gi]["ld_x0"] = L(x0c, di["x0"][s, 128 * t:128 * t + 128, :], "sx0")
            emit_ld_xt(k + 2)
            r.w("scalar", "sx0", ms[gi]["ld_x0"])
            r.w("scalar", "dve", EM(k - 2, "d_v"))
            ms[gi]["a_sq"] = r.do("scalar", lambda h, o=sqscr_b, i=x0c, a=rt0:
                                  h.activation(out=o, in_=i, func=AF.Square, accum_out=a))
            r.w("vector", "sx0", ms[gi]["ld_x0"])
            r.do("vector", lambda h, i=x0c, o=rt1: h.reduce_sum(out=o, in_=i, axis=AX.X))
            r.w("vector", "act", ms[gi]["a_sq"])
            r.do("vector", lambda h, o=rows["m"][:, gi:gi + 1], i=rt1:
                 h.tensor_scalar_mul(out=o, in0=i, scalar1=1.0 / D))
            ms[gi]["d_v"] = r.do("vector", lambda h, o=rows["v0"][:, gi:gi + 1], i=rt1,
                                 m_=rows["m"][:, gi:gi + 1], ss=rt0: h.scalar_tensor_tensor(
                                 out=o, in0=i, scalar=m_, in1=ss, op0=ALU.mult, op1=ALU.subtract))
            if "dbg_rows" in dbg:
                r.do("vector", lambda h, o=rows["ssqa"][:, gi:gi + 1], i=rt0: h.tensor_copy(out=o, in_=i))
            r.w("scalar", "dve", ms[gi]["d_v"])
            r.do("scalar", lambda h, o=rt[4 * par + 3], i=rows["v0"][:, gi:gi + 1]:
                 h.activation(out=o, in_=i, func=AF.Ln, scale=-1.0 / D, bias=1e-5))
            r.do("scalar", lambda h, o=rows["cln"][:, gi:gi + 1], i=rt[4 * par + 3]:
                 h.activation(out=o, in_=i, func=AF.Exp, scale=-0.5))
            r.do("scalar", lambda h, o=rt[8 + par], i=rows["v0"][:, gi:gi + 1]:
                 h.activation(out=o, in_=i, func=AF.Ln, scale=-1.000001 / D, bias=1e-11), sw=False)
            a_rows = r.do("scalar", lambda h, o=rows["ccomb"][:, gi:gi + 1], i=rt[8 + par]:
                          h.activation(out=o, in_=i, func=AF.Exp, scale=-0.5))
            ms[gi]["a_rows"] = a_rows
            r.w("vector", "act", a_rows)
            if t >= HNR - 1:
                r.w("vector", "pe", M(gi - (HNR - 1), "p_o"))
            ms[gi]["d_hn"] = r.do("vector", lambda h, o=hnc, i=x0c,
                                  m_=rows["m"][:, gi:gi + 1], c=rows["ccomb"][:, gi:gi + 1]:
                                  h.tensor_scalar(out=o, in0=i, scalar1=m_, scalar2=c,
                                                  op0=ALU.subtract, op1=ALU.mult))
            r.w("vector", "sh2st", EM(k - 6, "st_h2"))
            ms[gi]["d_ain"] = r.do("vector", lambda h, o=h2_b[k % 6], i=x0c,
                                   m_=rows["m"][:, gi:gi + 1], c=rows["cln"][:, gi:gi + 1]:
                                   h.tensor_scalar(out=o, in0=i, scalar1=m_, scalar2=c,
                                                   op0=ALU.subtract, op1=ALU.mult), sw=False)
            if "dbg_hn" in dbg:
                r.w("sync", "dve", ms[gi]["d_hn"])
                L(dbg_hn[gi], hnc, "sh2st")

        def S2(k):
            s, t, gi, par = ctx(k)
            r.w("tensor", "pxt", ms[gi]["ld_xt"])
            r.w("tensor", "dve", EM(k - 2, "d_at"))
            r.w("tensor", "act", EM(k - 2, "a_mrow"))
            for c in range(DC):
                p_mrow = r.do("tensor", lambda h, c=c, x=x0t_b[k % 4], o=mrow_ps(par):
                              h.matmul(o[0:1, 0:128], ones_col_b, x[:, 128 * c:128 * c + 128],
                                       start=(c == 0), stop=(c == DC - 1)),
                              inc="pe" if c == DC - 1 else None)
            r.w("scalar", "pe", p_mrow)
            ms[gi]["a_mrow"] = r.do("scalar", lambda h, o=mrow_b[par], i=mrow_ps(par):
                                    h.copy(out=o, in_=i))
            r.w("tensor", "pxt", ms[gi]["ld_xt"])
            r.w("tensor", "act", ms[gi]["a_mrow"])
            r.w("tensor", "dve", EM(k - 2, "d_qkevac"))
            for c in range(DC):
                r.do("tensor", lambda h, c=c, x=x0t_b[k % 4], o=q_ps(par): h.matmul(
                    o, uvp[:, 32 * c:32 * c + 16], x[:, 128 * c:128 * c + 128],
                    start=(c == 0), stop=False))
            r.do("tensor", lambda h, mr=mrow_b[par], o=q_ps(par): h.matmul(
                o, uvsum_nb[0:1, 0:R], mr, start=False, stop=True))
            for c in range(DC):
                r.do("tensor", lambda h, c=c, x=x0t_b[k % 4], o=k_ps(par): h.matmul(
                    o, uvp[:, 32 * c + 16:32 * c + 32], x[:, 128 * c:128 * c + 128],
                    start=(c == 0), stop=False))
            ms[gi]["p_qk"] = r.do("tensor", lambda h, mr=mrow_b[par], o=k_ps(par): h.matmul(
                o, uvsum_nb[0:1, R:2 * R], mr, start=False, stop=True), inc="pe")
            r.w("vector", "pe", ms[gi]["p_qk"])
            r.w("vector", "pe", EM(k - 4, "pe_end"))
            r.do("vector", lambda h, o=qkf[k % 4], i=qk_ps(par): h.tensor_copy(out=o, in_=i))
            ms[gi]["d_qkevac"] = r.do("vector", lambda h, o=sqf[par], i=qkf[k % 4]:
                                      h.tensor_mul(out=o, in0=i, in1=i))
            for i2 in alpha_sched.get(k, []):
                nm2, src2, a2, b2 = tiles_w[i2]
                r.w("sync", "act", lastred.get(i2 - 2, 0))
                src_ap = src2[128 * a2:128 * a2 + 128, 1024 * b2:1024 * b2 + 1024] \
                    if nm2 == "up" else src2[128 * a2:128 * a2 + 128, :]
                lw2 = L(wstg[i2 % 2], src_ap, "swst")
                r.w("scalar", "swst", lw2)
                lastred[i2] = r.do("scalar", lambda h, i2=i2, w=wstg[i2 % 2]: h.activation(
                    out=w, in_=w, func=AF.Abs, accum_out=part_arr[:, i2:i2 + 1]))

        def S3(k):
            s, t, gi, par = ctx(k)
            r.w("tensor", "dve", ms[gi]["d_qkevac"])
            r.w("tensor", "dve", EM(k - 2, "d_var"))
            r.w("tensor", "dve", EM(k - 2, "d_st"))
            p_ssq = r.do("tensor", lambda h, i=sqf[par], o=ssq_ps(par):
                         h.matmul(o, ones16_f, i, start=True, stop=True), inc="pe")
            r.w("vector", "pe", p_ssq)
            ms[gi]["d_inv"] = r.do("vector", lambda h, o=inv_row[par], i=ssq_ps(par):
                                   h.tensor_scalar_max(out=o, in0=i, scalar1=1e-16))
            r.w("scalar", "dve", ms[gi]["d_inv"])
            r.do("scalar", lambda h, o=inv_row[par]: h.activation(out=o, in_=o, func=AF.Ln))
            a_inv = r.do("scalar", lambda h, o=inv_row[par]:
                         h.activation(out=o, in_=o, func=AF.Exp, scale=-0.5))
            r.w("tensor", "act", a_inv)
            r.w("tensor", "dve", ms[gi]["d_inv"])
            r.do("tensor", lambda h, i=inv_row[par], o=bc_ps(par): h.matmul(
                o[0:R, 0:128], alpha_row, i[0:1, 0:128], start=True, stop=True))
            p_bc = r.do("tensor", lambda h, i=inv_row[par], o=bc_ps(par): h.matmul(
                o[0:R, 128:256], ones16r_f, i[0:1, 128:256], start=True, stop=True), inc="pe")
            r.w("vector", "pe", p_bc)
            r.w("vector", "pe", EM(k - 4, "pe_end"))
            r.do("vector", lambda h, o=qabf[k % 4], i=qkf[k % 4], b=bc_ps(par):
                 h.tensor_mul(out=o, in0=i[:, 0:128], in1=b[0:R, 0:128]))
            r.do("vector", lambda h, o=knbf[k % 4], i=qkf[k % 4], b=bc_ps(par):
                 h.tensor_mul(out=o, in0=i[:, 128:256], in1=b[0:R, 128:256]))
            r.do("vector", lambda h, o=qsbf[k % 4], i=qabf[k % 4]: h.tensor_mul(out=o, in0=i, in1=qpat["qs"]))
            for dlt in range(4):
                r.do("vector", lambda h, o=kvbf[dlt][k % 4], i=knbf[k % 4], pat=qpat[f"kv{dlt}"]:
                     h.tensor_mul(out=o, in0=i, in1=pat), sw=(dlt == 0))
            r.do("vector", lambda h, o=qeffbf[k % 4], i=qabf[k % 4]:
                 h.tensor_mul(out=o, in0=i, in1=qpat["qeff"]), sw=False)
            ms[gi]["d_var"] = r.do("vector", lambda h, o=keff_f[par], i=knbf[k % 4]:
                                   h.tensor_mul(out=o, in0=i, in1=qpat["keff"]), sw=False)
            if "dbg_q" in dbg:
                r.w("sync", "dve", ms[gi]["d_var"])
                L(dbg_q[gi], qabf[k % 4][0:R, :], "sh2st")

        def S4(k):
            s, t, gi, par = ctx(k)
            r.w("tensor", "dve", ms[gi]["d_var"])
            r.w("tensor", "act", EM(k - 2, "a_kt"))
            p_kt = r.do("tensor", lambda h, i=keff_f[par], o=kefft_ps(par):
                        h.transpose(o, i, ident_f[0:R, 0:R]), inc="pe")
            r.w("scalar", "pe", p_kt)
            ms[gi]["a_kt"] = r.do("scalar", lambda h, o=kefft_b[k % 4], i=kefft_ps(par):
                                  h.copy(out=o, in_=i))
            r.w("tensor", "dve", EM(k - 2, "d_at"))
            for b in range(4):
                for a in range(b, 4):
                    dlt = a - b
                    last = (b == 3 and a == 3)
                    r.do("tensor", lambda h, b=b, a=a, kv=kvbf[dlt][k % 4], qs=qsbf[k % 4],
                         o=pb[par]: h.matmul(
                        o[32 * b:32 * b + 32, 256 + 32 * a:256 + 32 * a + 32],
                        kv[:, 32 * b:32 * b + 32],
                        qs[:, 32 * a:32 * a + 32],
                        start=True, stop=True, tile_position=(0, 32 * b)),
                        inc="pe" if last else None)
            ms[gi]["p_at"] = r.cnt["pe"]
            r.w("vector", "pe", ms[gi]["p_at"])
            r.w("vector", "pe", EM(k - 2, "p_o"))
            ms[gi]["d_at"] = r.do("vector", lambda h, o=at_b[par], i=at_ps(par):
                                  h.tensor_tensor(out=o, in0=i, in1=bigmask, op=ALU.mult))
            if "dbg_at" in dbg:
                r.w("sync", "dve", ms[gi]["d_at"])
                L(dbg_at[gi], at_b[par], "sh2st")
            for i3 in w1_sched.get(k, []):
                c3, j3 = i3 // 4, i3 % 4
                r.w("sync", "dve", lw_d.get(i3 - 2, 0))
                lw3 = L(wstg1[i3 % 2],
                        di["w1"][128 * c3:128 * c3 + 128, 1024 * j3:1024 * j3 + 1024], "swst")
                r.w("vector", "swst", lw3)
                lw_d[i3] = r.do("vector", lambda h, c3=c3, j3=j3, i=wstg1[i3 % 2]:
                                h.tensor_scalar_mul(
                                    out=w1_bf[:, FF * c3 + 1024 * j3:FF * c3 + 1024 * j3 + 1024],
                                    in0=i, scalar1=n2w[:, c3:c3 + 1]))

        def S5(k):
            s, t, gi, par = ctx(k)
            hnc = hn_b[s][t % HNR]
            hnp = hn_b[s][(t - 1) % HNR]
            r.w("tensor", "dve", ms[gi]["d_at"])
            r.w("tensor", "dve", ms[gi]["d_hn"])
            if t >= 1:
                r.w("tensor", "dve", M(gi - 1, "d_st"))
            r.w("tensor", "dve", EM(k - 2, "d_h2"))
            r.w("tensor", "dve", EM(k - 2, "d_moevac"))
            r.w("tensor", "act", EM(k - 2, "a_moevac"))
            for c in range(DC):
                last_c = (c == DC - 1)
                r.do("tensor", lambda h, c=c, hn=hnc, at=at_b[par], o=o_ps(par, c): h.matmul(
                    o, hn[:, 128 * c:128 * c + 128], at, start=True, stop=False))
                r.do("tensor", lambda h, c=c, st=state_b[s], qe=qeffbf[k % 4], o=o_ps(par, c): h.matmul(
                    o, st[:, 128 * c:128 * c + 128], qe, start=False, stop=False))
                ms[gi]["p_o"] = r.do("tensor", lambda h, c=c, hn=hnc, t0=(t == 0), o=o_ps(par, c): h.matmul(
                    o, hn[:, 128 * c:128 * c + 128], convm_b, start=False, stop=t0),
                    inc="pe" if (last_c and t == 0) else None)
                if t >= 1:
                    ms[gi]["p_o"] = r.do("tensor", lambda h, c=c, hp=hnp, o=o_ps(par, c): h.matmul(
                        o, hp[:, 128 * c:128 * c + 128], cm3_b, start=False, stop=True),
                        inc="pe" if last_c else None)
            d_q4 = EM(k - 2, "d_st")
            for qq4 in range(4):
                r.w("tensor", "act", ms[gi]["a_kt"])
                r.w("tensor", "dve", d_q4)
                p_dl = r.do("tensor", lambda h, qq4=qq4, kt=kefft_b[k % 4], hn=hnc, o=dlt_ps(par):
                            h.matmul(o, kt[:, 0:R], hn[:, 256 * qq4:256 * qq4 + 256],
                                     start=True, stop=True), inc="pe")
                r.w("vector", "pe", p_dl)
                d_q4 = r.do("vector", lambda h, qq4=qq4, sf=state_f[s], dl=dlt_ps(par): h.scalar_tensor_tensor(
                    out=sf[:, 256 * qq4:256 * qq4 + 256], in0=sf[:, 256 * qq4:256 * qq4 + 256],
                    scalar=g128, in1=dl, op0=ALU.mult, op1=ALU.add))
            ms[gi]["d_st"] = r.do("vector", lambda h, sf=state_f[s], sb=state_b[s]:
                                  h.tensor_copy(out=sb[:, :], in_=sf[:, :]))
            r.w("scalar", "pe", ms[gi]["p_o"])
            r.w("vector", "pe", ms[gi]["p_o"])
            r.w("scalar", "pe", EM(k - 2, "p_pj"))
            r.w("vector", "pe", EM(k - 2, "p_pj"))
            for c in range(4):
                r.do("scalar", lambda h, c=c, o=mo_b[par], i=o_ps(par, c): h.copy(
                    out=o[:, 128 * c:128 * c + 128], in_=i), sw=(c == 0))
            for c in range(4, DC):
                r.do("vector", lambda h, c=c, o=mo_b[par], i=o_ps(par, c): h.tensor_copy(
                    out=o[:, 128 * c:128 * c + 128], in_=i), sw=(c == 4))
            ms[gi]["a_moevac"] = r.cnt["act"]
            ms[gi]["d_moevac"] = r.cnt["dve"]
            if "dbg_mo" in dbg:
                r.w("sync", "act", ms[gi]["a_moevac"])
                r.w("sync", "dve", ms[gi]["d_moevac"])
                L(dbg_mo[gi], mo_b[par], "sh2st")

        def S6(k):
            s, t, gi, par = ctx(k)
            r.w("tensor", "act", ms[gi]["a_moevac"])
            r.w("tensor", "dve", ms[gi]["d_moevac"])
            r.w("tensor", "pw", lp0)
            for eh in range(2):
                for c in range(DC):
                    r.do("tensor", lambda h, eh=eh, c=c, o=mo_b[par], pj=pj_ps(par, eh): h.matmul(
                        pj, o[:, 128 * c:128 * c + 128],
                        projr_bf[:, D * c + 512 * eh:D * c + 512 * eh + 512],
                        start=(c == 0), stop=False))
                ms[gi]["p_pj"] = r.do("tensor", lambda h, eh=eh, pj=pj_ps(par, eh): h.matmul(
                    pj, ones_row_b, pbrow_b[0:1, 512 * eh:512 * eh + 512],
                    start=False, stop=True), inc="pe" if eh == 1 else None)
            r.w("vector", "pe", ms[gi]["p_pj"])
            r.w("vector", "act", EM(k - 2, "a_rstd2"))
            for eh in range(2):
                r.do("vector", lambda h, eh=eh, o=h2_b[k % 6], pj=pj_ps(par, eh):
                     h.scalar_tensor_tensor(
                    out=o[:, 512 * eh:512 * eh + 512], in0=o[:, 512 * eh:512 * eh + 512],
                    scalar=1.0, in1=pj, op0=ALU.mult, op1=ALU.add), sw=(eh == 0))
            ms[gi]["d_h2"] = r.cnt["dve"]
            r.w("scalar", "dve", ms[gi]["d_h2"])
            r.w("scalar", "sh2st", EM(k - 2, "st_h2"))
            r.do("scalar", lambda h, o=hn2_b[par], i=h2_b[k % 6], a=rt[6]:
                 h.activation(out=o, in_=i, func=AF.Square, accum_out=a))
            r.do("scalar", lambda h, i=rt[6], o=rt[7]:
                 h.activation(out=o, in_=i, func=AF.Ln, scale=1.0 / D, bias=1e-6))
            a_rstd2 = r.do("scalar", lambda h, i=rt[7], o=rows["rstd2"][:, gi:gi + 1]:
                           h.activation(out=o, in_=i, func=AF.Exp, scale=-0.5))
            ms[gi]["a_rstd2"] = a_rstd2
            r.w("vector", "act", a_rstd2)
            d_hn2 = r.do("vector", lambda h, o=hn2_b[par], i=h2_b[k % 6],
                         sc=rows["rstd2"][:, gi:gi + 1]:
                         h.tensor_scalar_mul(out=o, in0=i, scalar1=sc))
            r.w("sync", "dve", ms[gi]["d_h2"])
            L(h2_d[gi], h2_b[k % 6], "sh2st")
            r.w("sync", "dve", d_hn2)
            ms[gi]["st_h2"] = L(hn2_d[gi], hn2_b[par], "sh2st")
            ms[gi]["pe_end"] = ms[gi]["p_pj"]

        for kk in range(NT + 5):
            for ph, off in ((S6, 5), (S5, 4), (S4, 3), (S3, 2), (S2, 1), (S1, 0)):
                kp = kk - off
                if 0 <= kp < NT:
                    ph(kp)

        if "dbg_rows" in dbg:
            for k2, nm2 in enumerate(["m", "cln", "ccomb", "mcln", "mccomb", "rstd2", "suma", "ssqa"]):
                r.w("sync", "dve", r.cnt["dve"])
                r.w("sync", "act", r.cnt["act"])
                L(dbg_rows[:, k2 * NT:(k2 + 1) * NT], rows[nm2], "sh2st")
        r.barrier()
        sc12.close()
        if upi < 1:
            finish(r, nc, sems, out_d, NT)
            return nc, di

        sc3, AB, AFr = open_scope("p3", 60550, 4200)
        w2_bf = AB.alloc(FFC * D)
        hn2T = AB.alloc(DC * 512)
        hid_b = AB.alloc(FFC * 512)
        h2t_b = [AB.alloc(D) for _ in range(2)]
        aout_b = [AB.alloc(D) for _ in range(2)]
        wqstg = [AB.alloc(D) for _ in range(2)]
        wstg3 = [AFr.alloc(D) for _ in range(2)]
        qscr = AB.alloc(D)
        x0p3 = [AFr.alloc(D) for _ in range(2)]

        for qq in range(4):
            L(w2_bf[:, 8192 * qq:8192 * qq + 8192].rearrange("p (m e) -> p m e", m=8),
              di["w2"][1024 * qq:1024 * qq + 1024, :].rearrange("(m p) e -> p m e", p=128),
              "pw")
        r.do("vector", lambda h: h.reduce_sum(out=rtq, in_=part_arr[:, 0:32], axis=AX.X))
        d_pa = r.do("vector", lambda h: h.reduce_sum(out=rtq2b, in_=part_arr[:, 32:64], axis=AX.X))
        r.w("tensor", "dve", d_pa)
        thr_sc = pb[1][0:1, 460:462]
        bcq_ps = pb[1][:, 462:464]
        for i, pp in enumerate((rtq, rtq2b)):
            r.do("tensor", lambda h, pp=pp, i=i: h.matmul(
                thr_sc[0:1, i:i + 1], ones_col_f, pp, start=True, stop=True), inc="pe")
        p_thr = r.cnt["pe"]
        rtq2 = AFr.alloc(16)[0:1, 0:2]
        r.w("vector", "pe", p_thr)
        r.do("vector", lambda h: h.tensor_scalar_mul(out=rtq2, in0=thr_sc, scalar1=0.5 / (D * FF)), inc="dve")
        d_thr = r.cnt["dve"]
        r.w("tensor", "dve", d_thr)
        r.do("tensor", lambda h: h.matmul(bcq_ps, ones_row_f, rtq2, start=True, stop=True), inc="pe")
        p_thrb = r.cnt["pe"]
        r.w("vector", "pe", p_thrb)
        r.do("vector", lambda h: h.tensor_copy(out=thr_up, in_=bcq_ps[:, 0:1]))
        r.do("vector", lambda h: h.tensor_copy(out=thr_dn, in_=bcq_ps[:, 1:2]))
        r.do("vector", lambda h: h.tensor_scalar_mul(out=nthr_up, in0=thr_up, scalar1=-1.0))
        r.do("vector", lambda h: h.tensor_scalar_mul(out=nthr_dn, in0=thr_dn, scalar1=-1.0), inc="dve")

        d_alpha = r.cnt["dve"]
        lastq = {}

        def emit_quant(i):
            nm, srct, a, b = tiles_w[i]
            r.w("sync", "dve", lastq.get(i - 2, d_alpha))
            lw = L(wstg3[i % 2],
                   srct[128 * a:128 * a + 128, 1024 * b:1024 * b + 1024] if nm == "up"
                   else srct[128 * a:128 * a + 128, :])
            thr, nthr = (thr_up, nthr_up) if nm == "up" else (thr_dn, nthr_dn)
            dstd = wqup_d if nm == "up" else wqdn_d
            dst = dstd[128 * a:128 * a + 128, 1024 * b:1024 * b + 1024] if nm == "up" \
                else dstd[128 * a:128 * a + 128, :]
            r.w("vector", "swst", lw)
            r.w("vector", "pst", lastq.get(("st", i - 2), 0))
            r.do("vector", lambda h, i=i, nthr=nthr: h.tensor_scalar(
                out=qscr, in0=wstg3[i % 2], scalar1=nthr, scalar2=None, op0=ALU.is_lt))
            lastq[i] = r.do("vector", lambda h, i=i, thr=thr: h.scalar_tensor_tensor(
                out=wqstg[i % 2], in0=wstg3[i % 2], scalar=thr, in1=qscr,
                op0=ALU.is_gt, op1=ALU.subtract), inc="dve")
            r.w("gpsimd", "dve", lastq[i])
            lastq[("st", i)] = L(dst, wqstg[i % 2], "pst")
        r.barrier()

        r.do("scalar", preload_set(nc, [AF.Gelu_apprx_tanh, AF.Square]))
        st_a = {}
        d_a2 = {}
        d_dot = {}
        ld_dot = {}
        g_ms = [dict() for _ in range(NG)]
        def GM(g, k):
            return g_ms[g][k] if g >= 0 else 0
        for g in range(NG):
            r.w("sync", "pe", GM(g - 1, "p_mlp1"))
            for c in range(DC):
                src = hn2_d[4 * g:4 * g + 4, :, 128 * c:128 * c + 128].rearrange("g p d -> (g p) d")
                stl = ("shl", "sh2st")[c % 2]
                r.w("sync", stl, r.cnt[stl])
                g_ms[g][f"ld_t{c % 2}"] = r.do("sync", lambda h, c=c, s=src: h.dma_start_transpose(
                    hn2T[:, 512 * c:512 * c + 512], s), inc=stl)
            r.w("tensor", "shl", g_ms[g]["ld_t0"])
            r.w("tensor", "sh2st", g_ms[g]["ld_t1"])
            a_hid = {}
            for m in range(FFC):
                bank = pb[(0, 1, 6, 7)[m % 4]][:, :]
                if m >= 4:
                    r.w("tensor", "act", a_hid[m - 4])
                else:
                    ag_prev = GM(g - 1, "a_g_all")
                    r.w("tensor", "act", ag_prev[m + FFC - 4] if ag_prev else 0)
                pm = None
                for c in range(DC):
                    pm = r.do("tensor", lambda h, m=m, c=c, bank=bank: h.matmul(
                        bank, w1_bf[:, FF * c + 128 * m:FF * c + 128 * m + 128],
                        hn2T[:, 512 * c:512 * c + 512],
                        start=(c == 0), stop=(c == DC - 1)),
                        inc="pe" if c == DC - 1 else None)
                r.w("scalar", "pe", pm)
                a_hid[m] = r.do("scalar", lambda h, m=m, bank=bank: h.activation(
                    out=hid_b[:, 512 * m:512 * m + 512], in_=bank,
                    func=AF.Gelu_apprx_tanh, bias=b1r[:, m:m + 1]), inc="act")
            g_ms[g]["p_mlp1"] = r.cnt["pe"]
            g_ms[g]["a_hid"] = a_hid[FFC - 1]
            g_ms[g]["a_g_all"] = a_hid
            d_aev = 0
            for ps4 in range(4):
                ti = 4 * g + ps4
                r.w("sync", "dve", d_a2.get(2 * (ti - 2) + 1, 0))
                ld_h2 = L(h2t_b[ti % 2], h2_d[ti], "shl")
                for eh in range(2):
                    bank = pb[2 + (2 * ps4 + eh) % 2][:, :]
                    r.w("tensor", "act", a_hid[FFC - 1])
                    r.w("tensor", "dve", d_a2.get(2 * ti + eh - 2, 0))
                    for m in range(FFC):
                        r.do("tensor", lambda h, m=m, ps4=ps4, eh=eh, bank=bank: h.matmul(
                            bank, hid_b[:, 512 * m + 128 * ps4:512 * m + 128 * ps4 + 128],
                            w2_bf[:, D * m + 512 * eh:D * m + 512 * eh + 512],
                            start=(m == 0), stop=False))
                    pm = r.do("tensor", lambda h, eh=eh, bank=bank: h.matmul(
                        bank, ones_row_b, b2row_b[0:1, 512 * eh:512 * eh + 512],
                        start=False, stop=True), inc="pe")
                    r.w("vector", "pe", pm)
                    r.w("vector", "shl", ld_h2)
                    r.w("vector", "pst", st_a.get(ti - 2, 0))
                    r.do("vector", lambda h, ps4=ps4, eh=eh, bank=bank, h2t=h2t_b[ti % 2]:
                         h.scalar_tensor_tensor(
                        out=aout_b[ps4 % 2][:, 512 * eh:512 * eh + 512], in0=bank, scalar=1.0,
                        in1=h2t[:, 512 * eh:512 * eh + 512],
                        op0=ALU.mult, op1=ALU.add, accum_out=rt[eh]))
                    d_a2[2 * ti + eh] = r.do(
                        "vector", lambda h: h.engine_nop(), inc="dve")
                r.do("vector", lambda h, o=rows["suma"][:, ti:ti + 1]:
                     h.tensor_add(out=o, in0=rt[0], in1=rt[1]), inc="dve")
                d_sum = r.cnt["dve"]
                r.w("scalar", "dve", d_sum)
                r.do("scalar", lambda h, ps4=ps4, a=rt[2]: h.activation(
                    out=qscr[:, 0:512], in_=aout_b[ps4 % 2][:, 0:512], func=AF.Square, accum_out=a))
                r.do("scalar", lambda h, ps4=ps4, a=rt[3]: h.activation(
                    out=qscr[:, 512:1024], in_=aout_b[ps4 % 2][:, 512:1024], func=AF.Square, accum_out=a), inc="act")
                a_ss = r.cnt["act"]
                r.w("vector", "act", a_ss)
                r.do("vector", lambda h, o=rows["ssqa"][:, ti:ti + 1]:
                     h.tensor_add(out=o, in0=rt[2], in1=rt[3]), inc="dve")
                if ti == 0:
                    ld_dot[0] = L(x0p3[0], di["x0"][0, 0:128, :], "sx0")
                if ti + 1 < NT:
                    r.w("sync", "dve", d_dot.get(ti - 1, 0))
                    ld_dot[ti + 1] = L(x0p3[(ti + 1) % 2],
                                       di["x0"][(ti + 1) // seqt,
                                                128 * ((ti + 1) % seqt):128 * ((ti + 1) % seqt) + 128, :],
                                       "sx0")
                r.w("vector", "sx0", ld_dot[ti])
                d_dot[ti] = r.do("vector", lambda h, ti=ti, ps4=ps4: h.scalar_tensor_tensor(
                    out=qscr, in0=x0p3[ti % 2], scalar=1.0, in1=aout_b[ps4 % 2],
                    op0=ALU.mult, op1=ALU.mult, accum_out=rows["Sdot"][:, ti:ti + 1]))
                r.w("gpsimd", "act", a_ss)
                r.w("gpsimd", "dve", d_a2[2 * ti + 1])
                st_a[ti] = L(aout_d[ti], aout_b[ps4 % 2], "pst")
                for i2 in range(ti * len(tiles_w) // NT, (ti + 1) * len(tiles_w) // NT):
                    emit_quant(i2)
            g_ms[g]["d_aev"] = r.cnt["dve"]
        r.barrier()
        sc3.close()
        scW.close()
        if upi < 2:
            finish(r, nc, sems, out_d, NT)
            return nc, di

        sc4, AB, AFr = open_scope("p4", 94230, 2760)
        wqu_b = AB.alloc(DC * FF)
        wqd_b = AB.alloc(FFC * D)
        finT = AB.alloc(DC * 512)
        g_b = AB.alloc(FFC * 512)
        raw_b = [AB.alloc(D) for _ in range(4)]
        ao4_b = [AB.alloc(D) for _ in range(2)]
        fin4_b = ao4_b
        gsq_b = [AB.alloc(512) for _ in range(4)]
        ssq_row = AFr.alloc(512)[0:1, :]
        x0p4 = [AFr.alloc(D) for _ in range(2)]
        x1l_b = x0p4
        lad_q = AFr.alloc(2 * NT); lad_y = AFr.alloc(2 * NT); lad_t = AFr.alloc(2 * NT)

        def emit_rsqrt(qap, yap, tap, q0, levels, iters=8):
            base = (1.0 / math.sqrt(q0)) * (2.0 ** (-0.5 - levels))
            r.do("vector", lambda h, y=yap, b=base: h.memset(y, b))
            for k in range(1, levels + 1):
                thr = q0 * (4.0 ** k)
                r.do("vector", lambda h, q=qap, t=tap, thr=thr: h.tensor_scalar(
                    out=t, in0=q, scalar1=thr, scalar2=1.0, op0=ALU.is_le, op1=ALU.add))
                r.do("vector", lambda h, y=yap, t=tap: h.tensor_mul(out=y, in0=y, in1=t))
            for _ in range(iters):
                r.do("vector", lambda h, y=yap, t=tap: h.tensor_mul(out=t, in0=y, in1=y))
                r.do("vector", lambda h, q=qap, t=tap: h.scalar_tensor_tensor(
                    out=t, in0=q, scalar=-0.5, in1=t, op0=ALU.mult, op1=ALU.mult))
                r.do("vector", lambda h, t=tap: h.tensor_scalar(
                    out=t, in0=t, scalar1=1.5, scalar2=None, op0=ALU.add))
                r.do("vector", lambda h, y=yap, t=tap: h.tensor_mul(out=y, in0=y, in1=t))
            return r.cnt["dve"]

        r.do("scalar", preload_set(nc, [AF.Gelu, AF.Square]))
        for qq in range(4):
            L(wqu_b[:, 8192 * qq:8192 * qq + 8192].rearrange("p (c f) -> p c f", c=2),
              wqup_d[256 * qq:256 * qq + 256, :].rearrange("(c p) f -> p c f", p=128), "pw")
        for qq in range(4):
            L(wqd_b[:, 8192 * qq:8192 * qq + 8192].rearrange("p (m e) -> p m e", m=8),
              wqdn_d[1024 * qq:1024 * qq + 1024, :].rearrange("(m p) e -> p m e", p=128), "pw")
        lq = r.cnt["pw"]

        r.do("vector", lambda h: h.tensor_scalar_mul(out=rows["suma"], in0=rows["suma"], scalar1=1.0 / D))
        r.do("vector", lambda h: h.tensor_scalar_mul(out=rows["ssqa"], in0=rows["ssqa"], scalar1=1.0 / D))
        r.do("vector", lambda h: h.tensor_mul(out=rows["B3"], in0=rows["suma"], in1=rows["suma"]))
        r.do("vector", lambda h: h.tensor_sub(out=rows["ssqa"], in0=rows["ssqa"], in1=rows["B3"]))
        r.do("vector", lambda h: h.tensor_scalar(out=rows["mcln"], in0=rows["ssqa"], scalar1=1e-5,
                                                 scalar2=None, op0=ALU.add))
        emit_rsqrt(rows["mcln"], rows["A3"], rows["mccomb"], 1e-5, 17)
        r.do("vector", lambda h: h.tensor_mul(out=rows["B3"], in0=rows["suma"], in1=rows["A3"]))
        r.do("vector", lambda h: h.tensor_scalar_mul(out=rows["B3"], in0=rows["B3"], scalar1=-1.0))
        r.do("vector", lambda h: h.tensor_scalar_mul(out=rows["Sdot"], in0=rows["Sdot"], scalar1=1.0 / D))
        r.do("vector", lambda h: h.tensor_mul(out=rows["M4"], in0=rows["suma"], in1=rows["m"]))
        r.do("vector", lambda h: h.tensor_sub(out=rows["Sdot"], in0=rows["Sdot"], in1=rows["M4"]))
        r.do("vector", lambda h: h.tensor_mul(out=rows["Sdot"], in0=rows["Sdot"], in1=rows["A3"]))
        r.do("vector", lambda h: h.reciprocal(out=rows["mccomb"], in_=rows["mcln"]))
        r.do("vector", lambda h: h.tensor_mul(out=rows["mccomb"], in0=rows["ssqa"], in1=rows["mccomb"]))
        r.do("vector", lambda h: h.scalar_tensor_tensor(out=rows["mcln"], in0=rows["Sdot"], scalar=2.0,
                                                        in1=rows["mccomb"], op0=ALU.mult, op1=ALU.add))
        r.do("vector", lambda h: h.scalar_tensor_tensor(out=rows["mcln"], in0=rows["v0"],
                                                        scalar=-1.0 / D, in1=rows["mcln"],
                                                        op0=ALU.mult, op1=ALU.add))
        r.do("vector", lambda h: h.tensor_scalar(out=lad_q[:, 0:NT], in0=rows["mcln"], scalar1=1e-5,
                                                 scalar2=None, op0=ALU.add))
        r.do("vector", lambda h: h.tensor_scalar(out=lad_q[:, NT:2 * NT], in0=rows["mcln"], scalar1=1e-12,
                                                 scalar2=None, op0=ALU.add))
        emit_rsqrt(lad_q, lad_y, lad_t, 1e-12, 27)
        r.do("vector", lambda h: h.tensor_mul(out=lad_t[:, 0:NT], in0=rows["mcln"], in1=lad_y[:, NT:2 * NT]))
        r.do("vector", lambda h: h.tensor_mul(out=lad_t[:, 0:NT], in0=lad_t[:, 0:NT], in1=lad_y[:, 0:NT]))
        r.do("vector", lambda h: h.tensor_scalar(out=lad_t[:, 0:NT], in0=lad_t[:, 0:NT], scalar1=1e-8,
                                                 scalar2=None, op0=ALU.add))
        r.do("vector", lambda h: h.reciprocal(out=lad_t[:, 0:NT], in_=lad_t[:, 0:NT]))
        r.do("vector", lambda h: h.tensor_mul(out=rows["S4"], in0=lad_y[:, 0:NT], in1=lad_t[:, 0:NT]))
        r.do("vector", lambda h: h.tensor_sub(out=rows["M4"], in0=rows["B3"], in1=rows["m"]))
        r.do("vector", lambda h: h.tensor_mul(out=rows["M4"], in0=rows["M4"], in1=rows["S4"]))

        fold_ms = {}
        def FM(key, f):
            return fold_ms[(key, f)] if f >= 0 else 0
        def emit_fold(f):
            s_, t_ = f // seqt, f % seqt
            r.w("sync", "dve", FM("d_x1", f - 2))
            r.w("sync", "swst", FM("st_x1", f - 2))
            r.w("sync", "sh2st", out_ms.get(("st_o", f - (f % 4) - 6), 0))
            r.w("sync", "pst", out_ms.get(("st_o", f - (f % 4) - 5), 0))
            lx = L(x0p4[f % 2], di["x0"][s_, 128 * t_:128 * t_ + 128, :], "sx0")
            r.w("gpsimd", "dve", FM("d_t", f - 2))
            r.w("gpsimd", "pxt", FM("st_f", f - 2))
            la = L(ao4_b[f % 2], aout_d[f], "pld")
            r.w("vector", "sx0", lx)
            r.w("vector", "pld", la)
            fold_ms[("d_t", f)] = r.do("vector", lambda h, f=f: h.scalar_tensor_tensor(
                out=x0p4[f % 2], in0=ao4_b[f % 2], scalar=rows["A3"][:, f:f + 1],
                in1=x0p4[f % 2], op0=ALU.mult, op1=ALU.add))
            r.w("vector", "pxt", FM("st_f", f - 2))
            d_f = r.do("vector", lambda h, f=f: h.tensor_scalar(
                out=fin4_b[f % 2], in0=x0p4[f % 2], scalar1=rows["S4"][:, f:f + 1],
                scalar2=rows["M4"][:, f:f + 1], op0=ALU.mult, op1=ALU.add))
            r.w("vector", "swst", FM("st_x1", f - 2))
            d_x1 = r.do("vector", lambda h, f=f: h.tensor_scalar(
                out=x0p4[f % 2], in0=x0p4[f % 2], scalar1=rows["B3"][:, f:f + 1],
                scalar2=None, op0=ALU.add))
            fold_ms[("d_x1", f)] = d_x1
            r.w("sync", "dve", d_x1)
            fold_ms[("st_x1", f)] = L(x1_d[f], x0p4[f % 2], "swst")
            r.w("gpsimd", "dve", d_f)
            fold_ms[("st_f", f)] = L(fin_d[f], fin4_b[f % 2], "pxt")

        out_ms = {}
        def OM(key, f):
            return out_ms[(key, f)] if f >= 0 else 0
        def emit_out(tj):
            lastf = min(tj - (tj % 4) + 7, NT - 1)
            r.w("gpsimd", "swst", fold_ms[("st_x1", lastf)])
            r.w("gpsimd", "dve", FM("d_x1", lastf))
            r.w("gpsimd", ("sh2st", "pst")[tj % 2], OM("st_o", tj - 2))
            lx1 = L(x1l_b[tj % 2], x1_d[tj], "pw")
            r.w("vector", "pw", lx1)
            r.do("vector", lambda h, tj=tj: h.scalar_tensor_tensor(
                out=x1l_b[tj % 2], in0=raw_b[tj % 4], scalar=rows["A4"][:, tj:tj + 1],
                in1=x1l_b[tj % 2], op0=ALU.mult, op1=ALU.add))
            ost = ("sh2st", "pst")[tj % 2]
            r.w("vector", ost, OM("st_o", tj - 2))
            d_o = r.do("vector", lambda h, tj=tj: h.tensor_scalar(
                out=x1l_b[tj % 2], in0=x1l_b[tj % 2], scalar1=rows["B4"][:, tj:tj + 1],
                scalar2=None, op0=ALU.add))
            out_ms[("d_o", tj)] = d_o
            r.w("sync" if ost[0] == "s" else "gpsimd", "dve", d_o)
            out_ms[("st_o", tj)] = L(out_d[tj], x1l_b[tj % 2], ost)

        def emit_a4rows(g):
            c0 = 4 * g
            cs = slice(c0, c0 + 4)
            r.w("vector", "act", g4_ms[g]["d_ssqg"])
            r.do("vector", lambda h, cs=cs: h.tensor_scalar_mul(
                out=rows["sumr"][:, cs], in0=rows["sumr"][:, cs], scalar1=1.0 / D))
            r.do("vector", lambda h, cs=cs: h.tensor_scalar_mul(
                out=rows["ssqr"][:, cs], in0=rows["ssqr"][:, cs], scalar1=1.0 / D))
            r.do("vector", lambda h, cs=cs: h.tensor_mul(
                out=lad_t[:, 0:4], in0=rows["sumr"][:, cs], in1=rows["sumr"][:, cs]))
            r.do("vector", lambda h, cs=cs: h.tensor_sub(
                out=rows["ssqr"][:, cs], in0=rows["ssqr"][:, cs], in1=lad_t[:, 0:4]))
            r.do("vector", lambda h, cs=cs: h.tensor_scalar(
                out=lad_q[:, 0:4], in0=rows["ssqg"][:, cs], scalar1=1.0 / FF, scalar2=1e-16,
                op0=ALU.mult, op1=ALU.add))
            emit_rsqrt(lad_q[:, 0:4], lad_y[:, 0:4], lad_t[:, 0:4], 1e-16, 36)
            r.do("vector", lambda h, cs=cs: h.tensor_scalar_mul(
                out=lad_t[:, 0:4], in0=rows["ssqg"][:, cs], scalar1=1.0 / FF))
            r.do("vector", lambda h: h.tensor_mul(out=lad_y[:, 0:4], in0=lad_y[:, 0:4], in1=lad_t[:, 0:4]))
            r.do("vector", lambda h: h.tensor_scalar(out=lad_y[:, 0:4], in0=lad_y[:, 0:4],
                                                     scalar1=1e-8, scalar2=None, op0=ALU.add))
            r.do("vector", lambda h: h.reciprocal(out=lad_y[:, 0:4], in_=lad_y[:, 0:4]))
            r.do("vector", lambda h: h.tensor_mul(out=lad_t[:, 0:4], in0=lad_y[:, 0:4], in1=lad_y[:, 0:4]))
            r.do("vector", lambda h, cs=cs: h.tensor_mul(
                out=lad_q[:, 4:8], in0=lad_t[:, 0:4], in1=rows["ssqr"][:, cs]))
            r.do("vector", lambda h: h.tensor_scalar(out=lad_q[:, 4:8], in0=lad_q[:, 4:8],
                                                     scalar1=1e-5, scalar2=None, op0=ALU.add))
            emit_rsqrt(lad_q[:, 4:8], lad_y[:, 4:8], lad_t[:, 4:8], 1e-5, 19)
            r.do("vector", lambda h, cs=cs: h.tensor_mul(
                out=rows["A4"][:, cs], in0=lad_y[:, 0:4], in1=lad_y[:, 4:8]))
            r.do("vector", lambda h, cs=cs: h.scalar_tensor_tensor(
                out=rows["B4"][:, cs], in0=rows["sumr"][:, cs], scalar=-1.0,
                in1=rows["A4"][:, cs], op0=ALU.mult, op1=ALU.mult))

        for f in range(min(4, NT)):
            emit_fold(f)

        ssq4_ps = pb[4][0:1, 0:512]
        tr4_ps = pb[5][:, 0:4]
        g4_ms = [dict() for _ in range(NG)]
        d_rev, a_ssr = {}, {}
        def G4(g, k):
            return g4_ms[g][k] if g >= 0 else 0
        for g in range(NG):
            r.w("sync", "pe", G4(g - 1, "p_up"))
            r.w("sync", "pxt", fold_ms[("st_f", 4 * g + 3)])
            for c in range(DC):
                src = fin_d[4 * g:4 * g + 4, :, 128 * c:128 * c + 128].rearrange("g p d -> (g p) d")
                stl = ("shl", "sx0")[c % 2]
                r.w("sync", stl, r.cnt[stl])
                g4_ms[g][f"ld_t{c % 2}"] = r.do("sync", lambda h, c=c, s=src: h.dma_start_transpose(
                    finT[:, 512 * c:512 * c + 512], s), inc=stl)
            if g >= 1:
                for j in range(4):
                    emit_out(4 * (g - 1) + j)
            r.w("tensor", "shl", g4_ms[g]["ld_t0"])
            r.w("tensor", "sx0", g4_ms[g]["ld_t1"])
            if g == 0:
                r.w("tensor", "pw", lq)
            a_g = {}
            d_gsq = {}
            for m in range(FFC):
                bank = pb[(0, 1, 6, 7)[m % 4]][:, :]
                if m >= 4:
                    r.w("tensor", "act", a_g[m - 4])
                else:
                    ag_prev = G4(g - 1, "a_g_all")
                    r.w("tensor", "act", ag_prev[m + FFC - 4] if ag_prev else 0)
                pm = None
                for c in range(DC):
                    pm = r.do("tensor", lambda h, m=m, c=c, bank=bank: h.matmul(
                        bank, wqu_b[:, FF * c + 128 * m:FF * c + 128 * m + 128],
                        finT[:, 512 * c:512 * c + 512],
                        start=(c == 0), stop=(c == DC - 1)),
                        inc="pe" if c == DC - 1 else None)
                r.w("scalar", "pe", pm)
                a_g[m] = r.do("scalar", lambda h, m=m, bank=bank: h.activation(
                    out=g_b[:, 512 * m:512 * m + 512], in_=bank, func=AF.Gelu), inc="act")
                if m >= 1:
                    if m >= 5:
                        r.w("scalar", "pe", g4_ms[g].get(f"p_ssq_{m-5}", 0))
                    else:
                        r.w("scalar", "pe", G4(g - 1, f"p_ssq_{m+27}"))
                    d_gsq[m - 1] = r.do("scalar", lambda h, m1=m - 1: h.activation(
                        out=gsq_b[m1 % 4], in_=g_b[:, 512 * m1:512 * m1 + 512],
                        func=AF.Square), inc="act")
                if m % 8 == 4 and 4 * g + 4 + m // 8 < NT:
                    emit_fold(4 * g + 4 + m // 8)
                if m >= 2:
                    r.w("tensor", "act", d_gsq[m - 2])
                    g4_ms[g][f"p_ssq_{m-2}"] = r.do("tensor", lambda h, m1=m - 2: h.matmul(
                        ssq4_ps, ones_col_b, gsq_b[m1 % 4],
                        start=(m1 == 0), stop=False), inc="pe")
            r.w("scalar", "pe", g4_ms[g][f"p_ssq_{FFC-5}"])
            d_gsq[FFC - 1] = r.do("scalar", lambda h: h.activation(
                out=gsq_b[(FFC - 1) % 4], in_=g_b[:, 512 * (FFC - 1):512 * FFC],
                func=AF.Square), inc="act")
            for m1 in (FFC - 2, FFC - 1):
                r.w("tensor", "act", d_gsq[m1])
                g4_ms[g][f"p_ssq_{m1}"] = r.do("tensor", lambda h, m1=m1: h.matmul(
                    ssq4_ps, ones_col_b, gsq_b[m1 % 4],
                    start=False, stop=(m1 == FFC - 1)), inc="pe")
            g4_ms[g]["p_up"] = r.cnt["pe"]
            g4_ms[g]["a_glast"] = a_g[FFC - 1]
            g4_ms[g]["a_g_all"] = a_g
            r.w("scalar", "pe", g4_ms[g]["p_up"])
            g4_ms[g]["d_srow"] = r.do(
                "scalar", lambda h: h.copy(out=ssq_row, in_=ssq4_ps), inc="act")
            r.w("tensor", "act", g4_ms[g]["d_srow"])
            r.w("tensor", "act", G4(g - 1, "d_ssqg"))
            for j in range(4):
                r.do("tensor", lambda h, j=j: h.transpose(
                    tr4_ps[:, j:j + 1], ssq_row[0:1, 128 * j:128 * j + 128], ident_f[0:1, 0:1]),
                    inc="pe" if j == 3 else None)
            p_tr = r.cnt["pe"]
            r.w("scalar", "pe", p_tr)
            for j in range(4):
                r.do("scalar", lambda h, j=j, ti=4 * g + j: h.copy(
                    out=rows["ssqg"][:, ti:ti + 1], in_=tr4_ps[:, j:j + 1]),
                    inc="act" if j == 3 else None)
            g4_ms[g]["d_ssqg"] = r.cnt["act"]
            for ps4 in range(4):
                ti = 4 * g + ps4
                for eh in range(2):
                    bank = pb[2 + (2 * ps4 + eh) % 2][:, :]
                    r.w("tensor", "dve", d_rev.get(2 * ti + eh - 2, 0))
                    pm = None
                    for m in range(FFC):
                        pm = r.do("tensor", lambda h, m=m, ps4=ps4, eh=eh, bank=bank: h.matmul(
                            bank, g_b[:, 512 * m + 128 * ps4:512 * m + 128 * ps4 + 128],
                            wqd_b[:, D * m + 512 * eh:D * m + 512 * eh + 512],
                            start=(m == 0), stop=(m == FFC - 1)),
                            inc="pe" if m == FFC - 1 else None)
                    r.w("vector", "pe", pm)
                    r.w("vector", "act", a_ssr.get(ti - 2, 0))
                    r.do("vector", lambda h, ti=ti, eh=eh, bank=bank: h.tensor_scalar(
                        out=raw_b[ti % 4][:, 512 * eh:512 * eh + 512], in0=bank,
                        scalar1=1.0, scalar2=None, op0=ALU.mult, op1=ALU.add, accum_out=rt[eh]))
                    d_rev[2 * ti + eh] = r.do("vector", lambda h: h.engine_nop(), inc="dve")
                r.do("vector", lambda h, ti=ti: h.tensor_add(
                    out=rows["sumr"][:, ti:ti + 1], in0=rt[0], in1=rt[1]), inc="dve")
                d_sr2 = r.cnt["dve"]
                r.w("scalar", "dve", d_sr2)
                r.do("scalar", lambda h, ti=ti, a=rt[2]: h.activation(
                    out=gsq_b[0], in_=raw_b[ti % 4][:, 0:512], func=AF.Square, accum_out=a))
                r.do("scalar", lambda h, ti=ti, a=rt[3]: h.activation(
                    out=gsq_b[1], in_=raw_b[ti % 4][:, 512:1024], func=AF.Square, accum_out=a), inc="act")
                a_ssr[ti] = r.cnt["act"]
                r.w("vector", "act", a_ssr[ti])
                r.do("vector", lambda h, ti=ti: h.tensor_add(
                    out=rows["ssqr"][:, ti:ti + 1], in0=rt[2], in1=rt[3]), inc="dve")
                if "raw" in dbg:
                    r.w("gpsimd", "dve", d_rev[2 * ti + 1])
                    L(raw_d[ti], raw_b[ti % 4], "pst")
            emit_a4rows(g)
        for j in range(4):
            emit_out(4 * (NG - 1) + j)
        r.barrier()
        sc4.close()

        finish(r, nc, sems, out_d, NT)
        return nc, di


def finish(r, nc, sems, out_d, NT):
    snap = dict(r.cnt)
    for e in ENGINES:
        for s in DMA_SEMS:
            r.w(e, s, snap[s])
    r.replay(nc, sems)


_CACHE = {}


def kernel(**inputs):
    nseq, S = 2, 2048
    key = (nseq, S)
    if key not in _CACHE:
        _CACHE[key] = build(nseq, S)
    nc, _ = _CACHE[key]
    in_maps = [host_prep(c, inputs, nseq, S) for c in range(8)]
    res = run_bass_kernel_spmd(nc, in_maps, core_ids=list(range(8)))
    seqt = S // 128
    outs = []
    for c in range(8):
        o = np.asarray(res.results[c]["out"])
        outs.append(o.reshape(nseq, S, D))
    return np.concatenate(outs, 0).astype(np.float32)
